# version 04
# baseline (speedup 1.0000x reference)
import contextlib
import numpy as np

import concourse.bass as bass
import concourse.mybir as mybir
from concourse.bass_utils import run_bass_kernel_spmd

F32 = mybir.dt.float32
BF16 = mybir.dt.bfloat16
AF = mybir.ActivationFunctionType
ALU = mybir.AluOpType
AX = mybir.AxisListType

T = 2048
D = 1024
NT = 4
TD = 512
NDC = 8
NCORES = 8

ENG2BODY = {"pe": "tensor", "act": "scalar", "dve": "vector", "pool": "gpsimd", "sp": "sync"}
DMA_SEM = {"sp": "dsp", "pool": "dpool", "act": "dact"}


class Op:
    __slots__ = ("fn", "eng", "waits", "inc_sem", "inc_amt", "need", "count", "flushed")

    def __init__(self, fn, eng, waits, inc_sem, inc_amt, need):
        self.fn = fn
        self.eng = eng
        self.waits = waits
        self.inc_sem = inc_sem
        self.inc_amt = inc_amt
        self.need = need
        self.count = None
        self.flushed = False


class Buf:
    __slots__ = ("ap", "wtok", "rtoks")

    def __init__(self, ap):
        self.ap = ap
        self.wtok = None
        self.rtoks = []

    def __getitem__(self, idx):
        return self.ap.__getitem__(idx)


NDMASEM = 64


class Prog:
    def __init__(self, nc, stack):
        self.nc = nc
        self.sem = {}
        for name in ("pe", "act", "dve", "pool", "sp", "dsp", "dpool", "dact", "cc"):
            self.sem[name] = stack.enter_context(nc.semaphore("sem_" + name))
        for i in range(NDMASEM):
            self.sem[f"ds{i}"] = stack.enter_context(nc.semaphore(f"ds{i}"))
        self.cnt = {k: 0 for k in self.sem}
        self.q = {e: [] for e in ENG2BODY}
        self.buf_sem = {}
        self.next_ds = 0
        self.phase_dma_sems = {e: set() for e in ENG2BODY}

    def _dma_sem_for(self, key_buf, eng):
        if key_buf is None:
            return DMA_SEM[eng]
        k = id(key_buf)
        s = self.buf_sem.get(k)
        if s is None:
            s = f"ds{self.next_ds % NDMASEM}"
            self.next_ds += 1
            self.buf_sem[k] = s
        return s

    def emit(self, eng, fn, ins=(), outs=(), waits=(), dma=False, collective=False):
        wl = []
        for b in ins:
            if isinstance(b, Buf):
                if b.wtok is not None:
                    wl.append(b.wtok)
            elif b is not None:
                wl.append(b)
        for b in outs:
            if b.wtok is not None:
                wl.append(b.wtok)
            wl.extend(b.rtoks)
        wl.extend(t for t in waits if t is not None)
        if dma:
            inc_sem, inc_amt, need = dma, 16, True
            self.phase_dma_sems[eng].add(dma)
        elif collective:
            inc_sem, inc_amt, need = "cc", 1, True
        else:
            inc_sem, inc_amt, need = eng, 1, False
        op = Op(fn, eng, wl, inc_sem, inc_amt, need)
        for t in wl:
            if not t.flushed:
                t.need = True
        self.q[eng].append(op)
        for b in ins:
            if isinstance(b, Buf):
                b.rtoks.append(op)
        for b in outs:
            b.wtok = op
            b.rtoks = []
        return op

    def dma(self, eng, out, in_, ins=(), outs=(), waits=(), key=None, **kw):
        def fn(e, o=out, i=in_, kw=kw):
            oap = o.ap if isinstance(o, Buf) else o
            iap = i.ap if isinstance(i, Buf) else i
            return e.dma_start(oap, iap, **kw)
        allins = list(ins) + ([in_] if isinstance(in_, Buf) else [])
        allouts = list(outs) + ([out] if isinstance(out, Buf) else [])
        if key is None:
            if isinstance(out, Buf):
                key = out
            elif allouts:
                key = allouts[0]
            elif isinstance(in_, Buf):
                key = in_
            elif allins:
                key = allins[0]
        sem = self._dma_sem_for(key, eng)
        return self.emit(eng, fn, ins=allins, outs=allouts, waits=waits, dma=sem)

    @staticmethod
    def begin_fill(buf):
        w = ([buf.wtok] if buf.wtok is not None else []) + list(buf.rtoks)
        buf.wtok = None
        buf.rtoks = []
        return w

    @staticmethod
    def setw(buf, tok):
        buf.wtok = tok
        buf.rtoks = []

    def flush(self, name=None):
        nc = self.nc
        for eng, lst in self.q.items():
            for op in lst:
                if op.need:
                    self.cnt[op.inc_sem] += op.inc_amt
                    op.count = self.cnt[op.inc_sem]
        qs = {e: l for e, l in self.q.items() if l}
        if not qs:
            return
        with nc.Block() as blk:
            for ename, lst in qs.items():
                def make_body(lst=lst, ename=ename):
                    def body(e):
                        waited = {}
                        for op in lst:
                            for t in op.waits:
                                if t.flushed or t.count is None:
                                    continue
                                if t.inc_sem == "pe" and ename == "pe":
                                    continue
                                if waited.get(t.inc_sem, 0) >= t.count:
                                    continue
                                e.wait_ge(self.sem[t.inc_sem], t.count)
                                waited[t.inc_sem] = t.count
                            inst = op.fn(e)
                            if op.count is not None:
                                inst.then_inc(self.sem[op.inc_sem], op.inc_amt)
                        for dsem in sorted(self.phase_dma_sems[ename]):
                            if waited.get(dsem, 0) < self.cnt[dsem]:
                                e.wait_ge(self.sem[dsem], self.cnt[dsem])
                        if ename == "pool" and any(
                                op.inc_sem == "cc" and op.count is not None for op in lst):
                            if waited.get("cc", 0) < self.cnt["cc"]:
                                e.wait_ge(self.sem["cc"], self.cnt["cc"])
                    return body
                getattr(blk, ENG2BODY[ename])(make_body())
        for lst in qs.values():
            for op in lst:
                op.flushed = True
        self.q = {e: [] for e in ENG2BODY}
        self.buf_sem = {}
        self.phase_dma_sems = {e: set() for e in ENG2BODY}


WEIGHT_SPECS = [
    ("L0_attn_w_qkv", (D, 3 * D)),
    ("L0_attn_wo", (D, D)),
    ("L0_attn_lambda_q1", (1, 64)),
    ("L0_attn_lambda_q2", (1, 64)),
    ("L0_attn_lambda_k1", (1, 64)),
    ("L0_attn_lambda_k2", (1, 64)),
    ("L0_attn_norm_w", (16, 64)),
    ("L0_ffn_g1", (D,)),
    ("L0_ffn_g2", (D,)),
    ("L0_ffn_wq", (D, D)),
    ("L0_ffn_wk", (D, 256)),
    ("L0_ffn_wv", (D, 256)),
    ("L0_ffn_wo", (D, D)),
    ("L0_ffn_router_w", (D, 8)),
    ("L0_ffn_router_b", (1, 8)),
    ("L0_ffn_w1", (8, D, 2048)),
    ("L0_ffn_w2", (8, D, 2048)),
    ("L0_ffn_w3", (8, 2048, D)),
    ("L1_attn_w_qkv", (D, 3 * D)),
    ("L1_attn_w_out", (D, D)),
    ("L1_attn_b_out", (D,)),
    ("L1_attn_threshold", (1, 1)),
    ("L1_ffn_attn_nw", (D,)),
    ("L1_ffn_attn_nb", (D,)),
    ("L1_ffn_wg", (D, 8)),
    ("L1_ffn_inter_w", (8, D, 4096)),
    ("L1_ffn_inter_b", (8, 4096)),
    ("L1_ffn_output_w", (8, 4096, D)),
    ("L1_ffn_output_b", (8, D)),
    ("L1_ffn_res_inter_w", (D, 4096)),
    ("L1_ffn_res_inter_b", (1, 4096)),
    ("L1_ffn_res_output_w", (4096, D)),
    ("L1_ffn_res_output_b", (D,)),
    ("L1_ffn_res_coef", (D, 2)),
]

HC_SPECS = [
    ("hc_sel8", (8, 1024)),
    ("hc_cos", (128, T)),
    ("hc_sin", (128, T)),
    ("hc_bo4", (128, 4)),
    ("hc_bo2", (128, 2)),
    ("hc_prev2", (8, 16)),
    ("hc_prev1", (8, 8)),
]


def host_consts(core):
    out = {}
    dim = 64
    thetas = 10000.0 ** (-np.arange(0, dim, 2, dtype=np.float32) / dim)
    t = np.arange(T, dtype=np.float32)
    ang = np.outer(thetas, t)
    ang4 = np.tile(ang, (4, 1))
    out["hc_cos"] = np.cos(ang4).astype(np.float32)
    out["hc_sin"] = np.sin(ang4).astype(np.float32)
    bo4 = np.zeros((128, 4), np.float32)
    for j in range(4):
        bo4[32 * j:32 * (j + 1), j] = 1.0
    out["hc_bo4"] = bo4
    bo2 = np.zeros((128, 2), np.float32)
    bo2[0:64, 0] = 1.0
    bo2[64:128, 1] = 1.0
    out["hc_bo2"] = bo2
    pm2 = np.repeat((np.arange(8) < core).astype(np.float32), 2).reshape(1, 16)
    out["hc_prev2"] = np.broadcast_to(pm2, (8, 16)).copy()
    pm1 = (np.arange(8) < core).astype(np.float32).reshape(1, 8)
    out["hc_prev1"] = np.broadcast_to(pm1, (8, 8)).copy()
    sel = np.zeros((8, 8, 128), np.float32)
    for e in range(8):
        sel[e, e, :] = 1.0
    out["hc_sel8"] = sel.transpose(1, 0, 2).reshape(8, 8 * 128)
    return out



def build_program(upto=99, dbg=False):
    nc = bass.Bass("TRN2", target_bir_lowering=False)
    gstack = contextlib.ExitStack()
    p = Prog(nc, gstack)
    uid = [0]

    dram = {}
    dram["x"] = nc.dram_tensor("x", [D, T], F32, kind="ExternalInput").ap()
    for name, shape in WEIGHT_SPECS:
        dram[name] = nc.dram_tensor(name, list(shape), F32, kind="ExternalInput").ap()
    for name, shape in HC_SPECS:
        dram[name] = nc.dram_tensor(name, list(shape), F32, kind="ExternalInput").ap()
    skind = "ExternalOutput" if dbg else "Internal"
    for name in ("xn_f32", "s0", "h1_f32", "acc0", "x1", "xn1_f32", "s1", "acc1"):
        dram[name] = nc.dram_tensor(name, [D, T], F32, kind=skind).ap()
    dram["out"] = nc.dram_tensor("out", [D, T], F32, kind="ExternalOutput").ap()
    dram["cnt0"] = nc.dram_tensor("cnt0", [8, 2], F32, kind="Internal").ap()
    dram["ag0"] = nc.dram_tensor("ag0", [64, 2], F32, kind="Internal").ap()
    dram["cnt1"] = nc.dram_tensor("cnt1", [8, 1], F32, kind="Internal").ap()
    dram["ag1"] = nc.dram_tensor("ag1", [64, 1], F32, kind="Internal").ap()
    NROWS_DBG = 64
    if dbg:
        dram["dbg_rows"] = nc.dram_tensor("dbg_rows", [NROWS_DBG, T], F32,
                                          kind="ExternalOutput").ap()
        dram["acc0b"] = nc.dram_tensor("acc0b", [D, T], F32, kind="ExternalOutput").ap()
    dbg_row_i = [0]

    def sb(name, shape, dt, stack=gstack, side="left"):
        uid[0] += 1
        return Buf(stack.enter_context(
            nc.sbuf_tensor(f"{name}_{uid[0]}", list(shape), dt, side=side)).ap())

    def psb(name, shape, dt=F32, stack=gstack):
        uid[0] += 1
        return Buf(stack.enter_context(
            nc.psum_tensor(f"{name}_{uid[0]}", list(shape), dt)).ap())

    xn_bf = sb("xn_bf", [128, NDC * T], BF16)
    ones_col = sb("ones_col", [128, 1], F32)
    ones_row = sb("ones_row", [1, 128], F32)
    ones_col_bf = sb("ones_col_bf", [128, 1], BF16)
    iota8_col = sb("iota8_col", [8, 1], F32)
    iota8_row128 = sb("iota8_r128", [128, 8], F32)
    ident_bf = sb("ident_bf", [128, 128], BF16)
    ident_f = sb("ident_f", [128, 128], F32)
    cvrow = sb("cvrow", [32, 128], F32)
    arow = sb("arow", [8, T], F32)
    brow = sb("brow", [8, T], F32)
    consts = {}
    for name, shape in HC_SPECS:
        if name in ("hc_cos", "hc_sin"):
            continue
        consts[name] = sb(name, shape, F32)

    def tap_rows(buf, src_slice, n=1, ncols=T):
        if not dbg:
            return None
        r = dbg_row_i[0]
        dbg_row_i[0] += n
        assert dbg_row_i[0] <= NROWS_DBG
        p.dma("sp", dram["dbg_rows"][r:r + n, 0:ncols], buf.ap[src_slice], ins=[buf])
        return r

    p.emit("pool", lambda e: e.memset(ones_col.ap[:, :], 1.0), outs=[ones_col])
    p.emit("pool", lambda e: e.memset(ones_row.ap[:, :], 1.0), outs=[ones_row])
    p.emit("pool", lambda e: e.memset(ones_col_bf.ap[:, :], 1.0), outs=[ones_col_bf])
    p.emit("pool", lambda e: e.iota(iota8_col.ap[:, :], [[1, 1]], base=0, channel_multiplier=1,
                                    allow_small_or_imprecise_dtypes=True), outs=[iota8_col])
    p.emit("pool", lambda e: e.iota(iota8_row128.ap[:, :], [[1, 8]], base=0, channel_multiplier=0,
                                    allow_small_or_imprecise_dtypes=True), outs=[iota8_row128])
    for idb, idf in ((ident_bf, None), (ident_f, None)):
        p.emit("pool", lambda e, b=idb: e.memset(b.ap[:, :], 0.0), outs=[idb])
        p.emit("pool", lambda e, b=idb: e.affine_select(
            b.ap[:, :], b.ap[:, :], [[-1, 128]], compare_op=ALU.not_equal,
            fill=1.0, base=0, channel_multiplier=1), ins=[idb], outs=[idb])
    for name, _ in HC_SPECS:
        if name in ("hc_cos", "hc_sin"):
            continue
        p.dma("sp", consts[name], dram[name])
    p.flush("consts")

    def load_fmaj(dst_buf, src_dram, nchunk=NDC, eng="sp"):
        war = p.begin_fill(dst_buf)
        last = None
        for c in range(nchunk):
            last = p.dma(eng, dst_buf.ap[:, c * T:(c + 1) * T],
                         src_dram[c * 128:(c + 1) * 128, :],
                         waits=war if c == 0 else (), key=dst_buf)
        p.setw(dst_buf, last)
        return last

    def store_fmaj(dst_dram, src_buf, nchunk=NDC, eng="sp"):
        for c in range(nchunk):
            p.dma(eng, dst_dram[c * 128:(c + 1) * 128, :],
                  src_buf.ap[:, c * T:(c + 1) * T], ins=[src_buf])

    def load_w_bf(name, src_dram, rows, cols, stack, eng="pool"):
        nch = rows // 128
        wbuf = sb(name, [128, nch * cols], BF16, stack)
        war = p.begin_fill(wbuf)
        last = p.dma(eng, wbuf.ap.rearrange("p (c k) -> p c k", c=nch),
                     src_dram.rearrange("(c p) k -> p c k", p=128),
                     waits=war, key=wbuf)
        p.setw(wbuf, last)
        return wbuf

    def hw_cast_load(dst_bf, src_dram_2d, nch, k, stg, eng="act"):
        war = p.begin_fill(stg)
        ld = p.dma(eng, stg.ap.rearrange("p (c k) -> p c k", c=nch),
                   src_dram_2d.rearrange("(c p) k -> p c k", p=128),
                   waits=war, key=stg)
        p.setw(stg, ld)
        dwar = p.begin_fill(dst_bf)
        tok = p.emit("dve", lambda e: e.tensor_copy(dst_bf.ap[:, :], stg.ap[:, :]),
                     ins=[stg], waits=dwar)
        p.setw(dst_bf, tok)
        return tok

    def load_col_vec(dst, src_dram_1d, n, psT128, eng="sp"):
        war = p.begin_fill(dst)
        cwar = p.begin_fill(cvrow)
        ld = p.dma(eng, cvrow.ap[0:n, 0:128],
                   src_dram_1d.rearrange("(a b) -> a b", a=n),
                   waits=list(war) + list(cwar), key=cvrow)
        p.setw(cvrow, ld)
        p.emit("pe", lambda e, n=n: e.transpose(psT128.ap[:, 0:n], cvrow.ap[0:n, 0:128],
                                                ident_f.ap[0:n, 0:n]),
               ins=[cvrow, ident_f], outs=[psT128])
        tok = p.emit("dve", lambda e, n=n: e.tensor_copy(dst.ap[:, 0:n],
                                                         psT128.ap[:, 0:n]),
                     ins=[psT128])
        p.setw(dst, tok)
        return tok

    def colsums(src_list, out_psum, square, stack):
        war = p.begin_fill(out_psum)
        sq = sb("csq", [128, T], F32, stack) if square else None
        first_tok = [True]
        for ci, (buf, off) in enumerate(src_list):
            if square:
                p.emit("act", lambda e, b=buf, o=off: e.activation(
                    sq.ap[:, :], b.ap[:, o:o + T], AF.Square), ins=[buf], outs=[sq])
                rbuf, roff = sq, 0
            else:
                rbuf, roff = buf, off
            for tt in range(NT):
                last = ci == NDC - 1 and tt == NT - 1
                tok = p.emit("pe", lambda e, rb=rbuf, ro=roff, tt=tt, ci=ci:
                             e.matmul(out_psum.ap[0:1, tt * TD:(tt + 1) * TD],
                                      ones_col.ap[:, 0:1],
                                      rb.ap[:, ro + tt * TD:ro + (tt + 1) * TD],
                                      start=(ci == 0), stop=(ci == NDC - 1)),
                             ins=[rbuf, ones_col],
                             waits=war if first_tok[0] else ())
                first_tok[0] = False
        p.setw(out_psum, tok)
        return tok

    def bcast(row_buf, part0_off, n, npart, out_ps):
        return p.emit("pe", lambda e: e.matmul(
            out_ps.ap[0:npart, 0:n], ones_row.ap[0:1, 0:npart],
            row_buf.ap[0:1, part0_off:part0_off + n], start=True, stop=True),
            ins=[row_buf, ones_row], outs=[out_ps])

    def phase_ln(src_dram, dst_f32_dram, dst_bf, eps, phname):
        stack = contextlib.ExitStack()
        xt = [sb(f"lnxt{i}", [128, T], F32, stack) for i in range(2)]
        sq = [sb(f"lnsq{i}", [128, T], F32, stack) for i in range(2)]
        stat = psb("lnstat", [1, T], F32, stack)
        mrow = sb("lnm", [1, T], F32, stack)
        srow = sb("lns", [1, T], F32, stack)
        trow = sb("lnt", [1, T], F32, stack)
        bps = psb("lnb", [128, TD], F32, stack)
        mB = sb("lnmB", [128, T], F32, stack)
        rB = sb("lnrB", [128, T], F32, stack)
        tmp = sb("lntmp", [128, T], F32, stack)
        for sweep, (dstrow, use_sq) in enumerate(((mrow, False), (srow, True))):
            war = p.begin_fill(stat)
            tok = None
            for c in range(NDC):
                b = c % 2
                p.dma("sp", xt[b], src_dram[c * 128:(c + 1) * 128, :])
                if use_sq:
                    p.emit("act", lambda e, b=b: e.activation(sq[b].ap[:, :],
                                                              xt[b].ap[:, :], AF.Square),
                           ins=[xt[b]], outs=[sq[b]])
                    rbuf = sq[b]
                else:
                    rbuf = xt[b]
                for tt in range(NT):
                    tok = p.emit("pe", lambda e, tt=tt, c=c, rb=rbuf: e.matmul(
                        stat.ap[0:1, tt * TD:(tt + 1) * TD], ones_col.ap[:, 0:1],
                        rb.ap[:, tt * TD:(tt + 1) * TD],
                        start=(c == 0), stop=(c == NDC - 1)),
                        ins=[rbuf, ones_col], waits=war if c == 0 and tt == 0 else ())
            p.setw(stat, tok)
            p.emit("dve", lambda e, dr=dstrow: e.tensor_scalar(
                dr.ap[0:1, :], stat.ap[0:1, :], 1.0 / D, None, ALU.mult),
                ins=[stat], outs=[dstrow])
        p.emit("dve", lambda e: e.tensor_mul(trow.ap[0:1, :], mrow.ap[0:1, :], mrow.ap[0:1, :]),
               ins=[mrow], outs=[trow])
        p.emit("dve", lambda e: e.tensor_sub(srow.ap[0:1, :], srow.ap[0:1, :], trow.ap[0:1, :]),
               ins=[trow, srow], outs=[srow])
        p.emit("dve", lambda e: e.tensor_scalar(srow.ap[0:1, :], srow.ap[0:1, :],
                                                eps, None, ALU.add), ins=[srow], outs=[srow])
        p.emit("act", lambda e: e.activation(srow.ap[0:1, :], srow.ap[0:1, :], AF.Sqrt),
               ins=[srow], outs=[srow])
        p.emit("dve", lambda e: e.reciprocal(srow.ap[0:1, :], srow.ap[0:1, :]),
               ins=[srow], outs=[srow])
        for dst, row in ((mB, mrow), (rB, srow)):
            war = p.begin_fill(dst)
            lastc = None
            for tt in range(NT):
                bcast(row, tt * TD, TD, 128, bps)
                lastc = p.emit("dve", lambda e, tt=tt, dst=dst: e.tensor_copy(
                    dst.ap[:, tt * TD:(tt + 1) * TD], bps.ap[:, 0:TD]),
                    ins=[bps], waits=war if tt == 0 else ())
            p.setw(dst, lastc)
        bfwar = p.begin_fill(dst_bf)
        lastbf = None
        tmp2 = sb("lntmp2", [128, T], F32, stack)
        tmps = (tmp, tmp2)
        for c in range(NDC):
            b = c % 2
            p.dma("sp", xt[b], src_dram[c * 128:(c + 1) * 128, :])
            p.emit("dve", lambda e, b=b: e.tensor_sub(tmps[b].ap[:, :], xt[b].ap[:, :],
                                                      mB.ap[:, :]),
                   ins=[xt[b], mB], outs=[tmps[b]])
            p.emit("dve", lambda e, b=b: e.tensor_mul(tmps[b].ap[:, :], tmps[b].ap[:, :],
                                                      rB.ap[:, :]),
                   ins=[rB], outs=[tmps[b]])
            lastbf = p.emit("dve", lambda e, c=c, b=b: e.tensor_copy(
                dst_bf.ap[:, c * T:(c + 1) * T], tmps[b].ap[:, :]),
                ins=[tmps[b]], waits=bfwar if c == 0 else ())
            p.dma("sp", dst_f32_dram[c * 128:(c + 1) * 128, :], tmps[b], ins=[tmps[b]])
        p.setw(dst_bf, lastbf)
        p.flush(phname)
        stack.close()

    phase_ln(dram["x"], dram["xn_f32"], xn_bf, 1e-5, "ln0")
    if upto <= 0:
        return nc, dram

    def phase_l0_attn():
        stackB = contextlib.ExitStack()
        qT = sb("qT", [128, NDC * T], BF16, stackB)
        kT = sb("kT", [128, NDC * T], BF16, stackB)
        vT = sb("vT", [128, NDC * T], BF16, stackB)
        lam_col = sb("lamcol", [128, 1], F32, stackB)
        nw02 = sb("nw02", [64, 16], F32, stackB)

        st0 = contextlib.ExitStack()
        lr = [sb(f"lam{i}", [1, 64], F32, st0) for i in range(4)]
        lsc = sb("lsc", [1, 4], F32, st0)
        psl = psb("psl", [128, 1], F32, st0)
        for i, wname in enumerate(("L0_attn_lambda_q1", "L0_attn_lambda_k1",
                                   "L0_attn_lambda_q2", "L0_attn_lambda_k2")):
            p.dma("sp", lr[i], dram[wname])
        for pair, (a, b_) in enumerate(((0, 1), (2, 3))):
            p.emit("dve", lambda e, a=a, b_=b_: e.tensor_mul(
                lr[a].ap[0:1, :], lr[a].ap[0:1, :], lr[b_].ap[0:1, :]),
                ins=[lr[b_], lr[a]], outs=[lr[a]])
            p.emit("dve", lambda e, a=a, pair=pair: e.reduce_sum(
                lsc.ap[0:1, pair:pair + 1], lr[a].ap[0:1, :], axis=AX.X),
                ins=[lr[a]], outs=[lsc] if pair else [], waits=())
        p.emit("act", lambda e: e.activation(lsc.ap[0:1, 0:2], lsc.ap[0:1, 0:2], AF.Exp),
               ins=[lsc], outs=[lsc])
        p.emit("dve", lambda e: e.tensor_sub(lsc.ap[0:1, 2:3], lsc.ap[0:1, 0:1],
                                             lsc.ap[0:1, 1:2]), ins=[lsc], outs=[lsc])
        p.emit("dve", lambda e: e.tensor_scalar(lsc.ap[0:1, 2:3], lsc.ap[0:1, 2:3],
                                                0.8, None, ALU.add), ins=[lsc], outs=[lsc])
        bcast(lsc, 2, 1, 128, psl)
        p.emit("dve", lambda e: e.tensor_copy(lam_col.ap[:, :], psl.ap[:, 0:1]),
               ins=[psl], outs=[lam_col])
        cwar = p.begin_fill(cvrow)
        ld = p.dma("sp", cvrow.ap[0:16, 0:64], dram["L0_attn_norm_w"],
                   waits=cwar, key=cvrow)
        p.setw(cvrow, ld)
        psnw = psb("psnw", [64, 16], F32, st0)
        p.emit("pe", lambda e: e.transpose(psnw.ap[:, :], cvrow.ap[0:16, 0:64],
                                           ident_f.ap[0:16, 0:16]),
               ins=[cvrow, ident_f], outs=[psnw])
        p.emit("dve", lambda e: e.tensor_scalar(nw02.ap[0:64, :], psnw.ap[:, :],
                                                0.2, None, ALU.mult), ins=[psnw], outs=[nw02])
        p.flush("b0")
        st0.close()

        st1 = contextlib.ExitStack()
        wq = load_w_bf("wqkv", dram["L0_attn_w_qkv"], D, 3 * D, st1)
        mmps = [psb(f"b1ps{i}", [128, TD], F32, st1) for i in range(2)]
        for fi, dst in ((0, qT), (1, kT), (2, vT)):
            war = p.begin_fill(dst)
            lastw = None
            for fc in range(NDC):
                for tt in range(NT):
                    pp = mmps[(fc * NT + tt) % 2]
                    ptok = None
                    for c in range(NDC):
                        ptok = p.emit("pe", lambda e, fi=fi, fc=fc, tt=tt, c=c, pp=pp: e.matmul(
                            pp.ap[:, :],
                            wq.ap[:, c * 3 * D + fi * D + fc * 128:
                                  c * 3 * D + fi * D + fc * 128 + 128],
                            xn_bf.ap[:, c * T + tt * TD:c * T + (tt + 1) * TD],
                            start=(c == 0), stop=(c == NDC - 1)),
                            ins=[wq, xn_bf], outs=[pp] if c == 0 else (),
                            waits=())
                    p.setw(pp, ptok)
                    lastw = p.emit("dve", lambda e, fc=fc, tt=tt, dst=dst, pp=pp: e.tensor_copy(
                        dst.ap[:, fc * T + tt * TD:fc * T + (tt + 1) * TD], pp.ap[:, :]),
                        ins=[pp], waits=war if fc == 0 and tt == 0 else ())
            p.setw(dst, lastw)
        p.flush("b1")
        st1.close()

        OT2 = sb("OT2", [128, NDC * T], BF16, stackB)
        st2 = contextlib.ExitStack()
        psS1 = psb("psS1", [128, 2 * TD], F32, st2)
        psS2 = psb("psS2", [128, 2 * TD], F32, st2)
        psPV1 = psb("psPV1", [65, TD], F32, st2)
        psPV2 = psb("psPV2", [65, TD], F32, st2)
        psB64 = psb("psB64", [128, TD], F32, st2)
        psT = psb("psT", [128, 64], BF16, st2)
        P1 = sb("P1", [128, 2 * TD], BF16, st2)
        P2 = sb("P2", [128, 2 * TD], BF16, st2)
        S1h = [Buf(psS1.ap[:, 0:TD]), Buf(psS1.ap[:, TD:2 * TD])]
        S2h = [Buf(psS2.ap[:, 0:TD]), Buf(psS2.ap[:, TD:2 * TD])]
        P1h = [Buf(P1.ap[:, 0:TD]), Buf(P1.ap[:, TD:2 * TD])]
        P2h = [Buf(P2.ap[:, 0:TD]), Buf(P2.ap[:, TD:2 * TD])]
        v_tok = sb("v_tok", [128, 16 * 65], BF16, st2)
        sqs_ = sb("sqs", [128, TD], F32, st2)
        negMb = sb("negMb", [128, 4], F32, st2)
        qmax = sb("qmax", [4, 1], F32, st2)
        kmax = sb("kmax", [4, 1], F32, st2)
        m4 = sb("m4", [4, 1], F32, st2)
        nrow = sb("nrow", [1, 4], F32, st2)
        drow = sb("drow", [1, 2 * TD], F32, st2)
        rb1 = sb("rb1", [64, TD], F32, st2)
        rb2 = sb("rb2", [64, TD], F32, st2)
        oa = sb("oa", [64, TD], F32, st2)
        rrow = sb("rrow", [1, TD], F32, st2)
        out_n = sb("out_n", [64, T], BF16, st2)
        OT2v = OT2.ap.rearrange("p (c d x) -> p c d x", c=8, d=64, x=32)

        def head_norms(src, dst, qc):
            war = p.begin_fill(dst)
            lt = None
            for tt in range(NT):
                p.emit("act", lambda e, tt=tt, qc=qc, src=src: e.activation(
                    sqs_.ap[:, :],
                    src.ap[:, qc * T + tt * TD:qc * T + (tt + 1) * TD], AF.Square),
                    ins=[src], outs=[sqs_])
                p.emit("pe", lambda e, tt=tt: e.matmul(
                    psB64.ap[0:4, :], consts["hc_bo4"].ap[:, :],
                    sqs_.ap[:, :], start=True, stop=True),
                    ins=[sqs_, consts["hc_bo4"]], outs=[psB64])
                if tt == 0:
                    lt = p.emit("dve", lambda e: e.reduce_max(
                        dst.ap[:, 0:1], psB64.ap[0:4, :], axis=AX.X),
                        ins=[psB64], waits=war)
                else:
                    p.emit("dve", lambda e: e.reduce_max(
                        m4.ap[:, 0:1], psB64.ap[0:4, :], axis=AX.X), ins=[psB64], outs=[m4])
                    lt = p.emit("dve", lambda e: e.tensor_max(
                        dst.ap[:, 0:1], dst.ap[:, 0:1], m4.ap[:, 0:1]),
                        ins=[m4], waits=[lt])
            p.setw(dst, lt)

        for qc in range(NDC):
            head_norms(qT, qmax, qc)
            head_norms(kT, kmax, qc)
            p.emit("dve", lambda e: e.tensor_mul(qmax.ap[:, :], qmax.ap[:, :], kmax.ap[:, :]),
                   ins=[kmax, qmax], outs=[qmax])
            p.emit("act", lambda e: e.activation(qmax.ap[:, :], qmax.ap[:, :], AF.Sqrt),
                   ins=[qmax], outs=[qmax])
            p.emit("dve", lambda e: e.tensor_scalar(qmax.ap[:, :], qmax.ap[:, :],
                                                    -0.125, None, ALU.mult),
                   ins=[qmax], outs=[qmax])
            p.emit("pe", lambda e: e.transpose(psB64.ap[0:1, 0:4], qmax.ap[:, 0:1],
                                               ident_f.ap[0:4, 0:4]),
                   ins=[qmax, ident_f], outs=[psB64])
            p.emit("dve", lambda e: e.tensor_copy(nrow.ap[0:1, :], psB64.ap[0:1, 0:4]),
                   ins=[psB64], outs=[nrow])
            bcast(nrow, 0, 4, 128, psB64)
            p.emit("dve", lambda e: e.tensor_copy(negMb.ap[:, :], psB64.ap[:, 0:4]),
                   ins=[psB64], outs=[negMb])

            for h2 in range(2):
                hb = 64 * h2
                head = qc * 2 + h2
                vwar = p.begin_fill(v_tok)
                p.emit("pool", lambda e: e.memset(v_tok.ap[:, :], 1.0), waits=vwar)
                vlast = None
                for cc in range(16):
                    p.emit("pe", lambda e, cc=cc, hb=hb, qc=qc: e.transpose(
                        psT.ap[:, :], vT.ap[hb:hb + 64, qc * T + cc * 128:qc * T + (cc + 1) * 128],
                        ident_bf.ap[hb:hb + 64, hb:hb + 64]),
                        ins=[vT, ident_bf], outs=[psT])
                    vlast = p.emit("dve", lambda e, cc=cc: e.tensor_copy(
                        v_tok.ap[:, cc * 65:cc * 65 + 64], psT.ap[:, :]), ins=[psT])
                p.setw(v_tok, vlast)
                for it in range(NT):
                    pv1war = p.begin_fill(psPV1)
                    pv2war = p.begin_fill(psPV2)
                    pvtok1 = pvtok2 = None

                    def b2_scores(jc, hb=hb, qc=qc, h2=h2, it=it):
                        jj = jc % 2
                        p.emit("pe", lambda e, jc=jc, jj=jj: e.matmul(
                            S1h[jj].ap,
                            kT.ap[hb:hb + 32, qc * T + jc * 128:qc * T + (jc + 1) * 128],
                            qT.ap[hb:hb + 32, qc * T + it * TD:qc * T + (it + 1) * TD],
                            start=True, stop=True, tile_position=(hb, 0)),
                            ins=[kT, qT], outs=[S1h[jj]])
                        p.emit("pe", lambda e, jc=jc, jj=jj: e.matmul(
                            S2h[jj].ap,
                            kT.ap[hb + 32:hb + 64, qc * T + jc * 128:qc * T + (jc + 1) * 128],
                            qT.ap[hb + 32:hb + 64, qc * T + it * TD:qc * T + (it + 1) * TD],
                            start=True, stop=True, tile_position=(hb + 32, 0)),
                            ins=[kT, qT], outs=[S2h[jj]])
                        p.emit("act", lambda e, jj=jj: e.activation(
                            P1h[jj].ap, S1h[jj].ap, AF.Exp,
                            bias=negMb.ap[:, 2 * h2:2 * h2 + 1], scale=0.125),
                            ins=[S1h[jj], negMb], outs=[P1h[jj]])
                        p.emit("act", lambda e, jj=jj: e.activation(
                            P2h[jj].ap, S2h[jj].ap, AF.Exp,
                            bias=negMb.ap[:, 2 * h2 + 1:2 * h2 + 2], scale=0.125),
                            ins=[S2h[jj], negMb], outs=[P2h[jj]])

                    def b2_pv(jc):
                        nonlocal pvtok1, pvtok2
                        jj = jc % 2
                        pvtok1 = p.emit("pe", lambda e, jc=jc, jj=jj: e.matmul(
                            psPV1.ap[:, :], v_tok.ap[:, jc * 65:(jc + 1) * 65],
                            P1h[jj].ap, start=(jc == 0), stop=(jc == 15)),
                            ins=[v_tok, P1h[jj]],
                            waits=pv1war if jc == 0 else ())
                        pvtok2 = p.emit("pe", lambda e, jc=jc, jj=jj: e.matmul(
                            psPV2.ap[:, :], v_tok.ap[:, jc * 65:(jc + 1) * 65],
                            P2h[jj].ap, start=(jc == 0), stop=(jc == 15)),
                            ins=[v_tok, P2h[jj]],
                            waits=pv2war if jc == 0 else ())

                    for jc in range(16):
                        if jc >= 2:
                            b2_pv(jc - 2)
                        b2_scores(jc)
                    b2_pv(14)
                    b2_pv(15)
                    p.setw(psPV1, pvtok1)
                    p.setw(psPV2, pvtok2)
                    p.emit("dve", lambda e: e.tensor_copy(drow.ap[0:1, 0:TD],
                                                          psPV1.ap[64:65, :]),
                           ins=[psPV1], outs=[drow])
                    p.emit("dve", lambda e: e.tensor_copy(drow.ap[0:1, TD:2 * TD],
                                                          psPV2.ap[64:65, :]),
                           ins=[psPV2, drow], outs=[drow])
                    p.emit("dve", lambda e: e.reciprocal(drow.ap[0:1, :], drow.ap[0:1, :]),
                           ins=[drow], outs=[drow])
                    bcast(drow, 0, TD, 64, psB64)
                    p.emit("dve", lambda e: e.tensor_copy(rb1.ap[:, :], psB64.ap[0:64, 0:TD]),
                           ins=[psB64], outs=[rb1])
                    bcast(drow, TD, TD, 64, psB64)
                    p.emit("dve", lambda e: e.tensor_copy(rb2.ap[:, :], psB64.ap[0:64, 0:TD]),
                           ins=[psB64], outs=[rb2])
                    p.emit("dve", lambda e: e.tensor_mul(rb1.ap[:, :], psPV1.ap[0:64, :],
                                                         rb1.ap[:, :]),
                           ins=[psPV1], outs=[rb1])
                    p.emit("dve", lambda e: e.tensor_mul(rb2.ap[:, :], psPV2.ap[0:64, :],
                                                         rb2.ap[:, :]),
                           ins=[psPV2], outs=[rb2])
                    p.emit("dve", lambda e: e.tensor_scalar(
                        rb2.ap[:, :], rb2.ap[:, :], lam_col.ap[0:64, 0:1], None, ALU.mult),
                        ins=[lam_col], outs=[rb2])
                    p.emit("dve", lambda e: e.tensor_sub(oa.ap[:, :], rb1.ap[:, :],
                                                         rb2.ap[:, :]),
                           ins=[rb1, rb2], outs=[oa])
                    p.emit("act", lambda e: e.activation(rb1.ap[:, :], oa.ap[:, :], AF.Square),
                           ins=[oa], outs=[rb1])
                    p.emit("pe", lambda e: e.matmul(psB64.ap[0:1, :], ones_col.ap[0:64, 0:1],
                                                    rb1.ap[:, :], start=True, stop=True),
                           ins=[rb1, ones_col], outs=[psB64])
                    p.emit("dve", lambda e: e.tensor_scalar(rrow.ap[0:1, :], psB64.ap[0:1, 0:TD],
                                                            1.0 / 64, 1e-6, ALU.mult, ALU.add),
                           ins=[psB64], outs=[rrow])
                    p.emit("act", lambda e: e.activation(rrow.ap[0:1, :], rrow.ap[0:1, :],
                                                         AF.Sqrt), ins=[rrow], outs=[rrow])
                    p.emit("dve", lambda e: e.reciprocal(rrow.ap[0:1, :], rrow.ap[0:1, :]),
                           ins=[rrow], outs=[rrow])
                    bcast(rrow, 0, TD, 64, psB64)
                    p.emit("dve", lambda e: e.tensor_copy(rb1.ap[:, :], psB64.ap[0:64, 0:TD]),
                           ins=[psB64], outs=[rb1])
                    p.emit("dve", lambda e: e.tensor_mul(oa.ap[:, :], oa.ap[:, :], rb1.ap[:, :]),
                           ins=[rb1, oa], outs=[oa])
                    p.emit("dve", lambda e, it=it, head=head: e.tensor_scalar(
                        out_n.ap[:, it * TD:(it + 1) * TD], oa.ap[:, :],
                        nw02.ap[0:64, head:head + 1], None, ALU.mult),
                        ins=[oa, nw02], outs=[out_n] if it == NT - 1 else ())
                for cc in range(16):
                    thi, c_lo = cc // 8, cc % 8
                    p.emit("pe", lambda e, cc=cc: e.transpose(
                        psT.ap[:, :], out_n.ap[:, cc * 128:(cc + 1) * 128],
                        ident_bf.ap[0:64, 0:64]),
                        ins=[out_n, ident_bf], outs=[psT])
                    p.emit("dve", lambda e, c_lo=c_lo, thi=thi, head=head: e.tensor_copy(
                        OT2v[:, c_lo, :, head * 2 + thi], psT.ap[:, :]),
                        ins=[psT], outs=[OT2] if cc == 15 else ())
        p.flush("b2")
        st2.close()

        st3 = contextlib.ExitStack()
        wo = load_w_bf("wo", dram["L0_attn_wo"], D, D, st3)
        wps = [psb(f"b3ps{i}", [128, TD], F32, st3) for i in range(2)]
        stile = sb("stile", [128, TD], F32, st3)
        for ec in range(NDC):
            for it in range(NT):
                pp = wps[(ec * NT + it) % 2]
                ptok = None
                for cl in range(NDC):
                    ptok = p.emit("pe", lambda e, ec=ec, it=it, cl=cl, pp=pp: e.matmul(
                        pp.ap[:, :],
                        wo.ap[:, cl * D + ec * 128:cl * D + ec * 128 + 128],
                        OT2.ap[:, cl * T + it * TD:cl * T + (it + 1) * TD],
                        start=(cl == 0), stop=(cl == NDC - 1)),
                        ins=[wo, OT2], outs=[pp] if cl == 0 else ())
                p.setw(pp, ptok)
                p.emit("dve", lambda e, pp=pp: e.tensor_copy(stile.ap[:, :], pp.ap[:, :]),
                       ins=[pp], outs=[stile])
                p.dma("sp", dram["s0"][ec * 128:(ec + 1) * 128, it * TD:(it + 1) * TD],
                      stile, ins=[stile])
        p.flush("b3")
        st3.close()
        stackB.close()

    phase_l0_attn()
    if upto <= 1:
        return nc, dram

    stackL0 = contextlib.ExitStack()

    def phase_l0_ffn_attn():
        stackC = contextlib.ExitStack()
        stackQKV = contextlib.ExitStack()
        stackU = contextlib.ExitStack()
        qT = sb("qT2", [128, NDC * T], BF16, stackQKV, side="right")
        kT = sb("kT2", [128, 2 * T], BF16, stackQKV, side="right")
        vT = sb("vT2", [128, 2 * T], BF16, stackQKV, side="right")

        st0 = contextlib.ExitStack()
        u_bf = sb("u_bf", [128, NDC * T], BF16, stackU)
        g1col = sb("g1col", [128, 8], F32, st0)
        stat = psb("c0stat", [1, T], F32, st0)
        rrow = sb("c0r", [1, T], F32, st0)
        bps = psb("c0b", [128, TD], F32, st0)
        rB = sb("c0rB", [128, T], F32, st0)
        sq = sb("c0sq", [128, TD], F32, st0)
        tmpc = sb("c0tmp", [128, T], F32, st0)
        load_col_vec(g1col, dram["L0_ffn_g1"], 8, bps)
        swar = p.begin_fill(stat)
        stok = None
        for c in range(NDC):
            for tt in range(NT):
                p.emit("act", lambda e, c=c, tt=tt: e.activation(
                    sq.ap[:, :], xn_bf.ap[:, c * T + tt * TD:c * T + (tt + 1) * TD],
                    AF.Square), ins=[xn_bf], outs=[sq])
                stok = p.emit("pe", lambda e, c=c, tt=tt: e.matmul(
                    stat.ap[0:1, tt * TD:(tt + 1) * TD], ones_col.ap[:, 0:1],
                    sq.ap[:, :], start=(c == 0), stop=(c == NDC - 1)),
                    ins=[sq, ones_col], waits=swar if c == 0 and tt == 0 else ())
        p.setw(stat, stok)
        p.emit("dve", lambda e: e.tensor_scalar(rrow.ap[0:1, :], stat.ap[0:1, :],
                                                1.0 / D, 1e-6, ALU.mult, ALU.add),
               ins=[stat], outs=[rrow])
        p.emit("act", lambda e: e.activation(rrow.ap[0:1, :], rrow.ap[0:1, :], AF.Sqrt),
               ins=[rrow], outs=[rrow])
        p.emit("dve", lambda e: e.reciprocal(rrow.ap[0:1, :], rrow.ap[0:1, :]),
               ins=[rrow], outs=[rrow])
        rwar = p.begin_fill(rB)
        rl = None
        for tt in range(NT):
            bcast(rrow, tt * TD, TD, 128, bps)
            rl = p.emit("dve", lambda e, tt=tt: e.tensor_copy(
                rB.ap[:, tt * TD:(tt + 1) * TD], bps.ap[:, 0:TD]),
                ins=[bps], waits=rwar if tt == 0 else ())
        p.setw(rB, rl)
        uwar = p.begin_fill(u_bf)
        ul = None
        for c in range(NDC):
            p.emit("dve", lambda e, c=c: e.tensor_scalar(
                tmpc.ap[:, :], xn_bf.ap[:, c * T:(c + 1) * T],
                g1col.ap[:, c:c + 1], None, ALU.mult),
                ins=[xn_bf, g1col], outs=[tmpc])
            ul = p.emit("dve", lambda e, c=c: e.tensor_mul(
                u_bf.ap[:, c * T:(c + 1) * T], tmpc.ap[:, :], rB.ap[:, :]),
                ins=[tmpc, rB], waits=uwar if c == 0 else ())
        p.setw(u_bf, ul)
        p.flush("c0")
        st0.close()

        st1 = contextlib.ExitStack()
        wqb = load_w_bf("wqf", dram["L0_ffn_wq"], D, D, st1)
        wkb = load_w_bf("wkf", dram["L0_ffn_wk"], D, 256, st1)
        wvb = load_w_bf("wvf", dram["L0_ffn_wv"], D, 256, st1)
        mmps = [psb(f"c1ps{i}", [128, TD], F32, st1) for i in range(2)]
        jobs = [(wqb, D, qT, NDC), (wkb, 256, kT, 2), (wvb, 256, vT, 2)]
        for wb, wcols, dst, nfc in jobs:
            war = p.begin_fill(dst)
            lastw = None
            for fc in range(nfc):
                for tt in range(NT):
                    pp = mmps[(fc * NT + tt) % 2]
                    ptok = None
                    for c in range(NDC):
                        ptok = p.emit("pe", lambda e, wb=wb, wcols=wcols, fc=fc, tt=tt, c=c, pp=pp:
                                      e.matmul(pp.ap[:, :],
                                               wb.ap[:, c * wcols + fc * 128:c * wcols + fc * 128 + 128],
                                               u_bf.ap[:, c * T + tt * TD:c * T + (tt + 1) * TD],
                                               start=(c == 0), stop=(c == NDC - 1)),
                                      ins=[wb, u_bf], outs=[pp] if c == 0 else ())
                    p.setw(pp, ptok)
                    lastw = p.emit("dve", lambda e, fc=fc, tt=tt, dst=dst, pp=pp: e.tensor_copy(
                        dst.ap[:, fc * T + tt * TD:fc * T + (tt + 1) * TD], pp.ap[:, :]),
                        ins=[pp], waits=war if fc == 0 and tt == 0 else ())
            p.setw(dst, lastw)
        p.flush("c1")
        st1.close()
        stackU.close()

        st2 = contextlib.ExitStack()
        cosb = sb("cosb", [128, T], F32, st2)
        sinb = sb("sinb", [128, T], F32, st2)
        rt1 = sb("rt1", [128, T], F32, st2)
        rt2 = sb("rt2", [128, T], F32, st2)
        p.dma("sp", cosb, dram["hc_cos"])
        p.dma("sp", sinb, dram["hc_sin"])
        for dst, nfc in ((qT, NDC), (kT, 2)):
            for fc in range(nfc):
                o = fc * T
                for hb in (0, 64):
                    for blk, srcb in enumerate((hb + 32, hb)):
                        tb = hb + 32 * blk
                        p.emit("dve", lambda e, tb=tb, srcb=srcb, o=o, dst=dst:
                               e.tensor_mul(rt1.ap[tb:tb + 32, :],
                                            dst.ap[srcb:srcb + 32, o:o + T],
                                            sinb.ap[srcb:srcb + 32, :]),
                               ins=[dst, sinb], outs=[rt1] if tb == hb + 32 else [])
                        p.emit("dve", lambda e, tb=tb, o=o, dst=dst:
                               e.tensor_mul(rt2.ap[tb:tb + 32, :],
                                            dst.ap[tb:tb + 32, o:o + T],
                                            cosb.ap[tb:tb + 32, :]),
                               ins=[dst, cosb], outs=[rt2] if tb == hb + 32 else [])
                    p.emit("dve", lambda e, hb=hb, o=o, dst=dst: e.tensor_sub(
                        dst.ap[hb:hb + 32, o:o + T], rt2.ap[hb:hb + 32, :],
                        rt1.ap[hb:hb + 32, :]),
                        ins=[rt1, rt2], outs=[dst])
                    p.emit("dve", lambda e, hb=hb, o=o, dst=dst: e.tensor_add(
                        dst.ap[hb + 32:hb + 64, o:o + T], rt2.ap[hb + 32:hb + 64, :],
                        rt1.ap[hb + 32:hb + 64, :]),
                        ins=[rt1, rt2], outs=[dst])
        p.flush("c2")
        st2.close()

        st3 = contextlib.ExitStack()
        aoT = sb("aoT", [128, NDC * T], BF16, stackC)
        psS = psb("psSg", [128, 2 * TD], F32, st3)
        psPV = psb("psPVg", [65, TD], F32, st3)
        psBq = psb("psBq", [128, TD], F32, st3)
        psTt = psb("psTg", [128, 64], BF16, st3)
        Pt = sb("Ptg", [128, 2 * TD], BF16, st3)
        Sh3 = [Buf(psS.ap[:, 0:TD]), Buf(psS.ap[:, TD:2 * TD])]
        Ph3 = [Buf(Pt.ap[:, 0:TD]), Buf(Pt.ap[:, TD:2 * TD])]
        v_tok4 = sb("v_tok4", [128, 4 * 16 * 65], BF16, st3)
        sqs_ = sb("sqsg", [128, TD], F32, st3)
        s2 = sb("s2g", [2, TD], F32, st3)
        m2 = sb("m2g", [2, 1], F32, st3)
        qmax2 = sb("qmax2", [2, 1], F32, st3)
        kmaxA = sb("kmaxA", [2, 1], F32, st3)
        kmaxB = sb("kmaxB", [2, 1], F32, st3)
        negMb2 = sb("negMb2", [128, 2], F32, st3)
        nrow2 = sb("nrow2", [1, 2], F32, st3)
        drow = sb("drowg", [1, TD], F32, st3)
        rbg = sb("rbg", [64, TD], F32, st3)

        def head_norms2(srcb, off, dst):
            war = p.begin_fill(dst)
            lt = None
            for tt in range(NT):
                p.emit("act", lambda e, tt=tt, srcb=srcb, off=off: e.activation(
                    sqs_.ap[:, :], srcb.ap[:, off + tt * TD:off + (tt + 1) * TD],
                    AF.Square), ins=[srcb], outs=[sqs_])
                p.emit("pe", lambda e: e.matmul(
                    psBq.ap[0:2, :], consts["hc_bo2"].ap[:, :], sqs_.ap[:, :],
                    start=True, stop=True), ins=[sqs_, consts["hc_bo2"]], outs=[psBq])
                p.emit("dve", lambda e: e.tensor_copy(s2.ap[:, :], psBq.ap[0:2, :]),
                       ins=[psBq], outs=[s2])
                if tt == 0:
                    lt = p.emit("dve", lambda e, dst=dst: e.reduce_max(
                        dst.ap[:, 0:1], s2.ap[:, :], axis=AX.X), ins=[s2], waits=war)
                else:
                    p.emit("dve", lambda e: e.reduce_max(
                        m2.ap[:, 0:1], s2.ap[:, :], axis=AX.X), ins=[s2], outs=[m2])
                    lt = p.emit("dve", lambda e, dst=dst: e.tensor_max(
                        dst.ap[:, 0:1], dst.ap[:, 0:1], m2.ap[:, 0:1]),
                        ins=[m2], waits=[lt])
            p.setw(dst, lt)

        head_norms2(kT, 0, kmaxA)
        head_norms2(kT, T, kmaxB)
        vwar = p.begin_fill(v_tok4)
        p.emit("pool", lambda e: e.memset(v_tok4.ap[:, :], 1.0), waits=vwar)
        vlast = None
        for kv in range(4):
            kc, kb = kv // 2, 64 * (kv % 2)
            for cc in range(16):
                p.emit("pe", lambda e, kc=kc, kb=kb, cc=cc: e.transpose(
                    psTt.ap[:, :], vT.ap[kb:kb + 64, kc * T + cc * 128:kc * T + (cc + 1) * 128],
                    ident_bf.ap[kb:kb + 64, kb:kb + 64]),
                    ins=[vT, ident_bf], outs=[psTt])
                vlast = p.emit("dve", lambda e, kv=kv, cc=cc: e.tensor_copy(
                    v_tok4.ap[:, kv * 1040 + cc * 65:kv * 1040 + cc * 65 + 64],
                    psTt.ap[:, :]), ins=[psTt])
        p.setw(v_tok4, vlast)

        for qc in range(NDC):
            head_norms2(qT, qc * T, qmax2)
            kmx = kmaxA if qc % 2 == 0 else kmaxB
            p.emit("dve", lambda e, kmx=kmx: e.tensor_mul(qmax2.ap[:, :], qmax2.ap[:, :],
                                                          kmx.ap[:, :]),
                   ins=[kmx, qmax2], outs=[qmax2])
            p.emit("act", lambda e: e.activation(qmax2.ap[:, :], qmax2.ap[:, :], AF.Sqrt),
                   ins=[qmax2], outs=[qmax2])
            p.emit("dve", lambda e: e.tensor_scalar(qmax2.ap[:, :], qmax2.ap[:, :],
                                                    -0.125, None, ALU.mult),
                   ins=[qmax2], outs=[qmax2])
            p.emit("pe", lambda e: e.transpose(psBq.ap[0:1, 0:2], qmax2.ap[:, 0:1],
                                               ident_f.ap[0:2, 0:2]),
                   ins=[qmax2, ident_f], outs=[psBq])
            p.emit("dve", lambda e: e.tensor_copy(nrow2.ap[0:1, :], psBq.ap[0:1, 0:2]),
                   ins=[psBq], outs=[nrow2])
            bcast(nrow2, 0, 2, 128, psBq)
            p.emit("dve", lambda e: e.tensor_copy(negMb2.ap[:, :], psBq.ap[:, 0:2]),
                   ins=[psBq], outs=[negMb2])
            for h2 in range(2):
                hb = 64 * h2
                head = qc * 2 + h2
                kv = head % 4
                kc, kb = kv // 2, 64 * (kv % 2)
                for it in range(NT):
                    njc = 4 * it + 4
                    pvwar = p.begin_fill(psPV)
                    pvtok = None

                    def c3_scores(jc, hb=hb, qc=qc, h2=h2, it=it, kb=kb, kc=kc):
                        jj = jc % 2
                        p.emit("pe", lambda e, jc=jc, jj=jj: e.matmul(
                            Sh3[jj].ap,
                            kT.ap[kb:kb + 64, kc * T + jc * 128:kc * T + (jc + 1) * 128],
                            qT.ap[hb:hb + 64, qc * T + it * TD:qc * T + (it + 1) * TD],
                            start=True, stop=True, tile_position=(kb, 0)),
                            ins=[kT, qT], outs=[Sh3[jj]])
                        p.emit("act", lambda e, jj=jj: e.activation(
                            Ph3[jj].ap, Sh3[jj].ap, AF.Exp,
                            bias=negMb2.ap[:, h2:h2 + 1], scale=0.125),
                            ins=[Sh3[jj], negMb2], outs=[Ph3[jj]])
                        if jc >= 4 * it:
                            p.emit("pool", lambda e, jc=jc, jj=jj: e.affine_select(
                                Ph3[jj].ap, Ph3[jj].ap,
                                [[1, TD]], compare_op=ALU.is_ge, fill=0.0,
                                base=it * TD - jc * 128, channel_multiplier=-1),
                                ins=[Ph3[jj]], outs=[Ph3[jj]])

                    def c3_pv(jc, kv=kv, njc=njc):
                        nonlocal pvtok
                        jj = jc % 2
                        pvtok = p.emit("pe", lambda e, jc=jc, jj=jj: e.matmul(
                            psPV.ap[:, :],
                            v_tok4.ap[:, kv * 1040 + jc * 65:kv * 1040 + (jc + 1) * 65],
                            Ph3[jj].ap, start=(jc == 0), stop=(jc == njc - 1)),
                            ins=[v_tok4, Ph3[jj]],
                            waits=pvwar if jc == 0 else ())

                    for jc in range(njc):
                        if jc >= 2:
                            c3_pv(jc - 2)
                        c3_scores(jc)
                    c3_pv(njc - 2)
                    c3_pv(njc - 1)
                    p.setw(psPV, pvtok)
                    p.emit("dve", lambda e: e.tensor_copy(drow.ap[0:1, :], psPV.ap[64:65, :]),
                           ins=[psPV], outs=[drow])
                    p.emit("dve", lambda e: e.reciprocal(drow.ap[0:1, :], drow.ap[0:1, :]),
                           ins=[drow], outs=[drow])
                    bcast(drow, 0, TD, 64, psBq)
                    p.emit("dve", lambda e: e.tensor_copy(rbg.ap[:, :], psBq.ap[0:64, 0:TD]),
                           ins=[psBq], outs=[rbg])
                    p.emit("dve", lambda e, hb=hb, qc=qc, it=it: e.tensor_mul(
                        aoT.ap[hb:hb + 64, qc * T + it * TD:qc * T + (it + 1) * TD],
                        psPV.ap[0:64, :], rbg.ap[:, :]),
                        ins=[psPV, rbg], outs=[aoT])
        p.flush("c3")
        st3.close()
        stackQKV.close()

        st4 = contextlib.ExitStack()
        wo_f = load_w_bf("wof", dram["L0_ffn_wo"], D, D, st4)
        wps = [psb(f"c4ps{i}", [128, TD], F32, st4) for i in range(2)]
        xnt = sb("c4xn", [128, TD], F32, st4)
        htile = sb("c4h", [128, TD], F32, st4)
        for ec in range(NDC):
            for it in range(NT):
                pp = wps[(ec * NT + it) % 2]
                ptok = None
                for cl in range(NDC):
                    ptok = p.emit("pe", lambda e, ec=ec, it=it, cl=cl, pp=pp: e.matmul(
                        pp.ap[:, :],
                        wo_f.ap[:, cl * D + ec * 128:cl * D + ec * 128 + 128],
                        aoT.ap[:, cl * T + it * TD:cl * T + (it + 1) * TD],
                        start=(cl == 0), stop=(cl == NDC - 1)),
                        ins=[wo_f, aoT], outs=[pp] if cl == 0 else ())
                p.setw(pp, ptok)
                p.dma("sp", xnt, dram["xn_f32"][ec * 128:(ec + 1) * 128,
                                                it * TD:(it + 1) * TD])
                p.emit("dve", lambda e, pp=pp: e.tensor_add(htile.ap[:, :], pp.ap[:, :],
                                                            xnt.ap[:, :]),
                       ins=[pp, xnt], outs=[htile])
                p.emit("dve", lambda e, ec=ec, it=it: e.tensor_copy(
                    st2_bf.ap[:, ec * T + it * TD:ec * T + (it + 1) * TD], htile.ap[:, :]),
                    ins=[htile], outs=[st2_bf] if ec == NDC - 1 and it == NT - 1 else ())
                p.dma("sp", dram["h1_f32"][ec * 128:(ec + 1) * 128, it * TD:(it + 1) * TD],
                      htile, ins=[htile])
        p.flush("c4")
        st4.close()
        stackC.close()

    stackH1 = contextlib.ExitStack()
    st2_bf = sb("st2_bf", [128, NDC * T], BF16, stackH1)
    phase_l0_ffn_attn()
    if upto <= 2:
        return nc, dram

    z_bf = sb("z_bf", [128, NDC * T], BF16, stackL0, side="right")

    def rmsnorm_into(src_bf, g_dram, dst_bf, stack, side="left", router=None):
        g1col = sb("rng", [128, 8], F32, stack, side=side)
        stat = psb("rnstat", [1, 2 * TD], F32, stack)
        rrow = sb("rnr", [1, T], F32, stack, side=side)
        bps = psb("rnb", [128, TD], F32, stack)
        rB = sb("rnrB", [128, T], F32, stack, side=side)
        sq = sb("rnsq", [128, TD], F32, stack, side=side)
        tmpc = sb("rntmp", [128, T], F32, stack, side=side)
        gwar = p.begin_fill(g1col)
        gl = None
        for c in range(NDC):
            gl = p.dma("sp", g1col.ap[:, c:c + 1],
                       g_dram[c * 128:(c + 1) * 128].rearrange("(d o) -> d o", o=1),
                       waits=gwar if c == 0 else (), key=g1col)
        p.setw(g1col, gl)
        for half in range(2):
            swar = p.begin_fill(stat)
            stok = None
            for c in range(NDC):
                for t2 in range(2):
                    tt = half * 2 + t2
                    p.emit("act", lambda e, c=c, tt=tt: e.activation(
                        sq.ap[:, :], src_bf.ap[:, c * T + tt * TD:c * T + (tt + 1) * TD],
                        AF.Square), ins=[src_bf], outs=[sq])
                    stok = p.emit("pe", lambda e, c=c, t2=t2: e.matmul(
                        stat.ap[0:1, t2 * TD:(t2 + 1) * TD], ones_col.ap[:, 0:1],
                        sq.ap[:, :], start=(c == 0), stop=(c == NDC - 1)),
                        ins=[sq, ones_col], waits=swar if c == 0 and t2 == 0 else ())
            p.setw(stat, stok)
            p.emit("dve", lambda e, half=half: e.tensor_scalar(
                rrow.ap[0:1, half * 2 * TD:(half + 1) * 2 * TD], stat.ap[0:1, :],
                1.0 / D, 1e-6, ALU.mult, ALU.add),
                ins=[stat], outs=[rrow] if half == 1 else [])
        p.emit("act", lambda e: e.activation(rrow.ap[0:1, :], rrow.ap[0:1, :], AF.Sqrt),
               ins=[rrow], outs=[rrow])
        p.emit("dve", lambda e: e.reciprocal(rrow.ap[0:1, :], rrow.ap[0:1, :]),
               ins=[rrow], outs=[rrow])
        rwar = p.begin_fill(rB)
        rl = None
        for tt in range(NT):
            bcast(rrow, tt * TD, TD, 128, bps)
            rl = p.emit("dve", lambda e, tt=tt: e.tensor_copy(
                rB.ap[:, tt * TD:(tt + 1) * TD], bps.ap[:, 0:TD]),
                ins=[bps], waits=rwar if tt == 0 else ())
        p.setw(rB, rl)
        dwar = p.begin_fill(dst_bf)
        dl = None
        if router is not None:
            rwf, psL8T = router
            lwar = p.begin_fill(psL8T)
        for c in range(NDC):
            p.emit("dve", lambda e, c=c: e.tensor_scalar(
                tmpc.ap[:, :], src_bf.ap[:, c * T:(c + 1) * T],
                g1col.ap[:, c:c + 1], None, ALU.mult),
                ins=[src_bf, g1col], outs=[tmpc])
            p.emit("dve", lambda e: e.tensor_mul(tmpc.ap[:, :], tmpc.ap[:, :], rB.ap[:, :]),
                   ins=[rB], outs=[tmpc])
            if router is not None:
                ltok = None
                for tt in range(NT):
                    ltok = p.emit("pe", lambda e, c=c, tt=tt: e.matmul(
                        psL8T.ap[:, tt * TD:(tt + 1) * TD], rwf.ap[:, c * 8:(c + 1) * 8],
                        tmpc.ap[:, tt * TD:(tt + 1) * TD],
                        start=(c == 0), stop=(c == NDC - 1)),
                        ins=[tmpc, rwf], waits=lwar if c == 0 and tt == 0 else ())
                if c == NDC - 1:
                    p.setw(psL8T, ltok)
            dl = p.emit("dve", lambda e, c=c: e.tensor_copy(
                dst_bf.ap[:, c * T:(c + 1) * T], tmpc.ap[:, :]),
                ins=[tmpc], waits=dwar if c == 0 else ())
        p.setw(dst_bf, dl)

    def phase_l0_router():
        stD = contextlib.ExitStack()
        rwf = sb("rwf", [128, 64], F32, stD)
        psL8T = psb("psL8T", [8, T], F32, stD)
        rwar = p.begin_fill(rwf)
        rl = None
        for c in range(NDC):
            rl = p.dma("sp", rwf.ap[:, c * 8:(c + 1) * 8],
                       dram["L0_ffn_router_w"][c * 128:(c + 1) * 128, :],
                       waits=rwar if c == 0 else (), key=rwf)
        p.setw(rwf, rl)
        rstack = contextlib.ExitStack()
        rmsnorm_into(st2_bf, dram["L0_ffn_g2"], z_bf, rstack, side="right",
                     router=(rwf, psL8T))
        p.flush("d0")
        rstack.close()
        rbrow = sb("rbrow", [1, 8], F32, stD)
        rbcol = sb("rbcol", [8, 1], F32, stD)
        R4 = sb("R4", [4, T], F32, stD)
        psTk = psb("psTk", [128, 8], F32, stD)
        psQT = psb("psQT", [4, 128], F32, stD)
        lgsb = sb("lgsb", [8, TD], F32, stD)
        lgt = sb("lgt", [128, 8], F32, stD)
        m1 = sb("m1c", [128, 1], F32, stD)
        nm1 = sb("nm1c", [128, 1], F32, stD)
        m2c = sb("m2c", [128, 1], F32, stD)
        e8 = sb("e8c", [128, 8], F32, stD)
        cand = sb("candc", [128, 8], F32, stD)
        lg2 = sb("lg2c", [128, 8], F32, stD)
        Qc = sb("Qc", [128, 4], F32, stD)
        dx = sb("dxc", [128, 1], F32, stD)
        p.dma("sp", rbrow, dram["L0_ffn_router_b"])
        p.emit("pe", lambda e: e.transpose(psTk.ap[0:8, 0:1], rbrow.ap[0:1, :],
                                           ident_f.ap[0:1, 0:1]),
               ins=[rbrow, ident_f], outs=[psTk])
        p.emit("dve", lambda e: e.tensor_copy(rbcol.ap[:, :], psTk.ap[0:8, 0:1]),
               ins=[psTk], outs=[rbcol])
        R4war = p.begin_fill(R4)
        R4tok = None
        for tt in range(NT):
            p.emit("dve", lambda e, tt=tt: e.tensor_scalar(
                lgsb.ap[:, :], psL8T.ap[:, tt * TD:(tt + 1) * TD],
                rbcol.ap[0:8, 0:1], None, ALU.add),
                ins=[psL8T, rbcol], outs=[lgsb])
            for s in range(4):
                cc = tt * 4 + s
                p.emit("pe", lambda e, s=s: e.transpose(
                    psTk.ap[:, :], lgsb.ap[:, s * 128:(s + 1) * 128],
                    ident_f.ap[0:8, 0:8]), ins=[lgsb, ident_f], outs=[psTk])
                p.emit("dve", lambda e: e.tensor_copy(lgt.ap[:, :], psTk.ap[:, :]),
                       ins=[psTk], outs=[lgt])
                p.emit("dve", lambda e: e.reduce_max(m1.ap[:, :], lgt.ap[:, :], axis=AX.X),
                       ins=[lgt], outs=[m1])
                p.emit("dve", lambda e: e.tensor_scalar(e8.ap[:, :], lgt.ap[:, :],
                                                        m1.ap[:, 0:1], None, ALU.is_equal),
                       ins=[lgt, m1], outs=[e8])
                p.emit("dve", lambda e: e.tensor_scalar(e8.ap[:, :], e8.ap[:, :],
                                                        -1000.0, 1000.0, ALU.mult, ALU.add),
                       ins=[e8], outs=[e8])
                p.emit("dve", lambda e: e.tensor_add(cand.ap[:, :], e8.ap[:, :],
                                                     iota8_row128.ap[:, :]),
                       ins=[e8, iota8_row128], outs=[cand])
                p.emit("dve", lambda e: e.tensor_reduce(Qc.ap[:, 0:1], cand.ap[:, :],
                                                        axis=AX.X, op=ALU.min),
                       ins=[cand], outs=[Qc])
                p.emit("dve", lambda e: e.tensor_scalar(e8.ap[:, :], iota8_row128.ap[:, :],
                                                        Qc.ap[:, 0:1], None, ALU.is_equal),
                       ins=[iota8_row128, Qc, e8], outs=[e8])
                p.emit("dve", lambda e: e.scalar_tensor_tensor(
                    lg2.ap[:, :], e8.ap[:, :], -1e9, lgt.ap[:, :], ALU.mult, ALU.add),
                    ins=[e8, lgt], outs=[lg2])
                p.emit("dve", lambda e: e.reduce_max(m2c.ap[:, :], lg2.ap[:, :], axis=AX.X),
                       ins=[lg2], outs=[m2c])
                p.emit("dve", lambda e: e.tensor_scalar(e8.ap[:, :], lg2.ap[:, :],
                                                        m2c.ap[:, 0:1], None, ALU.is_equal),
                       ins=[lg2, m2c, e8], outs=[e8])
                p.emit("dve", lambda e: e.tensor_scalar(e8.ap[:, :], e8.ap[:, :],
                                                        -1000.0, 1000.0, ALU.mult, ALU.add),
                       ins=[e8], outs=[e8])
                p.emit("dve", lambda e: e.tensor_add(cand.ap[:, :], e8.ap[:, :],
                                                     iota8_row128.ap[:, :]),
                       ins=[e8, iota8_row128], outs=[cand])
                p.emit("dve", lambda e: e.tensor_reduce(Qc.ap[:, 1:2], cand.ap[:, :],
                                                        axis=AX.X, op=ALU.min),
                       ins=[cand], outs=[Qc])
                p.emit("dve", lambda e: e.tensor_scalar(nm1.ap[:, :], m1.ap[:, :],
                                                        -1.0, None, ALU.mult),
                       ins=[m1], outs=[nm1])
                p.emit("act", lambda e: e.activation(dx.ap[:, :], m2c.ap[:, :], AF.Exp,
                                                     bias=nm1.ap[:, 0:1], scale=1.0),
                       ins=[m2c, nm1], outs=[dx])
                p.emit("dve", lambda e: e.tensor_scalar(Qc.ap[:, 2:3], dx.ap[:, :],
                                                        1.0, None, ALU.add),
                       ins=[dx, Qc], outs=[Qc])
                p.emit("dve", lambda e: e.reciprocal(Qc.ap[:, 2:3], Qc.ap[:, 2:3]),
                       ins=[Qc], outs=[Qc])
                p.emit("dve", lambda e: e.tensor_mul(Qc.ap[:, 3:4], dx.ap[:, 0:1],
                                                     Qc.ap[:, 2:3]),
                       ins=[dx, Qc], outs=[Qc])
                p.emit("pe", lambda e: e.transpose(psQT.ap[:, :], Qc.ap[:, :],
                                                   ident_f.ap[:, :]),
                       ins=[Qc, ident_f], outs=[psQT])
                R4tok = p.emit("dve", lambda e, cc=cc: e.tensor_copy(
                    R4.ap[:, cc * 128:(cc + 1) * 128], psQT.ap[:, :]),
                    ins=[psQT], waits=R4war if cc == 0 else ())
        p.setw(R4, R4tok)
        ohT = [sb(f"ohT{k}", [8, T], F32, stD) for k in range(2)]
        pos0 = sb("pos0", [8, T], F32, stD)
        p8 = sb("p8", [8, T], F32, stD)
        cnt2 = sb("cnt2", [8, 2], F32, stD)
        agsb = sb("agsb", [8, 16], F32, stD)
        off2 = sb("off2", [8, 2], F32, stD)
        psE = psb("psE", [8, TD], F32, stD)
        psR1 = psb("psR1", [1, TD], F32, stD)
        trow_ = [sb(f"trow{k}", [1, T], F32, stD) for k in range(2)]
        arow_t = sb("arowt", [1, T], F32, stD)
        sxr = sb("sxr", [1, TD], F32, stD)
        m8 = sb("m8", [8, T], F32, stD)
        for k in range(2):
            war = p.begin_fill(ohT[k])
            lt = None
            for tt in range(NT):
                p.emit("pe", lambda e, k=k, tt=tt: e.matmul(
                    psE.ap[:, :], consts["hc_sel8"].ap[0:4, k * 128:k * 128 + 8],
                    R4.ap[:, tt * TD:(tt + 1) * TD], start=True, stop=True),
                    ins=[R4, consts["hc_sel8"]], outs=[psE])
                lt = p.emit("dve", lambda e, k=k, tt=tt: e.tensor_scalar(
                    ohT[k].ap[:, tt * TD:(tt + 1) * TD], psE.ap[:, :],
                    iota8_col.ap[:, 0:1], None, ALU.is_equal),
                    ins=[psE, iota8_col], waits=war if tt == 0 else ())
            p.setw(ohT[k], lt)
            p.emit("dve", lambda e, k=k: e.reduce_sum(cnt2.ap[:, k:k + 1],
                                                      ohT[k].ap[:, :], axis=AX.X),
                   ins=[ohT[k]], outs=[cnt2] if k == 1 else [])
        cdma = p.dma("sp", dram["cnt0"], cnt2, ins=[cnt2])
        cctok = p.emit("pool", lambda e: e.collective_compute(
            "AllGather", ALU.bypass, replica_groups=[list(range(8))],
            ins=[dram["cnt0"].opt()], outs=[dram["ag0"].opt()]),
            waits=[cdma], collective=True)
        agld = p.dma("sp", agsb.ap.rearrange("e (c k) -> e c k", c=8),
                     dram["ag0"].rearrange("(c e) k -> e c k", c=8),
                     waits=[cctok], key=agsb)
        p.setw(agsb, agld)
        p.emit("dve", lambda e: e.tensor_mul(agsb.ap[:, :], agsb.ap[:, :],
                                             consts["hc_prev2"].ap[:, :]),
               ins=[consts["hc_prev2"], agsb], outs=[agsb])
        ot = p.emit("dve", lambda e: e.tensor_copy(off2.ap[:, :], agsb.ap[:, 0:2]),
                    ins=[agsb], outs=[off2])
        for c in range(1, 8):
            ot = p.emit("dve", lambda e, c=c: e.tensor_add(
                off2.ap[:, :], off2.ap[:, :], agsb.ap[:, 2 * c:2 * c + 2]),
                ins=[agsb], outs=[off2], waits=[ot])
        for k in range(2):
            pdst = pos0 if k == 0 else p8
            p.emit("dve", lambda e, k=k, pdst=pdst: e.tensor_tensor_scan(
                pdst.ap[:, :], ohT[k].ap[:, :], ohT[k].ap[:, :], 0.0,
                ALU.add, ALU.bypass), ins=[ohT[k]], outs=[pdst])
            p.emit("dve", lambda e, k=k, pdst=pdst: e.tensor_scalar(
                pdst.ap[:, :], pdst.ap[:, :], off2.ap[:, k:k + 1], None, ALU.add),
                ins=[off2, pdst], outs=[pdst])
        p.emit("dve", lambda e: e.tensor_add(p8.ap[:, :], p8.ap[:, :], pos0.ap[:, :]),
               ins=[pos0, p8], outs=[p8])
        for k in range(2):
            pdst = pos0 if k == 0 else p8
            p.emit("dve", lambda e, pdst=pdst: e.tensor_scalar(
                pdst.ap[:, :], pdst.ap[:, :], 2048.0, None, ALU.is_lt),
                ins=[pdst], outs=[pdst])
            p.emit("dve", lambda e, k=k, pdst=pdst: e.tensor_mul(
                ohT[k].ap[:, :], ohT[k].ap[:, :], pdst.ap[:, :]),
                ins=[pdst], outs=[ohT[k]])
            twar = p.begin_fill(trow_[k])
            ttok = None
            for tt in range(NT):
                p.emit("pe", lambda e, k=k, tt=tt: e.matmul(
                    psR1.ap[0:1, :], iota8_col.ap[:, 0:1],
                    ohT[k].ap[:, tt * TD:(tt + 1) * TD], start=True, stop=True),
                    ins=[ohT[k], iota8_col], outs=[psR1])
                ttok = p.emit("dve", lambda e, k=k, tt=tt: e.tensor_copy(
                    trow_[k].ap[0:1, tt * TD:(tt + 1) * TD], psR1.ap[0:1, :]),
                    ins=[psR1], waits=twar if tt == 0 else ())
            p.setw(trow_[k], ttok)
        p.emit("dve", lambda e: e.tensor_max(arow_t.ap[0:1, :], trow_[0].ap[0:1, :],
                                             trow_[1].ap[0:1, :]),
               ins=[trow_[0], trow_[1]], outs=[arow_t])
        for k in range(2):
            twar = p.begin_fill(trow_[k])
            ttok = None
            for tt in range(NT):
                p.emit("pe", lambda e, k=k, tt=tt: e.matmul(
                    psR1.ap[0:1, :], ones_col.ap[0:8, 0:1],
                    ohT[k].ap[:, tt * TD:(tt + 1) * TD], start=True, stop=True),
                    ins=[ohT[k], ones_col], outs=[psR1])
                p.emit("pe", lambda e, k=k, tt=tt: e.matmul(
                    psE.ap[0:1, :], consts["hc_sel8"].ap[0:4, (2 + k) * 128:(2 + k) * 128 + 1],
                    R4.ap[:, tt * TD:(tt + 1) * TD], start=True, stop=True),
                    ins=[R4, consts["hc_sel8"]], outs=[psE])
                p.emit("dve", lambda e: e.tensor_copy(sxr.ap[0:1, :], psE.ap[0:1, :]),
                       ins=[psE], outs=[sxr])
                ttok = p.emit("dve", lambda e, k=k, tt=tt: e.tensor_mul(
                    trow_[k].ap[0:1, tt * TD:(tt + 1) * TD], psR1.ap[0:1, :],
                    sxr.ap[0:1, :]),
                    ins=[psR1, sxr], waits=twar if tt == 0 else ())
            p.setw(trow_[k], ttok)
        p.emit("dve", lambda e: e.tensor_add(brow.ap[0:1, :], trow_[0].ap[0:1, :],
                                             trow_[1].ap[0:1, :]),
               ins=[trow_[0], trow_[1]], outs=[brow])
        war = p.begin_fill(m8)
        lt = None
        for tt in range(NT):
            bcast(arow_t, tt * TD, TD, 8, psE)
            lt = p.emit("dve", lambda e, tt=tt: e.tensor_copy(
                m8.ap[:, tt * TD:(tt + 1) * TD], psE.ap[:, 0:TD]),
                ins=[psE], waits=war if tt == 0 else ())
        p.setw(m8, lt)
        war = p.begin_fill(p8)
        lt = None
        for tt in range(NT):
            bcast(brow, tt * TD, TD, 8, psE)
            lt = p.emit("dve", lambda e, tt=tt: e.tensor_copy(
                p8.ap[:, tt * TD:(tt + 1) * TD], psE.ap[:, 0:TD]),
                ins=[psE], waits=war if tt == 0 else ())
        p.setw(p8, lt)
        p.emit("dve", lambda e: e.tensor_scalar(m8.ap[:, :], m8.ap[:, :],
                                                iota8_col.ap[:, 0:1], None, ALU.is_equal),
               ins=[iota8_col, m8], outs=[m8])
        p.emit("dve", lambda e: e.tensor_mul(arow.ap[:, :], m8.ap[:, :], p8.ap[:, :]),
               ins=[m8, p8], outs=[arow])
        p.emit("dve", lambda e: e.tensor_scalar(brow.ap[0:1, :], brow.ap[0:1, :],
                                                -1.0, 1.0, ALU.mult, ALU.add),
               ins=[], outs=[brow])
        p.flush("d1")
        stD.close()

    phase_l0_router()
    if upto <= 3:
        return nc, dram

    def phase_l0_experts():
        stackH1.close()
        stE = contextlib.ExitStack()
        st0 = contextlib.ExitStack()
        h1t = [sb(f"eh1t{i}", [128, T], F32, st0) for i in range(2)]
        zf = [sb(f"ezf{i}", [128, T], F32, st0) for i in range(2)]
        wzB = sb("ewzB", [128, T], F32, st0)
        itile = [sb(f"einit{i}", [128, T], F32, st0) for i in range(2)]
        psI = psb("epsI", [128, TD], F32, st0)
        init_toks = []
        wwar = p.begin_fill(wzB)
        wl = None
        for it in range(NT):
            p.emit("pe", lambda e, it=it: e.matmul(
                psI.ap[:, :], ones_row.ap[0:1, :],
                brow.ap[0:1, it * TD:(it + 1) * TD], start=True, stop=True),
                ins=[brow, ones_row], outs=[psI])
            wl = p.emit("dve", lambda e, it=it: e.tensor_copy(
                wzB.ap[:, it * TD:(it + 1) * TD], psI.ap[:, :]),
                ins=[psI], waits=wwar if it == 0 else ())
        p.setw(wzB, wl)
        for c in range(NDC):
            b = c % 2
            sl = slice(c * 128, (c + 1) * 128)
            p.dma("sp", h1t[b], dram["h1_f32"][sl, :])
            p.emit("dve", lambda e, c=c, b=b: e.tensor_copy(
                zf[b].ap[:, :], z_bf.ap[:, c * T:(c + 1) * T]),
                ins=[z_bf], outs=[zf[b]])
            p.emit("dve", lambda e, b=b: e.tensor_mul(zf[b].ap[:, :], zf[b].ap[:, :],
                                                      wzB.ap[:, :]),
                   ins=[wzB], outs=[zf[b]])
            p.emit("dve", lambda e, b=b: e.tensor_add(itile[b].ap[:, :], h1t[b].ap[:, :],
                                                      zf[b].ap[:, :]),
                   ins=[h1t[b], zf[b]], outs=[itile[b]])
            init_toks.append(p.dma("sp", dram["acc0"][sl, :], itile[b], ins=[itile[b]]))
        p.flush("e_init")
        st0.close()
        hbuf = sb("hbuf", [128, 16 * T], BF16, stE)
        hbufc = [Buf(hbuf.ap[:, hc * T:(hc + 1) * T]) for hc in range(16)]
        wc1 = [sb(f"wc1_{i}", [128, 8 * 128], BF16, stE) for i in range(2)]
        wc2 = [sb("wc2_0", [128, 8 * 128], BF16, stE)]
        wstg = [sb("wstg0", [128, 8 * 128], F32, stE)]
        aef = sb("aef", [128, TD], F32, stE)
        alB = sb("alB", [128, T], F32, stE)
        ctile = sb("ctile0", [128, 2 * TD], F32, stE)
        psA = [psb(f"epsA{i}", [128, TD], F32, stE) for i in range(2)]
        psB2 = [psb(f"epsB{i}", [128, TD], F32, stE) for i in range(2)]
        psY = [psb(f"epsY{i}", [128, TD], F32, stE) for i in range(2)]
        psAl = psb("epsAl", [128, TD], F32, stE)
        w3t = sb("w3t", [128, 16 * D], BF16, stE)
        prev_acc = init_toks[-1]
        for ex in range(8):
            w3war = p.begin_fill(w3t)
            w3l = p.dma("pool", w3t.ap.rearrange("p (c d) -> p c d", c=16),
                        dram["L0_ffn_w3"][ex].rearrange("(c p) d -> p c d", p=128),
                        waits=w3war, key=w3t)
            p.setw(w3t, w3l)
            awar = p.begin_fill(alB)
            al = None
            for it in range(NT):
                p.emit("pe", lambda e, ex=ex, it=it: e.matmul(
                    psAl.ap[:, :], consts["hc_sel8"].ap[:, ex * 128:(ex + 1) * 128],
                    arow.ap[:, it * TD:(it + 1) * TD], start=True, stop=True),
                    ins=[arow, consts["hc_sel8"]], outs=[psAl])
                al = p.emit("dve", lambda e, it=it: e.tensor_copy(
                    alB.ap[:, it * TD:(it + 1) * TD], psAl.ap[:, :]),
                    ins=[psAl], waits=awar if it == 0 else ())
            p.setw(alB, al)
            for hc in range(16):
                hcwar = p.begin_fill(hbufc[hc])
                b = hc % 2
                for wbuf, wname in ((wc1[b], "L0_ffn_w1"), (wc2[0], "L0_ffn_w2")):
                    hw_cast_load(wbuf, dram[wname][ex][:, hc * 128:(hc + 1) * 128],
                                 8, 128, wstg[0])
                for it in range(NT):
                    pb = (hc * NT + it) % 2
                    for psx, wbuf in ((psA[pb], wc1[b]), (psB2[pb], wc2[0])):
                        war = p.begin_fill(psx)
                        ptok = None
                        for c in range(NDC):
                            ptok = p.emit("pe", lambda e, c=c, it=it, psx=psx, wbuf=wbuf:
                                          e.matmul(
                                psx.ap[:, :], wbuf.ap[:, c * 128:(c + 1) * 128],
                                z_bf.ap[:, c * T + it * TD:c * T + (it + 1) * TD],
                                start=(c == 0), stop=(c == NDC - 1)),
                                ins=[wbuf, z_bf], waits=war if c == 0 else ())
                        p.setw(psx, ptok)
                    p.emit("dve", lambda e, pb=pb: e.tensor_copy(aef.ap[:, :], psA[pb].ap[:, :]),
                           ins=[psA[pb]], outs=[aef])
                    p.emit("dve", lambda e, pb=pb: e.tensor_mul(aef.ap[:, :], psB2[pb].ap[:, :],
                                                                aef.ap[:, :]),
                           ins=[psB2[pb]], outs=[aef])
                    hl = p.emit("act", lambda e, hc=hc, it=it: e.activation(
                        hbuf.ap[:, hc * T + it * TD:hc * T + (it + 1) * TD],
                        aef.ap[:, :], AF.Gelu),
                        ins=[aef], waits=hcwar if it == 0 else ())
                p.setw(hbufc[hc], hl)
            new_prev = prev_acc
            for dc in range(NDC):
                for half in range(2):
                    ct = Buf(ctile.ap[:, 0:2 * TD]) if False else ctile
                    cwar = p.begin_fill(ctile)
                    clast = None
                    for t2 in range(2):
                        it = half * 2 + t2
                        pb = (dc * NT + it) % 2
                        war = p.begin_fill(psY[pb])
                        ptok = None
                        for hc in range(16):
                            ptok = p.emit("pe", lambda e, hc=hc, dc=dc, it=it, pb=pb: e.matmul(
                                psY[pb].ap[:, :],
                                w3t.ap[:, hc * D + dc * 128:hc * D + dc * 128 + 128],
                                hbuf.ap[:, hc * T + it * TD:hc * T + (it + 1) * TD],
                                start=(hc == 0), stop=(hc == 15)),
                                ins=[w3t, hbufc[hc]], waits=war if hc == 0 else ())
                        p.setw(psY[pb], ptok)
                        clast = p.emit("dve", lambda e, it=it, t2=t2, pb=pb: e.tensor_mul(
                            ctile.ap[:, t2 * TD:(t2 + 1) * TD], psY[pb].ap[:, :],
                            alB.ap[:, it * TD:(it + 1) * TD]),
                            ins=[psY[pb], alB], waits=cwar if t2 == 0 else ())
                    p.setw(ctile, clast)
                    tok = p.dma("pool",
                                dram["acc0"][dc * 128:(dc + 1) * 128,
                                             half * 2 * TD:(half + 1) * 2 * TD],
                                ctile, ins=[ctile],
                                waits=[prev_acc] if dc == 0 and half == 0 else (),
                                accum_op=ALU.add)
                    new_prev = tok
            prev_acc = new_prev
        p.flush("e_experts")
        stE.close()

    phase_l0_experts()
    if upto <= 4:
        return nc, dram

    def phase_combine(x_dram, s_dram, acc_dram, out_dram, phname):
        stF = contextlib.ExitStack()
        if dbg and phname == "f0":
            p.dma("pool", dram["acc0b"], acc_dram)
        at = [sb(f"fat{i}", [128, T], F32, stF) for i in range(2)]
        gt = [sb(f"fgt{i}", [128, T], F32, stF) for i in range(2)]
        st_ = [sb(f"fst{i}", [128, T], F32, stF) for i in range(2)]
        xt = [sb(f"fxt{i}", [128, T], F32, stF) for i in range(2)]
        ot = [sb(f"fot{i}", [128, T], F32, stF) for i in range(2)]
        for c in range(NDC):
            b = c % 2
            sl = slice(c * 128, (c + 1) * 128)
            p.dma("sp", at[b], acc_dram[sl, :])
            p.dma("sp", st_[b], s_dram[sl, :])
            p.dma("act", xt[b], x_dram[sl, :])
            p.emit("act", lambda e, b=b: e.activation(gt[b].ap[:, :], at[b].ap[:, :],
                                                      AF.Silu),
                   ins=[at[b]], outs=[gt[b]])
            p.emit("dve", lambda e, b=b: e.tensor_mul(gt[b].ap[:, :], gt[b].ap[:, :],
                                                      st_[b].ap[:, :]),
                   ins=[st_[b]], outs=[gt[b]])
            p.emit("dve", lambda e, b=b: e.tensor_add(ot[b].ap[:, :], gt[b].ap[:, :],
                                                      xt[b].ap[:, :]),
                   ins=[gt[b], xt[b]], outs=[ot[b]])
            p.dma("sp", out_dram[sl, :], ot[b], ins=[ot[b]])
        p.flush(phname)
        stF.close()

    phase_combine(dram["x"], dram["s0"], dram["acc0"], dram["x1"], "f0")
    if upto <= 5:
        return nc, dram

    stackL0.close()
    phase_ln(dram["x1"], dram["xn1_f32"], xn_bf, 1e-5, "ln1")
    if upto <= 6:
        return nc, dram

    def phase_l1_attn():
        stackI = contextlib.ExitStack()
        stQKV = contextlib.ExitStack()
        qT = sb("qT3", [128, NDC * T], BF16, stQKV, side="right")
        kT = sb("kT3", [128, NDC * T], BF16, stQKV, side="right")
        vT = sb("vT3", [128, NDC * T], BF16, stQKV, side="right")
        st1 = contextlib.ExitStack()
        wq = load_w_bf("wqkv1", dram["L1_attn_w_qkv"], D, 3 * D, st1)
        mmps = [psb(f"i1ps{i}", [128, TD], F32, st1) for i in range(2)]
        for fi, dst in ((0, qT), (1, kT), (2, vT)):
            war = p.begin_fill(dst)
            lastw = None
            for fc in range(NDC):
                for tt in range(NT):
                    pp = mmps[(fc * NT + tt) % 2]
                    ptok = None
                    for c in range(NDC):
                        ptok = p.emit("pe", lambda e, fi=fi, fc=fc, tt=tt, c=c, pp=pp: e.matmul(
                            pp.ap[:, :],
                            wq.ap[:, c * 3 * D + fi * D + fc * 128:
                                  c * 3 * D + fi * D + fc * 128 + 128],
                            xn_bf.ap[:, c * T + tt * TD:c * T + (tt + 1) * TD],
                            start=(c == 0), stop=(c == NDC - 1)),
                            ins=[wq, xn_bf], outs=[pp] if c == 0 else ())
                    p.setw(pp, ptok)
                    lastw = p.emit("dve", lambda e, fc=fc, tt=tt, dst=dst, pp=pp: e.tensor_copy(
                        dst.ap[:, fc * T + tt * TD:fc * T + (tt + 1) * TD], pp.ap[:, :]),
                        ins=[pp], waits=war if fc == 0 and tt == 0 else ())
            p.setw(dst, lastw)
        p.flush("i1")
        st1.close()
        st2 = contextlib.ExitStack()
        a1T = sb("a1T", [128, NDC * T], BF16, stackI)
        psS = psb("ipsS", [128, 2 * TD], F32, st2)
        psPV = psb("ipsPV", [65, TD], F32, st2)
        psBq = psb("ipsB", [128, TD], F32, st2)
        psTt = psb("ipsT", [128, 64], BF16, st2)
        psVm = psb("ipsVm", [65, 1], F32, st2)
        Pt = sb("iPt", [128, 2 * TD], BF16, st2)
        msk = sb("imsk", [128, TD], BF16, st2)
        ShI = [Buf(psS.ap[:, 0:TD]), Buf(psS.ap[:, TD:2 * TD])]
        PhI = [Buf(Pt.ap[:, 0:TD]), Buf(Pt.ap[:, TD:2 * TD])]
        v_tok = sb("iv_tok", [128, 16 * 65], BF16, st2)
        sqs_ = sb("isqs", [128, TD], F32, st2)
        s2 = sb("is2", [2, TD], F32, st2)
        m2 = sb("im2", [2, 1], F32, st2)
        qmax2 = sb("iqmax", [2, 1], F32, st2)
        kmax2 = sb("ikmax", [2, 1], F32, st2)
        negMb2 = sb("inegM", [128, 2], F32, st2)
        tau2 = sb("itau", [128, 2], F32, st2)
        thrc = sb("ithr", [128, 1], F32, st2)
        thr1 = sb("ithr1", [1, 1], F32, st2)
        nrow2 = sb("inrow", [1, 2], F32, st2)
        drow = sb("idrow", [1, TD], F32, st2)
        indr = sb("iindr", [1, TD], F32, st2)
        rbg = sb("irbg", [64, TD], F32, st2)
        ibg = sb("iibg", [64, TD], F32, st2)
        vmc = sb("ivmc", [64, 1], F32, st2)

        p.dma("sp", thr1, dram["L1_attn_threshold"])
        p.emit("pe", lambda e: e.matmul(psBq.ap[:, 0:1], ones_row.ap[0:1, :],
                                        thr1.ap[0:1, 0:1], start=True, stop=True),
               ins=[thr1, ones_row], outs=[psBq])
        p.emit("dve", lambda e: e.tensor_copy(thrc.ap[:, :], psBq.ap[:, 0:1]),
               ins=[psBq], outs=[thrc])

        def head_norms3(srcb, off, dst):
            war = p.begin_fill(dst)
            lt = None
            for tt in range(NT):
                p.emit("act", lambda e, tt=tt, srcb=srcb, off=off: e.activation(
                    sqs_.ap[:, :], srcb.ap[:, off + tt * TD:off + (tt + 1) * TD],
                    AF.Square), ins=[srcb], outs=[sqs_])
                p.emit("pe", lambda e: e.matmul(
                    psBq.ap[0:2, :], consts["hc_bo2"].ap[:, :], sqs_.ap[:, :],
                    start=True, stop=True), ins=[sqs_, consts["hc_bo2"]], outs=[psBq])
                if tt == 0:
                    lt = p.emit("dve", lambda e, dst=dst: e.reduce_max(
                        dst.ap[:, 0:1], psBq.ap[0:2, :], axis=AX.X), ins=[psBq], waits=war)
                else:
                    p.emit("dve", lambda e: e.reduce_max(
                        m2.ap[:, 0:1], psBq.ap[0:2, :], axis=AX.X), ins=[psBq], outs=[m2])
                    lt = p.emit("dve", lambda e, dst=dst: e.tensor_max(
                        dst.ap[:, 0:1], dst.ap[:, 0:1], m2.ap[:, 0:1]),
                        ins=[m2], waits=[lt])
            p.setw(dst, lt)

        for qc in range(NDC):
            head_norms3(qT, qc * T, qmax2)
            head_norms3(kT, qc * T, kmax2)
            p.emit("dve", lambda e: e.tensor_mul(qmax2.ap[:, :], qmax2.ap[:, :],
                                                 kmax2.ap[:, :]),
                   ins=[kmax2, qmax2], outs=[qmax2])
            p.emit("act", lambda e: e.activation(qmax2.ap[:, :], qmax2.ap[:, :], AF.Sqrt),
                   ins=[qmax2], outs=[qmax2])
            p.emit("dve", lambda e: e.tensor_scalar(qmax2.ap[:, :], qmax2.ap[:, :],
                                                    -0.125, None, ALU.mult),
                   ins=[qmax2], outs=[qmax2])
            p.emit("pe", lambda e: e.transpose(psBq.ap[0:1, 0:2], qmax2.ap[:, 0:1],
                                               ident_f.ap[0:2, 0:2]),
                   ins=[qmax2, ident_f], outs=[psBq])
            p.emit("dve", lambda e: e.tensor_copy(nrow2.ap[0:1, :], psBq.ap[0:1, 0:2]),
                   ins=[psBq], outs=[nrow2])
            bcast(nrow2, 0, 2, 128, psBq)
            p.emit("dve", lambda e: e.tensor_copy(negMb2.ap[:, :], psBq.ap[:, 0:2]),
                   ins=[psBq], outs=[negMb2])
            p.emit("act", lambda e: e.activation(tau2.ap[:, :], negMb2.ap[:, :], AF.Exp,
                                                 bias=thrc.ap[:, 0:1], scale=1.0),
                   ins=[negMb2, thrc], outs=[tau2])
            for h2 in range(2):
                hb = 64 * h2
                head = qc * 2 + h2
                vwar = p.begin_fill(v_tok)
                p.emit("pool", lambda e: e.memset(v_tok.ap[:, :], 1.0), waits=vwar)
                vlast = None
                for cc in range(16):
                    p.emit("pe", lambda e, cc=cc, hb=hb, qc=qc: e.transpose(
                        psTt.ap[:, :], vT.ap[hb:hb + 64, qc * T + cc * 128:qc * T + (cc + 1) * 128],
                        ident_bf.ap[hb:hb + 64, hb:hb + 64]),
                        ins=[vT, ident_bf], outs=[psTt])
                    vlast = p.emit("dve", lambda e, cc=cc: e.tensor_copy(
                        v_tok.ap[:, cc * 65:cc * 65 + 64], psTt.ap[:, :]), ins=[psTt])
                p.setw(v_tok, vlast)
                vmwar = p.begin_fill(psVm)
                vmtok = None
                for cc in range(16):
                    vmtok = p.emit("pe", lambda e, cc=cc: e.matmul(
                        psVm.ap[:, :], v_tok.ap[:, cc * 65:(cc + 1) * 65],
                        ones_col_bf.ap[:, 0:1], start=(cc == 0), stop=(cc == 15)),
                        ins=[v_tok, ones_col_bf], waits=vmwar if cc == 0 else ())
                p.setw(psVm, vmtok)
                p.emit("dve", lambda e: e.tensor_scalar(vmc.ap[:, 0:1], psVm.ap[0:64, 0:1],
                                                        1.0 / T, None, ALU.mult),
                       ins=[psVm], outs=[vmc])
                for it in range(NT):
                    pvwar = p.begin_fill(psPV)
                    pvtok = None

                    def i2_scores(jc, hb=hb, qc=qc, h2=h2, it=it):
                        jj = jc % 2
                        p.emit("pe", lambda e, jc=jc, jj=jj: e.matmul(
                            ShI[jj].ap,
                            kT.ap[hb:hb + 64, qc * T + jc * 128:qc * T + (jc + 1) * 128],
                            qT.ap[hb:hb + 64, qc * T + it * TD:qc * T + (it + 1) * TD],
                            start=True, stop=True, tile_position=(hb, 0)),
                            ins=[kT, qT], outs=[ShI[jj]])
                        p.emit("act", lambda e, jj=jj: e.activation(
                            PhI[jj].ap, ShI[jj].ap, AF.Exp,
                            bias=negMb2.ap[:, h2:h2 + 1], scale=0.125),
                            ins=[ShI[jj], negMb2], outs=[PhI[jj]])
                        p.emit("pool", lambda e, jj=jj: e.tensor_scalar(
                            msk.ap[:, :], PhI[jj].ap, tau2.ap[:, h2:h2 + 1],
                            None, ALU.is_ge), ins=[PhI[jj], tau2], outs=[msk])
                        p.emit("pool", lambda e, jj=jj: e.tensor_mul(
                            PhI[jj].ap, PhI[jj].ap, msk.ap[:, :]),
                            ins=[msk], outs=[PhI[jj]])

                    def i2_pv(jc):
                        nonlocal pvtok
                        jj = jc % 2
                        pvtok = p.emit("pe", lambda e, jc=jc, jj=jj: e.matmul(
                            psPV.ap[:, :], v_tok.ap[:, jc * 65:(jc + 1) * 65],
                            PhI[jj].ap, start=(jc == 0), stop=(jc == 15)),
                            ins=[v_tok, PhI[jj]],
                            waits=pvwar if jc == 0 else ())

                    for jc in range(16):
                        if jc >= 2:
                            i2_pv(jc - 2)
                        i2_scores(jc)
                    i2_pv(14)
                    i2_pv(15)
                    p.setw(psPV, pvtok)
                    p.emit("dve", lambda e: e.tensor_copy(drow.ap[0:1, :], psPV.ap[64:65, :]),
                           ins=[psPV], outs=[drow])
                    p.emit("dve", lambda e: e.tensor_scalar(indr.ap[0:1, :], drow.ap[0:1, :],
                                                            0.0, None, ALU.is_equal),
                           ins=[drow], outs=[indr])
                    p.emit("dve", lambda e: e.tensor_add(drow.ap[0:1, :], drow.ap[0:1, :],
                                                         indr.ap[0:1, :]),
                           ins=[indr], outs=[drow])
                    p.emit("dve", lambda e: e.reciprocal(drow.ap[0:1, :], drow.ap[0:1, :]),
                           ins=[drow], outs=[drow])
                    bcast(drow, 0, TD, 64, psBq)
                    p.emit("dve", lambda e: e.tensor_copy(rbg.ap[:, :], psBq.ap[0:64, 0:TD]),
                           ins=[psBq], outs=[rbg])
                    bcast(indr, 0, TD, 64, psBq)
                    p.emit("dve", lambda e: e.tensor_copy(ibg.ap[:, :], psBq.ap[0:64, 0:TD]),
                           ins=[psBq], outs=[ibg])
                    p.emit("dve", lambda e: e.tensor_mul(rbg.ap[:, :], psPV.ap[0:64, :],
                                                         rbg.ap[:, :]),
                           ins=[psPV], outs=[rbg])
                    p.emit("dve", lambda e: e.tensor_scalar(ibg.ap[:, :], ibg.ap[:, :],
                                                            vmc.ap[0:64, 0:1], None, ALU.mult),
                           ins=[vmc], outs=[ibg])
                    p.emit("dve", lambda e, hb=hb, qc=qc, it=it: e.tensor_add(
                        a1T.ap[hb:hb + 64, qc * T + it * TD:qc * T + (it + 1) * TD],
                        rbg.ap[:, :], ibg.ap[:, :]),
                        ins=[rbg, ibg], outs=[a1T])
        p.flush("i2")
        st2.close()
        stQKV.close()
        st3 = contextlib.ExitStack()
        wo1 = load_w_bf("wo1", dram["L1_attn_w_out"], D, D, st3)
        bocol = sb("ibo", [128, 8], F32, st3)
        wps = [psb(f"i3ps{i}", [128, TD], F32, st3) for i in range(2)]
        stile = sb("istile", [128, TD], F32, st3)
        load_col_vec(bocol, dram["L1_attn_b_out"], 8, wps[0])
        for ec in range(NDC):
            for it in range(NT):
                pp = wps[(ec * NT + it) % 2]
                ptok = None
                for cl in range(NDC):
                    ptok = p.emit("pe", lambda e, ec=ec, it=it, cl=cl, pp=pp: e.matmul(
                        pp.ap[:, :],
                        wo1.ap[:, cl * D + ec * 128:cl * D + ec * 128 + 128],
                        a1T.ap[:, cl * T + it * TD:cl * T + (it + 1) * TD],
                        start=(cl == 0), stop=(cl == NDC - 1)),
                        ins=[wo1, a1T], outs=[pp] if cl == 0 else ())
                p.setw(pp, ptok)
                p.emit("dve", lambda e, ec=ec, pp=pp: e.tensor_scalar(
                    stile.ap[:, :], pp.ap[:, :], bocol.ap[:, ec:ec + 1], None, ALU.add),
                    ins=[pp, bocol], outs=[stile])
                p.dma("sp", dram["s1"][ec * 128:(ec + 1) * 128, it * TD:(it + 1) * TD],
                      stile, ins=[stile])
        p.flush("i3")
        st3.close()
        stackI.close()

    phase_l1_attn()
    if upto <= 7:
        return nc, dram


    stackL1 = contextlib.ExitStack()
    ao_bf = sb("ao_bf", [128, NDC * T], BF16, stackL1)
    stackLG = contextlib.ExitStack()
    lgT8 = sb("lgT8", [8, T], F32, stackLG)

    def phase_ln2():
        stH = contextlib.ExitStack()
        xt = sb("hxt", [128, T], F32, stH)
        sq = sb("hsq", [128, T], F32, stH)
        stat = psb("hstat", [1, 2 * TD], F32, stH)
        psG = psb("hpsG", [8, T], F32, stH)
        mrow = sb("hm", [1, T], F32, stH)
        srow = sb("hs", [1, T], F32, stH)
        trow = sb("ht", [1, T], F32, stH)
        bps = psb("hb", [128, TD], F32, stH)
        mB = sb("hmB", [128, T], F32, stH)
        rB = sb("hrB", [128, T], F32, stH)
        tmp = sb("htmp", [128, T], F32, stH)
        nwcol = sb("hnw", [128, 8], F32, stH)
        nbcol = sb("hnb", [128, 8], F32, stH)
        wgf = sb("hwgf", [128, 64], F32, stH)
        for dst, wname in ((nwcol, "L1_ffn_attn_nw"), (nbcol, "L1_ffn_attn_nb")):
            load_col_vec(dst, dram[wname], 8, bps)
        war = p.begin_fill(wgf)
        gl = None
        for c in range(NDC):
            gl = p.dma("sp", wgf.ap[:, c * 8:(c + 1) * 8],
                       dram["L1_ffn_wg"][c * 128:(c + 1) * 128, :],
                       waits=war if c == 0 else (), key=wgf)
        p.setw(wgf, gl)
        for sweep, (dstrow, use_sq) in enumerate(((mrow, False), (srow, True))):
            for half in range(2):
                war = p.begin_fill(stat)
                tok = None
                for c in range(NDC):
                    p.dma("sp", Buf(xt.ap[:, 0:2 * TD]),
                          dram["xn1_f32"][c * 128:(c + 1) * 128,
                                          half * 2 * TD:(half + 1) * 2 * TD],
                          outs=[xt])
                    if use_sq:
                        p.emit("act", lambda e: e.activation(sq.ap[:, 0:2 * TD],
                                                             xt.ap[:, 0:2 * TD], AF.Square),
                               ins=[xt], outs=[sq])
                        rbuf = sq
                    else:
                        rbuf = xt
                    for t2 in range(2):
                        tok = p.emit("pe", lambda e, t2=t2, c=c, rb=rbuf: e.matmul(
                            stat.ap[0:1, t2 * TD:(t2 + 1) * TD], ones_col.ap[:, 0:1],
                            rb.ap[:, t2 * TD:(t2 + 1) * TD],
                            start=(c == 0), stop=(c == NDC - 1)),
                            ins=[rbuf, ones_col], waits=war if c == 0 and t2 == 0 else ())
                p.setw(stat, tok)
                p.emit("dve", lambda e, dr=dstrow, half=half: e.tensor_scalar(
                    dr.ap[0:1, half * 2 * TD:(half + 1) * 2 * TD], stat.ap[0:1, :],
                    1.0 / D, None, ALU.mult),
                    ins=[stat], outs=[dstrow] if half == 1 else [])
        p.emit("dve", lambda e: e.tensor_mul(trow.ap[0:1, :], mrow.ap[0:1, :], mrow.ap[0:1, :]),
               ins=[mrow], outs=[trow])
        p.emit("dve", lambda e: e.tensor_sub(srow.ap[0:1, :], srow.ap[0:1, :], trow.ap[0:1, :]),
               ins=[trow, srow], outs=[srow])
        p.emit("dve", lambda e: e.tensor_scalar(srow.ap[0:1, :], srow.ap[0:1, :],
                                                1e-12, None, ALU.add), ins=[srow], outs=[srow])
        p.emit("act", lambda e: e.activation(srow.ap[0:1, :], srow.ap[0:1, :], AF.Sqrt),
               ins=[srow], outs=[srow])
        p.emit("dve", lambda e: e.reciprocal(srow.ap[0:1, :], srow.ap[0:1, :]),
               ins=[srow], outs=[srow])
        for dst, row in ((mB, mrow), (rB, srow)):
            war = p.begin_fill(dst)
            lastc = None
            for tt in range(NT):
                bcast(row, tt * TD, TD, 128, bps)
                lastc = p.emit("dve", lambda e, tt=tt, dst=dst: e.tensor_copy(
                    dst.ap[:, tt * TD:(tt + 1) * TD], bps.ap[:, 0:TD]),
                    ins=[bps], waits=war if tt == 0 else ())
            p.setw(dst, lastc)
        bfwar = p.begin_fill(ao_bf)
        lwar = p.begin_fill(psG)
        lastbf = None
        ltok = None
        for c in range(NDC):
            p.dma("sp", xt, dram["xn1_f32"][c * 128:(c + 1) * 128, :])
            p.emit("dve", lambda e: e.tensor_sub(tmp.ap[:, :], xt.ap[:, :], mB.ap[:, :]),
                   ins=[xt, mB], outs=[tmp])
            p.emit("dve", lambda e: e.tensor_mul(tmp.ap[:, :], tmp.ap[:, :], rB.ap[:, :]),
                   ins=[rB], outs=[tmp])
            p.emit("dve", lambda e, c=c: e.tensor_scalar(
                tmp.ap[:, :], tmp.ap[:, :], nwcol.ap[:, c:c + 1], nbcol.ap[:, c:c + 1],
                ALU.mult, ALU.add), ins=[nwcol, nbcol], outs=[tmp])
            for tt in range(NT):
                ltok = p.emit("pe", lambda e, c=c, tt=tt: e.matmul(
                    psG.ap[:, tt * TD:(tt + 1) * TD], wgf.ap[:, c * 8:(c + 1) * 8],
                    tmp.ap[:, tt * TD:(tt + 1) * TD],
                    start=(c == 0), stop=(c == NDC - 1)),
                    ins=[tmp, wgf], waits=lwar if c == 0 and tt == 0 else ())
            lastbf = p.emit("dve", lambda e, c=c: e.tensor_copy(
                ao_bf.ap[:, c * T:(c + 1) * T], tmp.ap[:, :]),
                ins=[tmp], waits=bfwar if c == 0 else ())
        p.setw(psG, ltok)
        p.setw(ao_bf, lastbf)
        p.emit("dve", lambda e: e.tensor_copy(lgT8.ap[:, :], psG.ap[:, :]),
               ins=[psG], outs=[lgT8])
        p.flush("h_ln2")
        stH.close()

    phase_ln2()
    if upto <= 8:
        return nc, dram

    def phase_l1_gate():
        stJ = contextlib.ExitStack()
        R2 = sb("R2", [2, T], F32, stJ)
        psTk = psb("jpsTk", [128, 8], F32, stJ)
        psQT = psb("jpsQT", [2, 128], F32, stJ)
        psE = psb("jpsE", [8, TD], F32, stJ)
        psR1 = psb("jpsR1", [1, TD], F32, stJ)
        psC = psb("jpsC", [1, TD], F32, stJ)
        lgsb = sb("jlg", [8, TD], F32, stJ)
        lgt = sb("jlgt", [128, 8], F32, stJ)
        m1 = sb("jm1", [128, 1], F32, stJ)
        nm1 = sb("jnm1", [128, 1], F32, stJ)
        e8 = sb("je8", [128, 8], F32, stJ)
        cand = sb("jcand", [128, 8], F32, stJ)
        esum = sb("jesum", [128, 1], F32, stJ)
        Qc = sb("jQc", [128, 2], F32, stJ)
        ohT = sb("johT", [8, T], F32, stJ)
        pos = sb("jpos", [8, T], F32, stJ)
        cnt1s = sb("jcnt", [8, 1], F32, stJ)
        agsb = sb("jag", [8, 8], F32, stJ)
        off1 = sb("joff", [8, 1], F32, stJ)
        trow = sb("jtrow", [1, T], F32, stJ)
        grow = sb("jgrow", [1, T], F32, stJ)
        sxr = sb("jsxr", [1, TD], F32, stJ)
        w8 = sb("jw8", [8, T], F32, stJ)
        c0r = sb("jc0", [1, T], F32, stJ)
        c1r = sb("jc1", [1, T], F32, stJ)
        mxr = sb("jmx", [1, T], F32, stJ)
        R2war = p.begin_fill(R2)
        R2tok = None
        for tt in range(NT):
            p.emit("dve", lambda e, tt=tt: e.tensor_copy(
                lgsb.ap[:, :], lgT8.ap[:, tt * TD:(tt + 1) * TD]),
                ins=[lgT8], outs=[lgsb])
            for s in range(4):
                cc = tt * 4 + s
                p.emit("pe", lambda e, s=s: e.transpose(
                    psTk.ap[:, :], lgsb.ap[:, s * 128:(s + 1) * 128],
                    ident_f.ap[0:8, 0:8]), ins=[lgsb, ident_f], outs=[psTk])
                p.emit("dve", lambda e: e.tensor_copy(lgt.ap[:, :], psTk.ap[:, :]),
                       ins=[psTk], outs=[lgt])
                p.emit("dve", lambda e: e.reduce_max(m1.ap[:, :], lgt.ap[:, :], axis=AX.X),
                       ins=[lgt], outs=[m1])
                p.emit("dve", lambda e: e.tensor_scalar(e8.ap[:, :], lgt.ap[:, :],
                                                        m1.ap[:, 0:1], None, ALU.is_equal),
                       ins=[lgt, m1], outs=[e8])
                p.emit("dve", lambda e: e.tensor_scalar(e8.ap[:, :], e8.ap[:, :],
                                                        -1000.0, 1000.0, ALU.mult, ALU.add),
                       ins=[e8], outs=[e8])
                p.emit("dve", lambda e: e.tensor_add(cand.ap[:, :], e8.ap[:, :],
                                                     iota8_row128.ap[:, :]),
                       ins=[e8, iota8_row128], outs=[cand])
                p.emit("dve", lambda e: e.tensor_reduce(Qc.ap[:, 0:1], cand.ap[:, :],
                                                        axis=AX.X, op=ALU.min),
                       ins=[cand], outs=[Qc])
                p.emit("dve", lambda e: e.tensor_scalar(nm1.ap[:, :], m1.ap[:, :],
                                                        -1.0, None, ALU.mult),
                       ins=[m1], outs=[nm1])
                p.emit("act", lambda e: e.activation(e8.ap[:, :], lgt.ap[:, :], AF.Exp,
                                                     bias=nm1.ap[:, 0:1], scale=1.0,
                                                     accum_out=esum.ap[:, 0:1]),
                       ins=[lgt, nm1], outs=[e8, esum])
                p.emit("dve", lambda e: e.reciprocal(Qc.ap[:, 1:2], esum.ap[:, 0:1]),
                       ins=[esum], outs=[Qc])
                p.emit("pe", lambda e: e.transpose(psQT.ap[:, :], Qc.ap[:, :],
                                                   ident_f.ap[:, :]),
                       ins=[Qc, ident_f], outs=[psQT])
                R2tok = p.emit("dve", lambda e, cc=cc: e.tensor_copy(
                    R2.ap[:, cc * 128:(cc + 1) * 128], psQT.ap[:, :]),
                    ins=[psQT], waits=R2war if cc == 0 else ())
        p.setw(R2, R2tok)
        war = p.begin_fill(ohT)
        lt = None
        for tt in range(NT):
            p.emit("pe", lambda e, tt=tt: e.matmul(
                psE.ap[:, :], consts["hc_sel8"].ap[0:2, 0:8],
                R2.ap[:, tt * TD:(tt + 1) * TD], start=True, stop=True),
                ins=[R2, consts["hc_sel8"]], outs=[psE])
            lt = p.emit("dve", lambda e, tt=tt: e.tensor_scalar(
                ohT.ap[:, tt * TD:(tt + 1) * TD], psE.ap[:, :],
                iota8_col.ap[:, 0:1], None, ALU.is_equal),
                ins=[psE, iota8_col], waits=war if tt == 0 else ())
        p.setw(ohT, lt)
        p.emit("dve", lambda e: e.reduce_sum(cnt1s.ap[:, 0:1], ohT.ap[:, :], axis=AX.X),
               ins=[ohT], outs=[cnt1s])
        cdma = p.dma("sp", dram["cnt1"], cnt1s, ins=[cnt1s])
        cctok = p.emit("pool", lambda e: e.collective_compute(
            "AllGather", ALU.bypass, replica_groups=[list(range(8))],
            ins=[dram["cnt1"].opt()], outs=[dram["ag1"].opt()]),
            waits=[cdma], collective=True)
        agld = p.dma("sp", agsb.ap.rearrange("e (c k) -> e c k", c=8),
                     dram["ag1"].rearrange("(c e) k -> e c k", c=8),
                     waits=[cctok], key=agsb, allow_slow_non_contiguous=True)
        p.setw(agsb, agld)
        p.emit("dve", lambda e: e.tensor_mul(agsb.ap[:, :], agsb.ap[:, :],
                                             consts["hc_prev1"].ap[:, :]),
               ins=[consts["hc_prev1"], agsb], outs=[agsb])
        ot = p.emit("dve", lambda e: e.tensor_copy(off1.ap[:, :], agsb.ap[:, 0:1]),
                    ins=[agsb], outs=[off1])
        for c in range(1, 8):
            ot = p.emit("dve", lambda e, c=c: e.tensor_add(
                off1.ap[:, :], off1.ap[:, :], agsb.ap[:, c:c + 1]),
                ins=[agsb], outs=[off1], waits=[ot])
        p.emit("dve", lambda e: e.tensor_tensor_scan(
            pos.ap[:, :], ohT.ap[:, :], ohT.ap[:, :], 0.0, ALU.add, ALU.bypass),
            ins=[ohT], outs=[pos])
        p.emit("dve", lambda e: e.tensor_scalar(
            pos.ap[:, :], pos.ap[:, :], off1.ap[:, 0:1], None, ALU.add),
            ins=[off1, pos], outs=[pos])
        p.emit("dve", lambda e: e.tensor_sub(pos.ap[:, :], pos.ap[:, :], ohT.ap[:, :]),
               ins=[ohT, pos], outs=[pos])
        p.emit("dve", lambda e: e.tensor_scalar(pos.ap[:, :], pos.ap[:, :],
                                                2048.0, None, ALU.is_lt),
               ins=[pos], outs=[pos])
        p.emit("dve", lambda e: e.tensor_mul(ohT.ap[:, :], ohT.ap[:, :], pos.ap[:, :]),
               ins=[pos], outs=[ohT])
        rcb = load_w_bf("rcb", dram["L1_ffn_res_coef"], D, 2, stJ)
        for cx, dstr in ((0, c0r), (1, c1r)):
            war = p.begin_fill(dstr)
            dl = None
            for tt in range(NT):
                ctok = None
                cwar = p.begin_fill(psC)
                for c in range(NDC):
                    ctok = p.emit("pe", lambda e, cx=cx, c=c, tt=tt: e.matmul(
                        psC.ap[0:1, :], rcb.ap[:, c * 2 + cx:c * 2 + cx + 1],
                        ao_bf.ap[:, c * T + tt * TD:c * T + (tt + 1) * TD],
                        start=(c == 0), stop=(c == NDC - 1)),
                        ins=[rcb, ao_bf], waits=cwar if c == 0 else ())
                p.setw(psC, ctok)
                dl = p.emit("dve", lambda e, tt=tt, dstr=dstr: e.tensor_copy(
                    dstr.ap[0:1, tt * TD:(tt + 1) * TD], psC.ap[0:1, :]),
                    ins=[psC], waits=war if tt == 0 else ())
            p.setw(dstr, dl)
        p.emit("dve", lambda e: e.tensor_max(mxr.ap[0:1, :], c0r.ap[0:1, :], c1r.ap[0:1, :]),
               ins=[c0r, c1r], outs=[mxr])
        for dstr in (c0r, c1r):
            p.emit("dve", lambda e, dstr=dstr: e.tensor_sub(
                dstr.ap[0:1, :], dstr.ap[0:1, :], mxr.ap[0:1, :]),
                ins=[mxr], outs=[dstr])
            p.emit("act", lambda e, dstr=dstr: e.activation(
                dstr.ap[0:1, :], dstr.ap[0:1, :], AF.Exp), ins=[], outs=[dstr])
        p.emit("dve", lambda e: e.tensor_add(mxr.ap[0:1, :], c0r.ap[0:1, :], c1r.ap[0:1, :]),
               ins=[c0r, c1r], outs=[mxr])
        p.emit("dve", lambda e: e.reciprocal(mxr.ap[0:1, :], mxr.ap[0:1, :]),
               ins=[], outs=[mxr])
        p.emit("dve", lambda e: e.tensor_mul(c0r.ap[0:1, :], c0r.ap[0:1, :], mxr.ap[0:1, :]),
               ins=[mxr], outs=[c0r])
        p.emit("dve", lambda e: e.tensor_mul(c1r.ap[0:1, :], c1r.ap[0:1, :], mxr.ap[0:1, :]),
               ins=[mxr], outs=[c1r])
        p.emit("dve", lambda e: e.tensor_copy(brow.ap[0:1, :], c0r.ap[0:1, :]),
               ins=[c0r], outs=[brow])
        twar = p.begin_fill(trow)
        ttok = None
        for tt in range(NT):
            p.emit("pe", lambda e, tt=tt: e.matmul(
                psR1.ap[0:1, :], ones_col.ap[0:8, 0:1],
                ohT.ap[:, tt * TD:(tt + 1) * TD], start=True, stop=True),
                ins=[ohT, ones_col], outs=[psR1])
            p.emit("pe", lambda e, tt=tt: e.matmul(
                psE.ap[0:1, :], consts["hc_sel8"].ap[0:2, 128:129],
                R2.ap[:, tt * TD:(tt + 1) * TD], start=True, stop=True),
                ins=[R2, consts["hc_sel8"]], outs=[psE])
            p.emit("dve", lambda e: e.tensor_copy(sxr.ap[0:1, :], psE.ap[0:1, :]),
                   ins=[psE], outs=[sxr])
            ttok = p.emit("dve", lambda e, tt=tt: e.tensor_mul(
                trow.ap[0:1, tt * TD:(tt + 1) * TD], psR1.ap[0:1, :], sxr.ap[0:1, :]),
                ins=[psR1, sxr], waits=twar if tt == 0 else ())
        p.setw(trow, ttok)
        p.emit("dve", lambda e: e.tensor_mul(grow.ap[0:1, :], trow.ap[0:1, :],
                                             c1r.ap[0:1, :]),
               ins=[trow, c1r], outs=[grow])
        war = p.begin_fill(w8)
        lt = None
        for tt in range(NT):
            bcast(grow, tt * TD, TD, 8, psE)
            lt = p.emit("dve", lambda e, tt=tt: e.tensor_copy(
                w8.ap[:, tt * TD:(tt + 1) * TD], psE.ap[:, 0:TD]),
                ins=[psE], waits=war if tt == 0 else ())
        p.setw(w8, lt)
        p.emit("dve", lambda e: e.tensor_mul(arow.ap[:, :], ohT.ap[:, :], w8.ap[:, :]),
               ins=[ohT, w8], outs=[arow])
        p.flush("j")
        stJ.close()

    phase_l1_gate()
    stackLG.close()
    if upto <= 9:
        return nc, dram

    def phase_l1_resmlp():
        stK = contextlib.ExitStack()
        rhbuf = sb("rhbuf", [128, 32 * TD], BF16, stK)
        wric = [sb(f"wric{i}", [128, 8 * 128], BF16, stK) for i in range(2)]
        wroc = [sb(f"wroc{i}", [128, 32 * 128], BF16, stK) for i in range(2)]
        kstg = [sb(f"kstg{i}", [128, 32 * 128], F32, stK) for i in range(2)]
        kstg2 = [sb(f"kstg2_{i}", [128, 8 * 128], F32, stK) for i in range(2)]
        ribc = sb("ribc", [128, 32], F32, stK)
        robc = sb("robc", [128, 8], F32, stK)
        cfB = sb("cfB", [128, TD], F32, stK)
        xnt = sb("kxnt", [128, TD], F32, stK)
        otile = [sb(f"kot{i}", [128, TD], F32, stK) for i in range(2)]
        psH = [psb(f"kpsH{i}", [128, TD], F32, stK) for i in range(2)]
        psO = [psb(f"kpsO{i}", [128, TD], F32, stK) for i in range(2)]
        psB = psb("kpsB", [128, TD], F32, stK)
        load_col_vec(ribc, dram["L1_ffn_res_inter_b"][0, :], 32, psB)
        load_col_vec(robc, dram["L1_ffn_res_output_b"], 8, psB)
        for it in range(NT):
            p.emit("pe", lambda e, it=it: e.matmul(
                psB.ap[:, :], ones_row.ap[0:1, :],
                brow.ap[0:1, it * TD:(it + 1) * TD], start=True, stop=True),
                ins=[brow, ones_row], outs=[psB])
            p.emit("dve", lambda e: e.tensor_copy(cfB.ap[:, :], psB.ap[:, :]),
                   ins=[psB], outs=[cfB])
            rhwar = p.begin_fill(rhbuf)
            rhl = None
            for ic in range(32):
                b = ic % 2
                hw_cast_load(wric[b],
                             dram["L1_ffn_res_inter_w"][:, ic * 128:(ic + 1) * 128],
                             8, 128, kstg2[b])
                htok = None
                hwar = p.begin_fill(psH[b])
                for c in range(NDC):
                    htok = p.emit("pe", lambda e, c=c, it=it, b=b: e.matmul(
                        psH[b].ap[:, :], wric[b].ap[:, c * 128:(c + 1) * 128],
                        ao_bf.ap[:, c * T + it * TD:c * T + (it + 1) * TD],
                        start=(c == 0), stop=(c == NDC - 1)),
                        ins=[wric[b], ao_bf], waits=hwar if c == 0 else ())
                p.setw(psH[b], htok)
                rhl = p.emit("act", lambda e, ic=ic, b=b: e.activation(
                    rhbuf.ap[:, ic * TD:(ic + 1) * TD], psH[b].ap[:, :],
                    AF.Gelu_apprx_tanh, bias=ribc.ap[:, ic:ic + 1], scale=1.0),
                    ins=[psH[b], ribc], waits=rhwar if ic == 0 else ())
            p.setw(rhbuf, rhl)
            for dc in range(NDC):
                b = dc % 2
                hw_cast_load(wroc[b],
                             dram["L1_ffn_res_output_w"][:, dc * 128:(dc + 1) * 128],
                             32, 128, kstg[b])
                otok = None
                owar = p.begin_fill(psO[b])
                for ic in range(32):
                    otok = p.emit("pe", lambda e, ic=ic, b=b: e.matmul(
                        psO[b].ap[:, :], wroc[b].ap[:, ic * 128:(ic + 1) * 128],
                        rhbuf.ap[:, ic * TD:(ic + 1) * TD],
                        start=(ic == 0), stop=(ic == 31)),
                        ins=[wroc[b], rhbuf], waits=owar if ic == 0 else ())
                p.setw(psO[b], otok)
                p.dma("sp", xnt, dram["xn1_f32"][dc * 128:(dc + 1) * 128,
                                                 it * TD:(it + 1) * TD])
                ot_ = otile[b]
                p.emit("dve", lambda e, dc=dc, b=b, ot_=ot_: e.tensor_scalar(
                    ot_.ap[:, :], psO[b].ap[:, :], robc.ap[:, dc:dc + 1], None, ALU.add),
                    ins=[psO[b], robc], outs=[ot_])
                p.emit("dve", lambda e, ot_=ot_: e.tensor_mul(ot_.ap[:, :], ot_.ap[:, :],
                                                              cfB.ap[:, :]),
                       ins=[cfB], outs=[ot_])
                p.emit("dve", lambda e, ot_=ot_: e.tensor_add(ot_.ap[:, :], ot_.ap[:, :],
                                                              xnt.ap[:, :]),
                       ins=[xnt], outs=[ot_])
                p.dma("sp", dram["acc1"][dc * 128:(dc + 1) * 128, it * TD:(it + 1) * TD],
                      ot_, ins=[ot_])
        p.flush("k")
        stK.close()

    phase_l1_resmlp()
    if upto <= 10:
        return nc, dram

    def phase_l1_experts():
        stL = contextlib.ExitStack()
        hbuf = sb("lhbuf", [128, 16 * T], BF16, stL)
        hbufc = [Buf(hbuf.ap[:, hc * T:(hc + 1) * T]) for hc in range(16)]
        woh = sb("lwoh", [128, 16 * D], BF16, stL)
        wic = [sb(f"lwic{i}", [128, 8 * 128], BF16, stL) for i in range(2)]
        lstg = [sb("lstg0", [128, 8 * 128], F32, stL)]
        ibc = sb("libc", [128, 16], F32, stL)
        obc = sb("lobc", [128, 8], F32, stL)
        alB = sb("lalB", [128, T], F32, stL)
        ctile = [sb(f"lct{i}", [128, TD], F32, stL) for i in range(2)]
        psA = [psb(f"lpsA{i}", [128, TD], F32, stL) for i in range(2)]
        psY = [psb(f"lpsY{i}", [128, TD], F32, stL) for i in range(2)]
        psAl = psb("lpsAl", [128, TD], F32, stL)
        prev_acc = None
        for ex in range(8):
            load_col_vec(obc, dram["L1_ffn_output_b"][ex], 8, psAl)
            awar = p.begin_fill(alB)
            al = None
            for it in range(NT):
                p.emit("pe", lambda e, ex=ex, it=it: e.matmul(
                    psAl.ap[:, :], consts["hc_sel8"].ap[:, ex * 128:(ex + 1) * 128],
                    arow.ap[:, it * TD:(it + 1) * TD], start=True, stop=True),
                    ins=[arow, consts["hc_sel8"]], outs=[psAl])
                al = p.emit("dve", lambda e, it=it: e.tensor_copy(
                    alB.ap[:, it * TD:(it + 1) * TD], psAl.ap[:, :]),
                    ins=[psAl], waits=awar if it == 0 else ())
            p.setw(alB, al)
            for hh in range(2):
                load_col_vec(ibc, dram["L1_ffn_inter_b"][ex, hh * 2048:(hh + 1) * 2048],
                             16, psAl)
                war = p.begin_fill(woh)
                wl = p.dma("pool", woh.ap.rearrange("p (c d) -> p c d", c=16),
                           dram["L1_ffn_output_w"][ex][hh * 2048:(hh + 1) * 2048, :]
                           .rearrange("(c p) d -> p c d", p=128),
                           waits=war, key=woh)
                p.setw(woh, wl)
                for ic in range(16):
                    hcwar = p.begin_fill(hbufc[ic])
                    b = ic % 2
                    hw_cast_load(wic[b],
                                 dram["L1_ffn_inter_w"][ex][:, hh * 2048 + ic * 128:
                                                            hh * 2048 + (ic + 1) * 128],
                                 8, 128, lstg[0])
                    for it in range(NT):
                        pb = (ic * NT + it) % 2
                        pwar = p.begin_fill(psA[pb])
                        ptok = None
                        for c in range(NDC):
                            ptok = p.emit("pe", lambda e, c=c, it=it, b=b, pb=pb: e.matmul(
                                psA[pb].ap[:, :], wic[b].ap[:, c * 128:(c + 1) * 128],
                                ao_bf.ap[:, c * T + it * TD:c * T + (it + 1) * TD],
                                start=(c == 0), stop=(c == NDC - 1)),
                                ins=[wic[b], ao_bf], waits=pwar if c == 0 else ())
                        p.setw(psA[pb], ptok)
                        hl = p.emit("act", lambda e, ic=ic, it=it, pb=pb: e.activation(
                            hbuf.ap[:, ic * T + it * TD:ic * T + (it + 1) * TD],
                            psA[pb].ap[:, :], AF.Gelu_apprx_tanh,
                            bias=ibc.ap[:, ic:ic + 1], scale=1.0),
                            ins=[psA[pb], ibc],
                            waits=hcwar if it == 0 else ())
                    p.setw(hbufc[ic], hl)
                for dc in range(NDC):
                    for it in range(NT):
                        pb = (dc * NT + it) % 2
                        ct = ctile[pb]
                        ywar = p.begin_fill(psY[pb])
                        ytok = None
                        for ic in range(16):
                            ytok = p.emit("pe", lambda e, ic=ic, dc=dc, it=it, pb=pb: e.matmul(
                                psY[pb].ap[:, :],
                                woh.ap[:, ic * D + dc * 128:ic * D + dc * 128 + 128],
                                hbuf.ap[:, ic * T + it * TD:ic * T + (it + 1) * TD],
                                start=(ic == 0), stop=(ic == 15)),
                                ins=[woh, hbufc[ic]], waits=ywar if ic == 0 else ())
                        p.setw(psY[pb], ytok)
                        cwar = p.begin_fill(ct)
                        if hh == 0:
                            p.emit("dve", lambda e, dc=dc, pb=pb, ct=ct: e.tensor_scalar(
                                ct.ap[:, :], psY[pb].ap[:, :], obc.ap[:, dc:dc + 1],
                                None, ALU.add), ins=[psY[pb], obc], waits=cwar)
                        else:
                            p.emit("dve", lambda e, pb=pb, ct=ct: e.tensor_copy(
                                ct.ap[:, :], psY[pb].ap[:, :]),
                                ins=[psY[pb]], waits=cwar)
                        clast = p.emit("dve", lambda e, it=it, ct=ct: e.tensor_mul(
                            ct.ap[:, :], ct.ap[:, :],
                            alB.ap[:, it * TD:(it + 1) * TD]),
                            ins=[alB], outs=[])
                        p.setw(ct, clast)
                        tok = p.dma("pool", dram["acc1"][dc * 128:(dc + 1) * 128,
                                                         it * TD:(it + 1) * TD],
                                    ct, ins=[ct],
                                    waits=[prev_acc] if prev_acc is not None and
                                          dc == 0 and it == 0 else (),
                                    accum_op=ALU.add)
                        prev_acc = tok
        p.flush("l")
        stL.close()

    phase_l1_experts()
    if upto <= 11:
        return nc, dram

    phase_combine(dram["x1"], dram["s1"], dram["acc1"], dram["out"], "m")
    return nc, dram




def _in_maps(inputs):
    maps = []
    x = np.asarray(inputs["L0_attn_x"], dtype=np.float32)
    base = {}
    for name, shape in WEIGHT_SPECS:
        base[name] = np.ascontiguousarray(
            np.asarray(inputs[name], dtype=np.float32).reshape(shape))
    for c in range(NCORES):
        m = dict(base)
        m["x"] = np.ascontiguousarray(x[c].T)
        m.update(host_consts(c))
        maps.append(m)
    return maps


def kernel(**inputs):
    nc, _ = build_program()
    res = run_bass_kernel_spmd(nc, _in_maps(inputs), core_ids=list(range(NCORES)))
    out = np.stack([np.ascontiguousarray(np.asarray(r["out"]).T) for r in res.results])
    return out.astype(np.float32)


# version 05
# speedup vs baseline: 1.3245x; 1.3245x over previous
import contextlib
import numpy as np

import concourse.bass as bass
import concourse.mybir as mybir
from concourse.bass_utils import run_bass_kernel_spmd

F32 = mybir.dt.float32
BF16 = mybir.dt.bfloat16
AF = mybir.ActivationFunctionType
ALU = mybir.AluOpType
AX = mybir.AxisListType

T = 2048
D = 1024
NT = 4
TD = 512
NDC = 8
NCORES = 8

ENG2BODY = {"pe": "tensor", "act": "scalar", "dve": "vector", "pool": "gpsimd", "sp": "sync"}
DMA_SEM = {"sp": "dsp", "pool": "dpool", "act": "dact"}


class Op:
    __slots__ = ("fn", "eng", "waits", "inc_sem", "inc_amt", "need", "count", "flushed")

    def __init__(self, fn, eng, waits, inc_sem, inc_amt, need):
        self.fn = fn
        self.eng = eng
        self.waits = waits
        self.inc_sem = inc_sem
        self.inc_amt = inc_amt
        self.need = need
        self.count = None
        self.flushed = False


class Buf:
    __slots__ = ("ap", "wtok", "rtoks")

    def __init__(self, ap):
        self.ap = ap
        self.wtok = None
        self.rtoks = []

    def __getitem__(self, idx):
        return self.ap.__getitem__(idx)


NDMASEM = 64


class Prog:
    def __init__(self, nc, stack):
        self.nc = nc
        self.sem = {}
        for name in ("pe", "act", "dve", "pool", "sp", "dsp", "dpool", "dact", "cc"):
            self.sem[name] = stack.enter_context(nc.semaphore("sem_" + name))
        for i in range(NDMASEM):
            self.sem[f"ds{i}"] = stack.enter_context(nc.semaphore(f"ds{i}"))
        self.cnt = {k: 0 for k in self.sem}
        self.q = {e: [] for e in ENG2BODY}
        self.buf_sem = {}
        self.next_ds = 0
        self.phase_dma_sems = {e: set() for e in ENG2BODY}

    def _dma_sem_for(self, key_buf, eng):
        if key_buf is None:
            return DMA_SEM[eng]
        k = id(key_buf)
        s = self.buf_sem.get(k)
        if s is None:
            s = f"ds{self.next_ds % NDMASEM}"
            self.next_ds += 1
            self.buf_sem[k] = s
        return s

    def emit(self, eng, fn, ins=(), outs=(), waits=(), dma=False, collective=False):
        wl = []
        for b in ins:
            if isinstance(b, Buf):
                if b.wtok is not None:
                    wl.append(b.wtok)
            elif b is not None:
                wl.append(b)
        for b in outs:
            if b.wtok is not None:
                wl.append(b.wtok)
            wl.extend(b.rtoks)
        wl.extend(t for t in waits if t is not None)
        if dma:
            inc_sem, inc_amt, need = dma, 16, True
            self.phase_dma_sems[eng].add(dma)
        elif collective:
            inc_sem, inc_amt, need = "cc", 1, True
        else:
            inc_sem, inc_amt, need = eng, 1, False
        op = Op(fn, eng, wl, inc_sem, inc_amt, need)
        for t in wl:
            if not t.flushed:
                t.need = True
        self.q[eng].append(op)
        for b in ins:
            if isinstance(b, Buf):
                b.rtoks.append(op)
        for b in outs:
            b.wtok = op
            b.rtoks = []
        return op

    def dma(self, eng, out, in_, ins=(), outs=(), waits=(), key=None, **kw):
        def fn(e, o=out, i=in_, kw=kw):
            oap = o.ap if isinstance(o, Buf) else o
            iap = i.ap if isinstance(i, Buf) else i
            return e.dma_start(oap, iap, **kw)
        allins = list(ins) + ([in_] if isinstance(in_, Buf) else [])
        allouts = list(outs) + ([out] if isinstance(out, Buf) else [])
        if key is None:
            if isinstance(out, Buf):
                key = out
            elif allouts:
                key = allouts[0]
            elif isinstance(in_, Buf):
                key = in_
            elif allins:
                key = allins[0]
        sem = self._dma_sem_for(key, eng)
        return self.emit(eng, fn, ins=allins, outs=allouts, waits=waits, dma=sem)

    @staticmethod
    def begin_fill(buf):
        w = ([buf.wtok] if buf.wtok is not None else []) + list(buf.rtoks)
        buf.wtok = None
        buf.rtoks = []
        return w

    @staticmethod
    def setw(buf, tok):
        buf.wtok = tok
        buf.rtoks = []

    def flush(self, name=None):
        nc = self.nc
        for eng, lst in self.q.items():
            for op in lst:
                if op.need:
                    self.cnt[op.inc_sem] += op.inc_amt
                    op.count = self.cnt[op.inc_sem]
        qs = {e: l for e, l in self.q.items() if l}
        if not qs:
            return
        with nc.Block() as blk:
            for ename, lst in qs.items():
                def make_body(lst=lst, ename=ename):
                    def body(e):
                        waited = {}
                        for op in lst:
                            for t in op.waits:
                                if t.flushed or t.count is None:
                                    continue
                                if t.inc_sem == "pe" and ename == "pe":
                                    continue
                                if waited.get(t.inc_sem, 0) >= t.count:
                                    continue
                                e.wait_ge(self.sem[t.inc_sem], t.count)
                                waited[t.inc_sem] = t.count
                            inst = op.fn(e)
                            if op.count is not None:
                                inst.then_inc(self.sem[op.inc_sem], op.inc_amt)
                        for dsem in sorted(self.phase_dma_sems[ename]):
                            if waited.get(dsem, 0) < self.cnt[dsem]:
                                e.wait_ge(self.sem[dsem], self.cnt[dsem])
                        if ename == "pool" and any(
                                op.inc_sem == "cc" and op.count is not None for op in lst):
                            if waited.get("cc", 0) < self.cnt["cc"]:
                                e.wait_ge(self.sem["cc"], self.cnt["cc"])
                    return body
                getattr(blk, ENG2BODY[ename])(make_body())
        for lst in qs.values():
            for op in lst:
                op.flushed = True
        self.q = {e: [] for e in ENG2BODY}
        self.buf_sem = {}
        self.phase_dma_sems = {e: set() for e in ENG2BODY}


WEIGHT_SPECS = [
    ("L0_attn_w_qkv", (D, 3 * D)),
    ("L0_attn_wo", (D, D)),
    ("L0_attn_lambda_q1", (1, 64)),
    ("L0_attn_lambda_q2", (1, 64)),
    ("L0_attn_lambda_k1", (1, 64)),
    ("L0_attn_lambda_k2", (1, 64)),
    ("L0_attn_norm_w", (16, 64)),
    ("L0_ffn_g1", (D,)),
    ("L0_ffn_g2", (D,)),
    ("L0_ffn_wq", (D, D)),
    ("L0_ffn_wk", (D, 256)),
    ("L0_ffn_wv", (D, 256)),
    ("L0_ffn_wo", (D, D)),
    ("L0_ffn_router_w", (D, 8)),
    ("L0_ffn_router_b", (1, 8)),
    ("L0_ffn_w1", (8, D, 2048)),
    ("L0_ffn_w2", (8, D, 2048)),
    ("L0_ffn_w3", (8, 2048, D)),
    ("L1_attn_w_qkv", (D, 3 * D)),
    ("L1_attn_w_out", (D, D)),
    ("L1_attn_b_out", (D,)),
    ("L1_attn_threshold", (1, 1)),
    ("L1_ffn_attn_nw", (D,)),
    ("L1_ffn_attn_nb", (D,)),
    ("L1_ffn_wg", (D, 8)),
    ("L1_ffn_inter_w", (8, D, 4096)),
    ("L1_ffn_inter_b", (8, 4096)),
    ("L1_ffn_output_w", (8, 4096, D)),
    ("L1_ffn_output_b", (8, D)),
    ("L1_ffn_res_inter_w", (D, 4096)),
    ("L1_ffn_res_inter_b", (1, 4096)),
    ("L1_ffn_res_output_w", (4096, D)),
    ("L1_ffn_res_output_b", (D,)),
    ("L1_ffn_res_coef", (D, 2)),
]

HC_SPECS = [
    ("hc_sel8", (8, 1024)),
    ("hc_cos", (128, T)),
    ("hc_sin", (128, T)),
    ("hc_bo4", (128, 4)),
    ("hc_bo2", (128, 2)),
    ("hc_prev2", (8, 16)),
    ("hc_prev1", (8, 8)),
]


def host_consts(core):
    out = {}
    dim = 64
    thetas = 10000.0 ** (-np.arange(0, dim, 2, dtype=np.float32) / dim)
    t = np.arange(T, dtype=np.float32)
    ang = np.outer(thetas, t)
    ang4 = np.tile(ang, (4, 1))
    out["hc_cos"] = np.cos(ang4).astype(np.float32)
    out["hc_sin"] = np.sin(ang4).astype(np.float32)
    bo4 = np.zeros((128, 4), np.float32)
    for j in range(4):
        bo4[32 * j:32 * (j + 1), j] = 1.0
    out["hc_bo4"] = bo4
    bo2 = np.zeros((128, 2), np.float32)
    bo2[0:64, 0] = 1.0
    bo2[64:128, 1] = 1.0
    out["hc_bo2"] = bo2
    pm2 = np.repeat((np.arange(8) < core).astype(np.float32), 2).reshape(1, 16)
    out["hc_prev2"] = np.broadcast_to(pm2, (8, 16)).copy()
    pm1 = (np.arange(8) < core).astype(np.float32).reshape(1, 8)
    out["hc_prev1"] = np.broadcast_to(pm1, (8, 8)).copy()
    sel = np.zeros((8, 8, 128), np.float32)
    for e in range(8):
        sel[e, e, :] = 1.0
    out["hc_sel8"] = sel.transpose(1, 0, 2).reshape(8, 8 * 128)
    return out



def build_program(upto=99, dbg=False):
    nc = bass.Bass("TRN2", target_bir_lowering=False)
    gstack = contextlib.ExitStack()
    p = Prog(nc, gstack)
    uid = [0]

    dram = {}
    dram["x"] = nc.dram_tensor("x", [D, T], F32, kind="ExternalInput").ap()
    for name, shape in WEIGHT_SPECS:
        dram[name] = nc.dram_tensor(name, list(shape), F32, kind="ExternalInput").ap()
    for name, shape in HC_SPECS:
        dram[name] = nc.dram_tensor(name, list(shape), F32, kind="ExternalInput").ap()
    skind = "ExternalOutput" if dbg else "Internal"
    for name in ("xn_f32", "s0", "h1_f32", "acc0", "x1", "xn1_f32", "s1", "acc1"):
        dram[name] = nc.dram_tensor(name, [D, T], F32, kind=skind).ap()
    dram["out"] = nc.dram_tensor("out", [D, T], F32, kind="ExternalOutput").ap()
    dram["cnt0"] = nc.dram_tensor("cnt0", [8, 2], F32, kind="Internal").ap()
    dram["ag0"] = nc.dram_tensor("ag0", [64, 2], F32, kind="Internal").ap()
    dram["cnt1"] = nc.dram_tensor("cnt1", [8, 1], F32, kind="Internal").ap()
    dram["ag1"] = nc.dram_tensor("ag1", [64, 1], F32, kind="Internal").ap()
    NROWS_DBG = 64
    if dbg:
        dram["dbg_rows"] = nc.dram_tensor("dbg_rows", [NROWS_DBG, T], F32,
                                          kind="ExternalOutput").ap()
        dram["acc0b"] = nc.dram_tensor("acc0b", [D, T], F32, kind="ExternalOutput").ap()
    dbg_row_i = [0]

    def sb(name, shape, dt, stack=gstack, side="left"):
        uid[0] += 1
        return Buf(stack.enter_context(
            nc.sbuf_tensor(f"{name}_{uid[0]}", list(shape), dt, side=side)).ap())

    def psb(name, shape, dt=F32, stack=gstack):
        uid[0] += 1
        return Buf(stack.enter_context(
            nc.psum_tensor(f"{name}_{uid[0]}", list(shape), dt)).ap())

    xn_bf = sb("xn_bf", [128, NDC * T], BF16)
    ones_col = sb("ones_col", [128, 1], F32)
    ones_row = sb("ones_row", [1, 128], F32)
    ones_col_bf = sb("ones_col_bf", [128, 1], BF16)
    iota8_col = sb("iota8_col", [8, 1], F32)
    iota8_row128 = sb("iota8_r128", [128, 8], F32)
    ident_bf = sb("ident_bf", [128, 128], BF16)
    ident_f = sb("ident_f", [128, 128], F32)
    cvrow = sb("cvrow", [32, 128], F32)
    arow = sb("arow", [8, T], F32)
    brow = sb("brow", [8, T], F32)
    consts = {}
    for name, shape in HC_SPECS:
        if name in ("hc_cos", "hc_sin"):
            continue
        consts[name] = sb(name, shape, F32)

    def tap_rows(buf, src_slice, n=1, ncols=T):
        if not dbg:
            return None
        r = dbg_row_i[0]
        dbg_row_i[0] += n
        assert dbg_row_i[0] <= NROWS_DBG
        p.dma("sp", dram["dbg_rows"][r:r + n, 0:ncols], buf.ap[src_slice], ins=[buf])
        return r

    p.emit("pool", lambda e: e.memset(ones_col.ap[:, :], 1.0), outs=[ones_col])
    p.emit("pool", lambda e: e.memset(ones_row.ap[:, :], 1.0), outs=[ones_row])
    p.emit("pool", lambda e: e.memset(ones_col_bf.ap[:, :], 1.0), outs=[ones_col_bf])
    p.emit("pool", lambda e: e.iota(iota8_col.ap[:, :], [[1, 1]], base=0, channel_multiplier=1,
                                    allow_small_or_imprecise_dtypes=True), outs=[iota8_col])
    p.emit("pool", lambda e: e.iota(iota8_row128.ap[:, :], [[1, 8]], base=0, channel_multiplier=0,
                                    allow_small_or_imprecise_dtypes=True), outs=[iota8_row128])
    for idb, idf in ((ident_bf, None), (ident_f, None)):
        p.emit("pool", lambda e, b=idb: e.memset(b.ap[:, :], 0.0), outs=[idb])
        p.emit("pool", lambda e, b=idb: e.affine_select(
            b.ap[:, :], b.ap[:, :], [[-1, 128]], compare_op=ALU.not_equal,
            fill=1.0, base=0, channel_multiplier=1), ins=[idb], outs=[idb])
    for name, _ in HC_SPECS:
        if name in ("hc_cos", "hc_sin"):
            continue
        p.dma("sp", consts[name], dram[name])
    p.flush("consts")

    def load_fmaj(dst_buf, src_dram, nchunk=NDC, eng="sp"):
        war = p.begin_fill(dst_buf)
        last = None
        for c in range(nchunk):
            last = p.dma(eng, dst_buf.ap[:, c * T:(c + 1) * T],
                         src_dram[c * 128:(c + 1) * 128, :],
                         waits=war if c == 0 else (), key=dst_buf)
        p.setw(dst_buf, last)
        return last

    def store_fmaj(dst_dram, src_buf, nchunk=NDC, eng="sp"):
        for c in range(nchunk):
            p.dma(eng, dst_dram[c * 128:(c + 1) * 128, :],
                  src_buf.ap[:, c * T:(c + 1) * T], ins=[src_buf])

    def load_w_bf(name, src_dram, rows, cols, stack, eng="pool"):
        nch = rows // 128
        wbuf = sb(name, [128, nch * cols], BF16, stack)
        war = p.begin_fill(wbuf)
        last = p.dma(eng, wbuf.ap.rearrange("p (c k) -> p c k", c=nch),
                     src_dram.rearrange("(c p) k -> p c k", p=128),
                     waits=war, key=wbuf)
        p.setw(wbuf, last)
        return wbuf

    def hw_cast_load(dst_bf, src_dram_2d, nch, k, stg, eng="act"):
        war = p.begin_fill(stg)
        ld = p.dma(eng, stg.ap.rearrange("p (c k) -> p c k", c=nch),
                   src_dram_2d.rearrange("(c p) k -> p c k", p=128),
                   waits=war, key=stg)
        p.setw(stg, ld)
        dwar = p.begin_fill(dst_bf)
        tok = p.emit("dve", lambda e: e.tensor_copy(dst_bf.ap[:, :], stg.ap[:, :]),
                     ins=[stg], waits=dwar)
        p.setw(dst_bf, tok)
        return tok

    def load_col_vec(dst, src_dram_1d, n, psT128, eng="sp"):
        war = p.begin_fill(dst)
        cwar = p.begin_fill(cvrow)
        ld = p.dma(eng, cvrow.ap[0:n, 0:128],
                   src_dram_1d.rearrange("(a b) -> a b", a=n),
                   waits=list(war) + list(cwar), key=cvrow)
        p.setw(cvrow, ld)
        p.emit("pe", lambda e, n=n: e.transpose(psT128.ap[:, 0:n], cvrow.ap[0:n, 0:128],
                                                ident_f.ap[0:n, 0:n]),
               ins=[cvrow, ident_f], outs=[psT128])
        tok = p.emit("dve", lambda e, n=n: e.tensor_copy(dst.ap[:, 0:n],
                                                         psT128.ap[:, 0:n]),
                     ins=[psT128])
        p.setw(dst, tok)
        return tok

    def colsums(src_list, out_psum, square, stack):
        war = p.begin_fill(out_psum)
        sq = sb("csq", [128, T], F32, stack) if square else None
        first_tok = [True]
        for ci, (buf, off) in enumerate(src_list):
            if square:
                p.emit("act", lambda e, b=buf, o=off: e.activation(
                    sq.ap[:, :], b.ap[:, o:o + T], AF.Square), ins=[buf], outs=[sq])
                rbuf, roff = sq, 0
            else:
                rbuf, roff = buf, off
            for tt in range(NT):
                last = ci == NDC - 1 and tt == NT - 1
                tok = p.emit("pe", lambda e, rb=rbuf, ro=roff, tt=tt, ci=ci:
                             e.matmul(out_psum.ap[0:1, tt * TD:(tt + 1) * TD],
                                      ones_col.ap[:, 0:1],
                                      rb.ap[:, ro + tt * TD:ro + (tt + 1) * TD],
                                      start=(ci == 0), stop=(ci == NDC - 1)),
                             ins=[rbuf, ones_col],
                             waits=war if first_tok[0] else ())
                first_tok[0] = False
        p.setw(out_psum, tok)
        return tok

    def bcast(row_buf, part0_off, n, npart, out_ps):
        return p.emit("pe", lambda e: e.matmul(
            out_ps.ap[0:npart, 0:n], ones_row.ap[0:1, 0:npart],
            row_buf.ap[0:1, part0_off:part0_off + n], start=True, stop=True),
            ins=[row_buf, ones_row], outs=[out_ps])

    def phase_ln(src_dram, dst_f32_dram, dst_bf, eps, phname):
        stack = contextlib.ExitStack()
        xt = [sb(f"lnxt{i}", [128, T], F32, stack) for i in range(2)]
        sq = [sb(f"lnsq{i}", [128, T], F32, stack) for i in range(2)]
        stat = psb("lnstat", [1, T], F32, stack)
        mrow = sb("lnm", [1, T], F32, stack)
        srow = sb("lns", [1, T], F32, stack)
        trow = sb("lnt", [1, T], F32, stack)
        bps = psb("lnb", [128, TD], F32, stack)
        mB = sb("lnmB", [128, T], F32, stack)
        rB = sb("lnrB", [128, T], F32, stack)
        tmp = sb("lntmp", [128, T], F32, stack)
        for sweep, (dstrow, use_sq) in enumerate(((mrow, False), (srow, True))):
            war = p.begin_fill(stat)
            tok = None
            for c in range(NDC):
                b = c % 2
                p.dma("sp", xt[b], src_dram[c * 128:(c + 1) * 128, :])
                if use_sq:
                    p.emit("act", lambda e, b=b: e.activation(sq[b].ap[:, :],
                                                              xt[b].ap[:, :], AF.Square),
                           ins=[xt[b]], outs=[sq[b]])
                    rbuf = sq[b]
                else:
                    rbuf = xt[b]
                for tt in range(NT):
                    tok = p.emit("pe", lambda e, tt=tt, c=c, rb=rbuf: e.matmul(
                        stat.ap[0:1, tt * TD:(tt + 1) * TD], ones_col.ap[:, 0:1],
                        rb.ap[:, tt * TD:(tt + 1) * TD],
                        start=(c == 0), stop=(c == NDC - 1)),
                        ins=[rbuf, ones_col], waits=war if c == 0 and tt == 0 else ())
            p.setw(stat, tok)
            p.emit("dve", lambda e, dr=dstrow: e.tensor_scalar(
                dr.ap[0:1, :], stat.ap[0:1, :], 1.0 / D, None, ALU.mult),
                ins=[stat], outs=[dstrow])
        p.emit("dve", lambda e: e.tensor_mul(trow.ap[0:1, :], mrow.ap[0:1, :], mrow.ap[0:1, :]),
               ins=[mrow], outs=[trow])
        p.emit("dve", lambda e: e.tensor_sub(srow.ap[0:1, :], srow.ap[0:1, :], trow.ap[0:1, :]),
               ins=[trow, srow], outs=[srow])
        p.emit("dve", lambda e: e.tensor_scalar(srow.ap[0:1, :], srow.ap[0:1, :],
                                                eps, None, ALU.add), ins=[srow], outs=[srow])
        p.emit("act", lambda e: e.activation(srow.ap[0:1, :], srow.ap[0:1, :], AF.Sqrt),
               ins=[srow], outs=[srow])
        p.emit("dve", lambda e: e.reciprocal(srow.ap[0:1, :], srow.ap[0:1, :]),
               ins=[srow], outs=[srow])
        for dst, row in ((mB, mrow), (rB, srow)):
            war = p.begin_fill(dst)
            lastc = None
            for tt in range(NT):
                bcast(row, tt * TD, TD, 128, bps)
                lastc = p.emit("dve", lambda e, tt=tt, dst=dst: e.tensor_copy(
                    dst.ap[:, tt * TD:(tt + 1) * TD], bps.ap[:, 0:TD]),
                    ins=[bps], waits=war if tt == 0 else ())
            p.setw(dst, lastc)
        bfwar = p.begin_fill(dst_bf)
        lastbf = None
        tmp2 = sb("lntmp2", [128, T], F32, stack)
        tmps = (tmp, tmp2)
        for c in range(NDC):
            b = c % 2
            p.dma("sp", xt[b], src_dram[c * 128:(c + 1) * 128, :])
            p.emit("dve", lambda e, b=b: e.tensor_sub(tmps[b].ap[:, :], xt[b].ap[:, :],
                                                      mB.ap[:, :]),
                   ins=[xt[b], mB], outs=[tmps[b]])
            p.emit("dve", lambda e, b=b: e.tensor_mul(tmps[b].ap[:, :], tmps[b].ap[:, :],
                                                      rB.ap[:, :]),
                   ins=[rB], outs=[tmps[b]])
            lastbf = p.emit("dve", lambda e, c=c, b=b: e.tensor_copy(
                dst_bf.ap[:, c * T:(c + 1) * T], tmps[b].ap[:, :]),
                ins=[tmps[b]], waits=bfwar if c == 0 else ())
            p.dma("sp", dst_f32_dram[c * 128:(c + 1) * 128, :], tmps[b], ins=[tmps[b]])
        p.setw(dst_bf, lastbf)
        p.flush(phname)
        stack.close()

    phase_ln(dram["x"], dram["xn_f32"], xn_bf, 1e-5, "ln0")
    if upto <= 0:
        return nc, dram

    def phase_l0_attn():
        stackB = contextlib.ExitStack()
        qT = sb("qT", [128, NDC * T], BF16, stackB)
        kT = sb("kT", [128, NDC * T], BF16, stackB)
        vT = sb("vT", [128, NDC * T], BF16, stackB)
        lam_col = sb("lamcol", [128, 1], F32, stackB)
        nw02 = sb("nw02", [64, 16], F32, stackB)

        st0 = contextlib.ExitStack()
        lr = [sb(f"lam{i}", [1, 64], F32, st0) for i in range(4)]
        lsc = sb("lsc", [1, 4], F32, st0)
        psl = psb("psl", [128, 1], F32, st0)
        for i, wname in enumerate(("L0_attn_lambda_q1", "L0_attn_lambda_k1",
                                   "L0_attn_lambda_q2", "L0_attn_lambda_k2")):
            p.dma("sp", lr[i], dram[wname])
        for pair, (a, b_) in enumerate(((0, 1), (2, 3))):
            p.emit("dve", lambda e, a=a, b_=b_: e.tensor_mul(
                lr[a].ap[0:1, :], lr[a].ap[0:1, :], lr[b_].ap[0:1, :]),
                ins=[lr[b_], lr[a]], outs=[lr[a]])
            p.emit("dve", lambda e, a=a, pair=pair: e.reduce_sum(
                lsc.ap[0:1, pair:pair + 1], lr[a].ap[0:1, :], axis=AX.X),
                ins=[lr[a]], outs=[lsc] if pair else [], waits=())
        p.emit("act", lambda e: e.activation(lsc.ap[0:1, 0:2], lsc.ap[0:1, 0:2], AF.Exp),
               ins=[lsc], outs=[lsc])
        p.emit("dve", lambda e: e.tensor_sub(lsc.ap[0:1, 2:3], lsc.ap[0:1, 0:1],
                                             lsc.ap[0:1, 1:2]), ins=[lsc], outs=[lsc])
        p.emit("dve", lambda e: e.tensor_scalar(lsc.ap[0:1, 2:3], lsc.ap[0:1, 2:3],
                                                0.8, None, ALU.add), ins=[lsc], outs=[lsc])
        bcast(lsc, 2, 1, 128, psl)
        p.emit("dve", lambda e: e.tensor_copy(lam_col.ap[:, :], psl.ap[:, 0:1]),
               ins=[psl], outs=[lam_col])
        cwar = p.begin_fill(cvrow)
        ld = p.dma("sp", cvrow.ap[0:16, 0:64], dram["L0_attn_norm_w"],
                   waits=cwar, key=cvrow)
        p.setw(cvrow, ld)
        psnw = psb("psnw", [64, 16], F32, st0)
        p.emit("pe", lambda e: e.transpose(psnw.ap[:, :], cvrow.ap[0:16, 0:64],
                                           ident_f.ap[0:16, 0:16]),
               ins=[cvrow, ident_f], outs=[psnw])
        p.emit("dve", lambda e: e.tensor_scalar(nw02.ap[0:64, :], psnw.ap[:, :],
                                                0.2, None, ALU.mult), ins=[psnw], outs=[nw02])
        p.flush("b0")
        st0.close()

        st1 = contextlib.ExitStack()
        wq = load_w_bf("wqkv", dram["L0_attn_w_qkv"], D, 3 * D, st1)
        mmps = [psb(f"b1ps{i}", [128, TD], F32, st1) for i in range(2)]
        for fi, dst in ((0, qT), (1, kT), (2, vT)):
            war = p.begin_fill(dst)
            lastw = None
            for fc in range(NDC):
                for tt in range(NT):
                    pp = mmps[(fc * NT + tt) % 2]
                    ptok = None
                    for c in range(NDC):
                        ptok = p.emit("pe", lambda e, fi=fi, fc=fc, tt=tt, c=c, pp=pp: e.matmul(
                            pp.ap[:, :],
                            wq.ap[:, c * 3 * D + fi * D + fc * 128:
                                  c * 3 * D + fi * D + fc * 128 + 128],
                            xn_bf.ap[:, c * T + tt * TD:c * T + (tt + 1) * TD],
                            start=(c == 0), stop=(c == NDC - 1)),
                            ins=[wq, xn_bf], outs=[pp] if c == 0 else (),
                            waits=())
                    p.setw(pp, ptok)
                    lastw = p.emit("dve", lambda e, fc=fc, tt=tt, dst=dst, pp=pp: e.tensor_copy(
                        dst.ap[:, fc * T + tt * TD:fc * T + (tt + 1) * TD], pp.ap[:, :]),
                        ins=[pp], waits=war if fc == 0 and tt == 0 else ())
            p.setw(dst, lastw)
        p.flush("b1")
        st1.close()

        OT2 = sb("OT2", [128, NDC * T], BF16, stackB)
        st2 = contextlib.ExitStack()
        psS1 = psb("psS1", [128, 2 * TD], F32, st2)
        psS2 = psb("psS2", [128, 2 * TD], F32, st2)
        psPV1 = psb("psPV1", [65, TD], F32, st2)
        psPV2 = psb("psPV2", [65, TD], F32, st2)
        psB64 = psb("psB64", [128, TD], F32, st2)
        psT = psb("psT", [128, 64], BF16, st2)
        P1 = sb("P1", [128, 2 * TD], BF16, st2)
        P2 = sb("P2", [128, 2 * TD], BF16, st2)
        S1h = [Buf(psS1.ap[:, 0:TD]), Buf(psS1.ap[:, TD:2 * TD])]
        S2h = [Buf(psS2.ap[:, 0:TD]), Buf(psS2.ap[:, TD:2 * TD])]
        P1h = [Buf(P1.ap[:, 0:TD]), Buf(P1.ap[:, TD:2 * TD])]
        P2h = [Buf(P2.ap[:, 0:TD]), Buf(P2.ap[:, TD:2 * TD])]
        v_tok = sb("v_tok", [128, 16 * 65], BF16, st2)
        sqs_ = sb("sqs", [128, TD], F32, st2)
        negMb = sb("negMb", [128, 4], F32, st2)
        qmax = sb("qmax", [4, 1], F32, st2)
        kmax = sb("kmax", [4, 1], F32, st2)
        m4 = sb("m4", [4, 1], F32, st2)
        nrow = sb("nrow", [1, 4], F32, st2)
        drow = sb("drow", [1, 2 * TD], F32, st2)
        rb1 = sb("rb1", [64, TD], F32, st2)
        rb2 = sb("rb2", [64, TD], F32, st2)
        oa = sb("oa", [64, TD], F32, st2)
        rrow = sb("rrow", [1, TD], F32, st2)
        out_n = sb("out_n", [64, T], BF16, st2)
        OT2v = OT2.ap.rearrange("p (c d x) -> p c d x", c=8, d=64, x=32)

        def head_norms(src, dst, qc):
            war = p.begin_fill(dst)
            lt = None
            for tt in range(NT):
                p.emit("act", lambda e, tt=tt, qc=qc, src=src: e.activation(
                    sqs_.ap[:, :],
                    src.ap[:, qc * T + tt * TD:qc * T + (tt + 1) * TD], AF.Square),
                    ins=[src], outs=[sqs_])
                p.emit("pe", lambda e, tt=tt: e.matmul(
                    psB64.ap[0:4, :], consts["hc_bo4"].ap[:, :],
                    sqs_.ap[:, :], start=True, stop=True),
                    ins=[sqs_, consts["hc_bo4"]], outs=[psB64])
                if tt == 0:
                    lt = p.emit("dve", lambda e: e.reduce_max(
                        dst.ap[:, 0:1], psB64.ap[0:4, :], axis=AX.X),
                        ins=[psB64], waits=war)
                else:
                    p.emit("dve", lambda e: e.reduce_max(
                        m4.ap[:, 0:1], psB64.ap[0:4, :], axis=AX.X), ins=[psB64], outs=[m4])
                    lt = p.emit("dve", lambda e: e.tensor_max(
                        dst.ap[:, 0:1], dst.ap[:, 0:1], m4.ap[:, 0:1]),
                        ins=[m4], waits=[lt])
            p.setw(dst, lt)

        for qc in range(NDC):
            head_norms(qT, qmax, qc)
            head_norms(kT, kmax, qc)
            p.emit("dve", lambda e: e.tensor_mul(qmax.ap[:, :], qmax.ap[:, :], kmax.ap[:, :]),
                   ins=[kmax, qmax], outs=[qmax])
            p.emit("act", lambda e: e.activation(qmax.ap[:, :], qmax.ap[:, :], AF.Sqrt),
                   ins=[qmax], outs=[qmax])
            p.emit("dve", lambda e: e.tensor_scalar(qmax.ap[:, :], qmax.ap[:, :],
                                                    -0.125, None, ALU.mult),
                   ins=[qmax], outs=[qmax])
            p.emit("pe", lambda e: e.transpose(psB64.ap[0:1, 0:4], qmax.ap[:, 0:1],
                                               ident_f.ap[0:4, 0:4]),
                   ins=[qmax, ident_f], outs=[psB64])
            p.emit("dve", lambda e: e.tensor_copy(nrow.ap[0:1, :], psB64.ap[0:1, 0:4]),
                   ins=[psB64], outs=[nrow])
            bcast(nrow, 0, 4, 128, psB64)
            p.emit("dve", lambda e: e.tensor_copy(negMb.ap[:, :], psB64.ap[:, 0:4]),
                   ins=[psB64], outs=[negMb])

            for h2 in range(2):
                hb = 64 * h2
                head = qc * 2 + h2
                vwar = p.begin_fill(v_tok)
                p.emit("pool", lambda e: e.memset(v_tok.ap[:, :], 1.0), waits=vwar)
                vlast = None
                for cc in range(16):
                    p.emit("pe", lambda e, cc=cc, hb=hb, qc=qc: e.transpose(
                        psT.ap[:, :], vT.ap[hb:hb + 64, qc * T + cc * 128:qc * T + (cc + 1) * 128],
                        ident_bf.ap[hb:hb + 64, hb:hb + 64]),
                        ins=[vT, ident_bf], outs=[psT])
                    vlast = p.emit("dve", lambda e, cc=cc: e.tensor_copy(
                        v_tok.ap[:, cc * 65:cc * 65 + 64], psT.ap[:, :]), ins=[psT])
                p.setw(v_tok, vlast)
                for it in range(NT):
                    pv1war = p.begin_fill(psPV1)
                    pv2war = p.begin_fill(psPV2)
                    pvtok1 = pvtok2 = None

                    def b2_scores(jc, hb=hb, qc=qc, h2=h2, it=it):
                        jj = jc % 2
                        p.emit("pe", lambda e, jc=jc, jj=jj: e.matmul(
                            S1h[jj].ap,
                            kT.ap[hb:hb + 32, qc * T + jc * 128:qc * T + (jc + 1) * 128],
                            qT.ap[hb:hb + 32, qc * T + it * TD:qc * T + (it + 1) * TD],
                            start=True, stop=True, tile_position=(hb, 0)),
                            ins=[kT, qT], outs=[S1h[jj]])
                        p.emit("pe", lambda e, jc=jc, jj=jj: e.matmul(
                            S2h[jj].ap,
                            kT.ap[hb + 32:hb + 64, qc * T + jc * 128:qc * T + (jc + 1) * 128],
                            qT.ap[hb + 32:hb + 64, qc * T + it * TD:qc * T + (it + 1) * TD],
                            start=True, stop=True, tile_position=(hb + 32, 0)),
                            ins=[kT, qT], outs=[S2h[jj]])
                        p.emit("act", lambda e, jj=jj: e.activation(
                            P1h[jj].ap, S1h[jj].ap, AF.Exp,
                            bias=negMb.ap[:, 2 * h2:2 * h2 + 1], scale=0.125),
                            ins=[S1h[jj], negMb], outs=[P1h[jj]])
                        p.emit("act", lambda e, jj=jj: e.activation(
                            P2h[jj].ap, S2h[jj].ap, AF.Exp,
                            bias=negMb.ap[:, 2 * h2 + 1:2 * h2 + 2], scale=0.125),
                            ins=[S2h[jj], negMb], outs=[P2h[jj]])

                    def b2_pv(jc):
                        nonlocal pvtok1, pvtok2
                        jj = jc % 2
                        pvtok1 = p.emit("pe", lambda e, jc=jc, jj=jj: e.matmul(
                            psPV1.ap[:, :], v_tok.ap[:, jc * 65:(jc + 1) * 65],
                            P1h[jj].ap, start=(jc == 0), stop=(jc == 15)),
                            ins=[v_tok, P1h[jj]],
                            waits=pv1war if jc == 0 else ())
                        pvtok2 = p.emit("pe", lambda e, jc=jc, jj=jj: e.matmul(
                            psPV2.ap[:, :], v_tok.ap[:, jc * 65:(jc + 1) * 65],
                            P2h[jj].ap, start=(jc == 0), stop=(jc == 15)),
                            ins=[v_tok, P2h[jj]],
                            waits=pv2war if jc == 0 else ())

                    for jc in range(16):
                        if jc >= 2:
                            b2_pv(jc - 2)
                        b2_scores(jc)
                    b2_pv(14)
                    b2_pv(15)
                    p.setw(psPV1, pvtok1)
                    p.setw(psPV2, pvtok2)
                    p.emit("dve", lambda e: e.tensor_copy(drow.ap[0:1, 0:TD],
                                                          psPV1.ap[64:65, :]),
                           ins=[psPV1], outs=[drow])
                    p.emit("dve", lambda e: e.tensor_copy(drow.ap[0:1, TD:2 * TD],
                                                          psPV2.ap[64:65, :]),
                           ins=[psPV2, drow], outs=[drow])
                    p.emit("dve", lambda e: e.reciprocal(drow.ap[0:1, :], drow.ap[0:1, :]),
                           ins=[drow], outs=[drow])
                    bcast(drow, 0, TD, 64, psB64)
                    p.emit("dve", lambda e: e.tensor_copy(rb1.ap[:, :], psB64.ap[0:64, 0:TD]),
                           ins=[psB64], outs=[rb1])
                    bcast(drow, TD, TD, 64, psB64)
                    p.emit("dve", lambda e: e.tensor_copy(rb2.ap[:, :], psB64.ap[0:64, 0:TD]),
                           ins=[psB64], outs=[rb2])
                    p.emit("dve", lambda e: e.tensor_mul(rb1.ap[:, :], psPV1.ap[0:64, :],
                                                         rb1.ap[:, :]),
                           ins=[psPV1], outs=[rb1])
                    p.emit("dve", lambda e: e.tensor_mul(rb2.ap[:, :], psPV2.ap[0:64, :],
                                                         rb2.ap[:, :]),
                           ins=[psPV2], outs=[rb2])
                    p.emit("dve", lambda e: e.tensor_scalar(
                        rb2.ap[:, :], rb2.ap[:, :], lam_col.ap[0:64, 0:1], None, ALU.mult),
                        ins=[lam_col], outs=[rb2])
                    p.emit("dve", lambda e: e.tensor_sub(oa.ap[:, :], rb1.ap[:, :],
                                                         rb2.ap[:, :]),
                           ins=[rb1, rb2], outs=[oa])
                    p.emit("act", lambda e: e.activation(rb1.ap[:, :], oa.ap[:, :], AF.Square),
                           ins=[oa], outs=[rb1])
                    p.emit("pe", lambda e: e.matmul(psB64.ap[0:1, :], ones_col.ap[0:64, 0:1],
                                                    rb1.ap[:, :], start=True, stop=True),
                           ins=[rb1, ones_col], outs=[psB64])
                    p.emit("dve", lambda e: e.tensor_scalar(rrow.ap[0:1, :], psB64.ap[0:1, 0:TD],
                                                            1.0 / 64, 1e-6, ALU.mult, ALU.add),
                           ins=[psB64], outs=[rrow])
                    p.emit("act", lambda e: e.activation(rrow.ap[0:1, :], rrow.ap[0:1, :],
                                                         AF.Sqrt), ins=[rrow], outs=[rrow])
                    p.emit("dve", lambda e: e.reciprocal(rrow.ap[0:1, :], rrow.ap[0:1, :]),
                           ins=[rrow], outs=[rrow])
                    bcast(rrow, 0, TD, 64, psB64)
                    p.emit("dve", lambda e: e.tensor_copy(rb1.ap[:, :], psB64.ap[0:64, 0:TD]),
                           ins=[psB64], outs=[rb1])
                    p.emit("dve", lambda e: e.tensor_mul(oa.ap[:, :], oa.ap[:, :], rb1.ap[:, :]),
                           ins=[rb1, oa], outs=[oa])
                    p.emit("dve", lambda e, it=it, head=head: e.tensor_scalar(
                        out_n.ap[:, it * TD:(it + 1) * TD], oa.ap[:, :],
                        nw02.ap[0:64, head:head + 1], None, ALU.mult),
                        ins=[oa, nw02], outs=[out_n] if it == NT - 1 else ())
                for cc in range(16):
                    thi, c_lo = cc // 8, cc % 8
                    p.emit("pe", lambda e, cc=cc: e.transpose(
                        psT.ap[:, :], out_n.ap[:, cc * 128:(cc + 1) * 128],
                        ident_bf.ap[0:64, 0:64]),
                        ins=[out_n, ident_bf], outs=[psT])
                    p.emit("dve", lambda e, c_lo=c_lo, thi=thi, head=head: e.tensor_copy(
                        OT2v[:, c_lo, :, head * 2 + thi], psT.ap[:, :]),
                        ins=[psT], outs=[OT2] if cc == 15 else ())
        p.flush("b2")
        st2.close()

        st3 = contextlib.ExitStack()
        wo = load_w_bf("wo", dram["L0_attn_wo"], D, D, st3)
        wps = [psb(f"b3ps{i}", [128, TD], F32, st3) for i in range(2)]
        stile = sb("stile", [128, TD], F32, st3)
        for ec in range(NDC):
            for it in range(NT):
                pp = wps[(ec * NT + it) % 2]
                ptok = None
                for cl in range(NDC):
                    ptok = p.emit("pe", lambda e, ec=ec, it=it, cl=cl, pp=pp: e.matmul(
                        pp.ap[:, :],
                        wo.ap[:, cl * D + ec * 128:cl * D + ec * 128 + 128],
                        OT2.ap[:, cl * T + it * TD:cl * T + (it + 1) * TD],
                        start=(cl == 0), stop=(cl == NDC - 1)),
                        ins=[wo, OT2], outs=[pp] if cl == 0 else ())
                p.setw(pp, ptok)
                p.emit("dve", lambda e, pp=pp: e.tensor_copy(stile.ap[:, :], pp.ap[:, :]),
                       ins=[pp], outs=[stile])
                p.dma("sp", dram["s0"][ec * 128:(ec + 1) * 128, it * TD:(it + 1) * TD],
                      stile, ins=[stile])
        p.flush("b3")
        st3.close()
        stackB.close()

    phase_l0_attn()
    if upto <= 1:
        return nc, dram

    stackL0 = contextlib.ExitStack()

    def phase_l0_ffn_attn():
        stackC = contextlib.ExitStack()
        stackQKV = contextlib.ExitStack()
        stackU = contextlib.ExitStack()
        qT = sb("qT2", [128, NDC * T], BF16, stackQKV, side="right")
        kT = sb("kT2", [128, 2 * T], BF16, stackQKV, side="right")
        vT = sb("vT2", [128, 2 * T], BF16, stackQKV, side="right")

        st0 = contextlib.ExitStack()
        u_bf = sb("u_bf", [128, NDC * T], BF16, stackU)
        g1col = sb("g1col", [128, 8], F32, st0)
        stat = psb("c0stat", [1, T], F32, st0)
        rrow = sb("c0r", [1, T], F32, st0)
        bps = psb("c0b", [128, TD], F32, st0)
        rB = sb("c0rB", [128, T], F32, st0)
        sq = sb("c0sq", [128, TD], F32, st0)
        tmpc = sb("c0tmp", [128, T], F32, st0)
        load_col_vec(g1col, dram["L0_ffn_g1"], 8, bps)
        swar = p.begin_fill(stat)
        stok = None
        for c in range(NDC):
            for tt in range(NT):
                p.emit("act", lambda e, c=c, tt=tt: e.activation(
                    sq.ap[:, :], xn_bf.ap[:, c * T + tt * TD:c * T + (tt + 1) * TD],
                    AF.Square), ins=[xn_bf], outs=[sq])
                stok = p.emit("pe", lambda e, c=c, tt=tt: e.matmul(
                    stat.ap[0:1, tt * TD:(tt + 1) * TD], ones_col.ap[:, 0:1],
                    sq.ap[:, :], start=(c == 0), stop=(c == NDC - 1)),
                    ins=[sq, ones_col], waits=swar if c == 0 and tt == 0 else ())
        p.setw(stat, stok)
        p.emit("dve", lambda e: e.tensor_scalar(rrow.ap[0:1, :], stat.ap[0:1, :],
                                                1.0 / D, 1e-6, ALU.mult, ALU.add),
               ins=[stat], outs=[rrow])
        p.emit("act", lambda e: e.activation(rrow.ap[0:1, :], rrow.ap[0:1, :], AF.Sqrt),
               ins=[rrow], outs=[rrow])
        p.emit("dve", lambda e: e.reciprocal(rrow.ap[0:1, :], rrow.ap[0:1, :]),
               ins=[rrow], outs=[rrow])
        rwar = p.begin_fill(rB)
        rl = None
        for tt in range(NT):
            bcast(rrow, tt * TD, TD, 128, bps)
            rl = p.emit("dve", lambda e, tt=tt: e.tensor_copy(
                rB.ap[:, tt * TD:(tt + 1) * TD], bps.ap[:, 0:TD]),
                ins=[bps], waits=rwar if tt == 0 else ())
        p.setw(rB, rl)
        uwar = p.begin_fill(u_bf)
        ul = None
        for c in range(NDC):
            p.emit("dve", lambda e, c=c: e.tensor_scalar(
                tmpc.ap[:, :], xn_bf.ap[:, c * T:(c + 1) * T],
                g1col.ap[:, c:c + 1], None, ALU.mult),
                ins=[xn_bf, g1col], outs=[tmpc])
            ul = p.emit("dve", lambda e, c=c: e.tensor_mul(
                u_bf.ap[:, c * T:(c + 1) * T], tmpc.ap[:, :], rB.ap[:, :]),
                ins=[tmpc, rB], waits=uwar if c == 0 else ())
        p.setw(u_bf, ul)
        p.flush("c0")
        st0.close()

        st1 = contextlib.ExitStack()
        wqb = load_w_bf("wqf", dram["L0_ffn_wq"], D, D, st1)
        wkb = load_w_bf("wkf", dram["L0_ffn_wk"], D, 256, st1)
        wvb = load_w_bf("wvf", dram["L0_ffn_wv"], D, 256, st1)
        mmps = [psb(f"c1ps{i}", [128, TD], F32, st1) for i in range(2)]
        jobs = [(wqb, D, qT, NDC), (wkb, 256, kT, 2), (wvb, 256, vT, 2)]
        for wb, wcols, dst, nfc in jobs:
            war = p.begin_fill(dst)
            lastw = None
            for fc in range(nfc):
                for tt in range(NT):
                    pp = mmps[(fc * NT + tt) % 2]
                    ptok = None
                    for c in range(NDC):
                        ptok = p.emit("pe", lambda e, wb=wb, wcols=wcols, fc=fc, tt=tt, c=c, pp=pp:
                                      e.matmul(pp.ap[:, :],
                                               wb.ap[:, c * wcols + fc * 128:c * wcols + fc * 128 + 128],
                                               u_bf.ap[:, c * T + tt * TD:c * T + (tt + 1) * TD],
                                               start=(c == 0), stop=(c == NDC - 1)),
                                      ins=[wb, u_bf], outs=[pp] if c == 0 else ())
                    p.setw(pp, ptok)
                    lastw = p.emit("dve", lambda e, fc=fc, tt=tt, dst=dst, pp=pp: e.tensor_copy(
                        dst.ap[:, fc * T + tt * TD:fc * T + (tt + 1) * TD], pp.ap[:, :]),
                        ins=[pp], waits=war if fc == 0 and tt == 0 else ())
            p.setw(dst, lastw)
        p.flush("c1")
        st1.close()
        stackU.close()

        st2 = contextlib.ExitStack()
        cosb = sb("cosb", [128, T], F32, st2)
        sinb = sb("sinb", [128, T], F32, st2)
        rt1 = sb("rt1", [128, T], F32, st2)
        rt2 = sb("rt2", [128, T], F32, st2)
        p.dma("sp", cosb, dram["hc_cos"])
        p.dma("sp", sinb, dram["hc_sin"])
        for dst, nfc in ((qT, NDC), (kT, 2)):
            for fc in range(nfc):
                o = fc * T
                for hb in (0, 64):
                    for blk, srcb in enumerate((hb + 32, hb)):
                        tb = hb + 32 * blk
                        p.emit("dve", lambda e, tb=tb, srcb=srcb, o=o, dst=dst:
                               e.tensor_mul(rt1.ap[tb:tb + 32, :],
                                            dst.ap[srcb:srcb + 32, o:o + T],
                                            sinb.ap[srcb:srcb + 32, :]),
                               ins=[dst, sinb], outs=[rt1] if tb == hb + 32 else [])
                        p.emit("dve", lambda e, tb=tb, o=o, dst=dst:
                               e.tensor_mul(rt2.ap[tb:tb + 32, :],
                                            dst.ap[tb:tb + 32, o:o + T],
                                            cosb.ap[tb:tb + 32, :]),
                               ins=[dst, cosb], outs=[rt2] if tb == hb + 32 else [])
                    p.emit("dve", lambda e, hb=hb, o=o, dst=dst: e.tensor_sub(
                        dst.ap[hb:hb + 32, o:o + T], rt2.ap[hb:hb + 32, :],
                        rt1.ap[hb:hb + 32, :]),
                        ins=[rt1, rt2], outs=[dst])
                    p.emit("dve", lambda e, hb=hb, o=o, dst=dst: e.tensor_add(
                        dst.ap[hb + 32:hb + 64, o:o + T], rt2.ap[hb + 32:hb + 64, :],
                        rt1.ap[hb + 32:hb + 64, :]),
                        ins=[rt1, rt2], outs=[dst])
        p.flush("c2")
        st2.close()

        st3 = contextlib.ExitStack()
        aoT = sb("aoT", [128, NDC * T], BF16, stackC)
        psS = psb("psSg", [128, 2 * TD], F32, st3)
        psPV = psb("psPVg", [65, TD], F32, st3)
        psBq = psb("psBq", [128, TD], F32, st3)
        psTt = psb("psTg", [128, 64], BF16, st3)
        Pt = sb("Ptg", [128, 2 * TD], BF16, st3)
        Sh3 = [Buf(psS.ap[:, 0:TD]), Buf(psS.ap[:, TD:2 * TD])]
        Ph3 = [Buf(Pt.ap[:, 0:TD]), Buf(Pt.ap[:, TD:2 * TD])]
        v_tok4 = sb("v_tok4", [128, 4 * 16 * 65], BF16, st3)
        sqs_ = sb("sqsg", [128, TD], F32, st3)
        s2 = sb("s2g", [2, TD], F32, st3)
        m2 = sb("m2g", [2, 1], F32, st3)
        qmax2 = sb("qmax2", [2, 1], F32, st3)
        kmaxA = sb("kmaxA", [2, 1], F32, st3)
        kmaxB = sb("kmaxB", [2, 1], F32, st3)
        negMb2 = sb("negMb2", [128, 2], F32, st3)
        nrow2 = sb("nrow2", [1, 2], F32, st3)
        drow = sb("drowg", [1, TD], F32, st3)
        rbg = sb("rbg", [64, TD], F32, st3)

        def head_norms2(srcb, off, dst):
            war = p.begin_fill(dst)
            lt = None
            for tt in range(NT):
                p.emit("act", lambda e, tt=tt, srcb=srcb, off=off: e.activation(
                    sqs_.ap[:, :], srcb.ap[:, off + tt * TD:off + (tt + 1) * TD],
                    AF.Square), ins=[srcb], outs=[sqs_])
                p.emit("pe", lambda e: e.matmul(
                    psBq.ap[0:2, :], consts["hc_bo2"].ap[:, :], sqs_.ap[:, :],
                    start=True, stop=True), ins=[sqs_, consts["hc_bo2"]], outs=[psBq])
                p.emit("dve", lambda e: e.tensor_copy(s2.ap[:, :], psBq.ap[0:2, :]),
                       ins=[psBq], outs=[s2])
                if tt == 0:
                    lt = p.emit("dve", lambda e, dst=dst: e.reduce_max(
                        dst.ap[:, 0:1], s2.ap[:, :], axis=AX.X), ins=[s2], waits=war)
                else:
                    p.emit("dve", lambda e: e.reduce_max(
                        m2.ap[:, 0:1], s2.ap[:, :], axis=AX.X), ins=[s2], outs=[m2])
                    lt = p.emit("dve", lambda e, dst=dst: e.tensor_max(
                        dst.ap[:, 0:1], dst.ap[:, 0:1], m2.ap[:, 0:1]),
                        ins=[m2], waits=[lt])
            p.setw(dst, lt)

        head_norms2(kT, 0, kmaxA)
        head_norms2(kT, T, kmaxB)
        vwar = p.begin_fill(v_tok4)
        p.emit("pool", lambda e: e.memset(v_tok4.ap[:, :], 1.0), waits=vwar)
        vlast = None
        for kv in range(4):
            kc, kb = kv // 2, 64 * (kv % 2)
            for cc in range(16):
                p.emit("pe", lambda e, kc=kc, kb=kb, cc=cc: e.transpose(
                    psTt.ap[:, :], vT.ap[kb:kb + 64, kc * T + cc * 128:kc * T + (cc + 1) * 128],
                    ident_bf.ap[kb:kb + 64, kb:kb + 64]),
                    ins=[vT, ident_bf], outs=[psTt])
                vlast = p.emit("dve", lambda e, kv=kv, cc=cc: e.tensor_copy(
                    v_tok4.ap[:, kv * 1040 + cc * 65:kv * 1040 + cc * 65 + 64],
                    psTt.ap[:, :]), ins=[psTt])
        p.setw(v_tok4, vlast)

        for qc in range(NDC):
            head_norms2(qT, qc * T, qmax2)
            kmx = kmaxA if qc % 2 == 0 else kmaxB
            p.emit("dve", lambda e, kmx=kmx: e.tensor_mul(qmax2.ap[:, :], qmax2.ap[:, :],
                                                          kmx.ap[:, :]),
                   ins=[kmx, qmax2], outs=[qmax2])
            p.emit("act", lambda e: e.activation(qmax2.ap[:, :], qmax2.ap[:, :], AF.Sqrt),
                   ins=[qmax2], outs=[qmax2])
            p.emit("dve", lambda e: e.tensor_scalar(qmax2.ap[:, :], qmax2.ap[:, :],
                                                    -0.125, None, ALU.mult),
                   ins=[qmax2], outs=[qmax2])
            p.emit("pe", lambda e: e.transpose(psBq.ap[0:1, 0:2], qmax2.ap[:, 0:1],
                                               ident_f.ap[0:2, 0:2]),
                   ins=[qmax2, ident_f], outs=[psBq])
            p.emit("dve", lambda e: e.tensor_copy(nrow2.ap[0:1, :], psBq.ap[0:1, 0:2]),
                   ins=[psBq], outs=[nrow2])
            bcast(nrow2, 0, 2, 128, psBq)
            p.emit("dve", lambda e: e.tensor_copy(negMb2.ap[:, :], psBq.ap[:, 0:2]),
                   ins=[psBq], outs=[negMb2])
            for h2 in range(2):
                hb = 64 * h2
                head = qc * 2 + h2
                kv = head % 4
                kc, kb = kv // 2, 64 * (kv % 2)
                for it in range(NT):
                    njc = 4 * it + 4
                    pvwar = p.begin_fill(psPV)
                    pvtok = None

                    def c3_scores(jc, hb=hb, qc=qc, h2=h2, it=it, kb=kb, kc=kc):
                        jj = jc % 2
                        p.emit("pe", lambda e, jc=jc, jj=jj: e.matmul(
                            Sh3[jj].ap,
                            kT.ap[kb:kb + 64, kc * T + jc * 128:kc * T + (jc + 1) * 128],
                            qT.ap[hb:hb + 64, qc * T + it * TD:qc * T + (it + 1) * TD],
                            start=True, stop=True, tile_position=(kb, 0)),
                            ins=[kT, qT], outs=[Sh3[jj]])
                        p.emit("act", lambda e, jj=jj: e.activation(
                            Ph3[jj].ap, Sh3[jj].ap, AF.Exp,
                            bias=negMb2.ap[:, h2:h2 + 1], scale=0.125),
                            ins=[Sh3[jj], negMb2], outs=[Ph3[jj]])
                        if jc >= 4 * it:
                            p.emit("pool", lambda e, jc=jc, jj=jj: e.affine_select(
                                Ph3[jj].ap, Ph3[jj].ap,
                                [[1, TD]], compare_op=ALU.is_ge, fill=0.0,
                                base=it * TD - jc * 128, channel_multiplier=-1),
                                ins=[Ph3[jj]], outs=[Ph3[jj]])

                    def c3_pv(jc, kv=kv, njc=njc):
                        nonlocal pvtok
                        jj = jc % 2
                        pvtok = p.emit("pe", lambda e, jc=jc, jj=jj: e.matmul(
                            psPV.ap[:, :],
                            v_tok4.ap[:, kv * 1040 + jc * 65:kv * 1040 + (jc + 1) * 65],
                            Ph3[jj].ap, start=(jc == 0), stop=(jc == njc - 1)),
                            ins=[v_tok4, Ph3[jj]],
                            waits=pvwar if jc == 0 else ())

                    for jc in range(njc):
                        if jc >= 2:
                            c3_pv(jc - 2)
                        c3_scores(jc)
                    c3_pv(njc - 2)
                    c3_pv(njc - 1)
                    p.setw(psPV, pvtok)
                    p.emit("dve", lambda e: e.tensor_copy(drow.ap[0:1, :], psPV.ap[64:65, :]),
                           ins=[psPV], outs=[drow])
                    p.emit("dve", lambda e: e.reciprocal(drow.ap[0:1, :], drow.ap[0:1, :]),
                           ins=[drow], outs=[drow])
                    bcast(drow, 0, TD, 64, psBq)
                    p.emit("dve", lambda e: e.tensor_copy(rbg.ap[:, :], psBq.ap[0:64, 0:TD]),
                           ins=[psBq], outs=[rbg])
                    p.emit("dve", lambda e, hb=hb, qc=qc, it=it: e.tensor_mul(
                        aoT.ap[hb:hb + 64, qc * T + it * TD:qc * T + (it + 1) * TD],
                        psPV.ap[0:64, :], rbg.ap[:, :]),
                        ins=[psPV, rbg], outs=[aoT])
        p.flush("c3")
        st3.close()
        stackQKV.close()

        st4 = contextlib.ExitStack()
        wo_f = load_w_bf("wof", dram["L0_ffn_wo"], D, D, st4)
        wps = [psb(f"c4ps{i}", [128, TD], F32, st4) for i in range(2)]
        xnt = sb("c4xn", [128, TD], F32, st4)
        htile = sb("c4h", [128, TD], F32, st4)
        for ec in range(NDC):
            for it in range(NT):
                pp = wps[(ec * NT + it) % 2]
                ptok = None
                for cl in range(NDC):
                    ptok = p.emit("pe", lambda e, ec=ec, it=it, cl=cl, pp=pp: e.matmul(
                        pp.ap[:, :],
                        wo_f.ap[:, cl * D + ec * 128:cl * D + ec * 128 + 128],
                        aoT.ap[:, cl * T + it * TD:cl * T + (it + 1) * TD],
                        start=(cl == 0), stop=(cl == NDC - 1)),
                        ins=[wo_f, aoT], outs=[pp] if cl == 0 else ())
                p.setw(pp, ptok)
                p.dma("sp", xnt, dram["xn_f32"][ec * 128:(ec + 1) * 128,
                                                it * TD:(it + 1) * TD])
                p.emit("dve", lambda e, pp=pp: e.tensor_add(htile.ap[:, :], pp.ap[:, :],
                                                            xnt.ap[:, :]),
                       ins=[pp, xnt], outs=[htile])
                p.emit("dve", lambda e, ec=ec, it=it: e.tensor_copy(
                    st2_bf.ap[:, ec * T + it * TD:ec * T + (it + 1) * TD], htile.ap[:, :]),
                    ins=[htile], outs=[st2_bf] if ec == NDC - 1 and it == NT - 1 else ())
                p.dma("sp", dram["h1_f32"][ec * 128:(ec + 1) * 128, it * TD:(it + 1) * TD],
                      htile, ins=[htile])
        p.flush("c4")
        st4.close()
        stackC.close()

    stackH1 = contextlib.ExitStack()
    st2_bf = sb("st2_bf", [128, NDC * T], BF16, stackH1)
    phase_l0_ffn_attn()
    if upto <= 2:
        return nc, dram

    z_bf = sb("z_bf", [128, NDC * T], BF16, stackL0, side="right")

    def rmsnorm_into(src_bf, g_dram, dst_bf, stack, side="left", router=None):
        g1col = sb("rng", [128, 8], F32, stack, side=side)
        stat = psb("rnstat", [1, 2 * TD], F32, stack)
        rrow = sb("rnr", [1, T], F32, stack, side=side)
        bps = psb("rnb", [128, TD], F32, stack)
        rB = sb("rnrB", [128, T], F32, stack, side=side)
        sq = sb("rnsq", [128, TD], F32, stack, side=side)
        tmpc = sb("rntmp", [128, T], F32, stack, side=side)
        gwar = p.begin_fill(g1col)
        gl = None
        for c in range(NDC):
            gl = p.dma("sp", g1col.ap[:, c:c + 1],
                       g_dram[c * 128:(c + 1) * 128].rearrange("(d o) -> d o", o=1),
                       waits=gwar if c == 0 else (), key=g1col)
        p.setw(g1col, gl)
        for half in range(2):
            swar = p.begin_fill(stat)
            stok = None
            for c in range(NDC):
                for t2 in range(2):
                    tt = half * 2 + t2
                    p.emit("act", lambda e, c=c, tt=tt: e.activation(
                        sq.ap[:, :], src_bf.ap[:, c * T + tt * TD:c * T + (tt + 1) * TD],
                        AF.Square), ins=[src_bf], outs=[sq])
                    stok = p.emit("pe", lambda e, c=c, t2=t2: e.matmul(
                        stat.ap[0:1, t2 * TD:(t2 + 1) * TD], ones_col.ap[:, 0:1],
                        sq.ap[:, :], start=(c == 0), stop=(c == NDC - 1)),
                        ins=[sq, ones_col], waits=swar if c == 0 and t2 == 0 else ())
            p.setw(stat, stok)
            p.emit("dve", lambda e, half=half: e.tensor_scalar(
                rrow.ap[0:1, half * 2 * TD:(half + 1) * 2 * TD], stat.ap[0:1, :],
                1.0 / D, 1e-6, ALU.mult, ALU.add),
                ins=[stat], outs=[rrow] if half == 1 else [])
        p.emit("act", lambda e: e.activation(rrow.ap[0:1, :], rrow.ap[0:1, :], AF.Sqrt),
               ins=[rrow], outs=[rrow])
        p.emit("dve", lambda e: e.reciprocal(rrow.ap[0:1, :], rrow.ap[0:1, :]),
               ins=[rrow], outs=[rrow])
        rwar = p.begin_fill(rB)
        rl = None
        for tt in range(NT):
            bcast(rrow, tt * TD, TD, 128, bps)
            rl = p.emit("dve", lambda e, tt=tt: e.tensor_copy(
                rB.ap[:, tt * TD:(tt + 1) * TD], bps.ap[:, 0:TD]),
                ins=[bps], waits=rwar if tt == 0 else ())
        p.setw(rB, rl)
        dwar = p.begin_fill(dst_bf)
        dl = None
        if router is not None:
            rwf, psL8T = router
            lwar = p.begin_fill(psL8T)
        for c in range(NDC):
            p.emit("dve", lambda e, c=c: e.tensor_scalar(
                tmpc.ap[:, :], src_bf.ap[:, c * T:(c + 1) * T],
                g1col.ap[:, c:c + 1], None, ALU.mult),
                ins=[src_bf, g1col], outs=[tmpc])
            p.emit("dve", lambda e: e.tensor_mul(tmpc.ap[:, :], tmpc.ap[:, :], rB.ap[:, :]),
                   ins=[rB], outs=[tmpc])
            if router is not None:
                ltok = None
                for tt in range(NT):
                    ltok = p.emit("pe", lambda e, c=c, tt=tt: e.matmul(
                        psL8T.ap[:, tt * TD:(tt + 1) * TD], rwf.ap[:, c * 8:(c + 1) * 8],
                        tmpc.ap[:, tt * TD:(tt + 1) * TD],
                        start=(c == 0), stop=(c == NDC - 1)),
                        ins=[tmpc, rwf], waits=lwar if c == 0 and tt == 0 else ())
                if c == NDC - 1:
                    p.setw(psL8T, ltok)
            dl = p.emit("dve", lambda e, c=c: e.tensor_copy(
                dst_bf.ap[:, c * T:(c + 1) * T], tmpc.ap[:, :]),
                ins=[tmpc], waits=dwar if c == 0 else ())
        p.setw(dst_bf, dl)

    def phase_l0_router():
        stD = contextlib.ExitStack()
        rwf = sb("rwf", [128, 64], F32, stD)
        psL8T = psb("psL8T", [8, T], F32, stD)
        rwar = p.begin_fill(rwf)
        rl = None
        for c in range(NDC):
            rl = p.dma("sp", rwf.ap[:, c * 8:(c + 1) * 8],
                       dram["L0_ffn_router_w"][c * 128:(c + 1) * 128, :],
                       waits=rwar if c == 0 else (), key=rwf)
        p.setw(rwf, rl)
        rstack = contextlib.ExitStack()
        rmsnorm_into(st2_bf, dram["L0_ffn_g2"], z_bf, rstack, side="right",
                     router=(rwf, psL8T))
        p.flush("d0")
        rstack.close()
        rbrow = sb("rbrow", [1, 8], F32, stD)
        rbcol = sb("rbcol", [8, 1], F32, stD)
        R4 = sb("R4", [4, T], F32, stD)
        psTk = psb("psTk", [128, 8], F32, stD)
        psQT = psb("psQT", [4, 128], F32, stD)
        lgsb = sb("lgsb", [8, TD], F32, stD)
        lgt = sb("lgt", [128, 8], F32, stD)
        m1 = sb("m1c", [128, 1], F32, stD)
        nm1 = sb("nm1c", [128, 1], F32, stD)
        m2c = sb("m2c", [128, 1], F32, stD)
        e8 = sb("e8c", [128, 8], F32, stD)
        cand = sb("candc", [128, 8], F32, stD)
        lg2 = sb("lg2c", [128, 8], F32, stD)
        Qc = sb("Qc", [128, 4], F32, stD)
        dx = sb("dxc", [128, 1], F32, stD)
        p.dma("sp", rbrow, dram["L0_ffn_router_b"])
        p.emit("pe", lambda e: e.transpose(psTk.ap[0:8, 0:1], rbrow.ap[0:1, :],
                                           ident_f.ap[0:1, 0:1]),
               ins=[rbrow, ident_f], outs=[psTk])
        p.emit("dve", lambda e: e.tensor_copy(rbcol.ap[:, :], psTk.ap[0:8, 0:1]),
               ins=[psTk], outs=[rbcol])
        R4war = p.begin_fill(R4)
        R4tok = None
        for tt in range(NT):
            p.emit("dve", lambda e, tt=tt: e.tensor_scalar(
                lgsb.ap[:, :], psL8T.ap[:, tt * TD:(tt + 1) * TD],
                rbcol.ap[0:8, 0:1], None, ALU.add),
                ins=[psL8T, rbcol], outs=[lgsb])
            for s in range(4):
                cc = tt * 4 + s
                p.emit("pe", lambda e, s=s: e.transpose(
                    psTk.ap[:, :], lgsb.ap[:, s * 128:(s + 1) * 128],
                    ident_f.ap[0:8, 0:8]), ins=[lgsb, ident_f], outs=[psTk])
                p.emit("dve", lambda e: e.tensor_copy(lgt.ap[:, :], psTk.ap[:, :]),
                       ins=[psTk], outs=[lgt])
                p.emit("dve", lambda e: e.reduce_max(m1.ap[:, :], lgt.ap[:, :], axis=AX.X),
                       ins=[lgt], outs=[m1])
                p.emit("dve", lambda e: e.tensor_scalar(e8.ap[:, :], lgt.ap[:, :],
                                                        m1.ap[:, 0:1], None, ALU.is_equal),
                       ins=[lgt, m1], outs=[e8])
                p.emit("dve", lambda e: e.tensor_scalar(e8.ap[:, :], e8.ap[:, :],
                                                        -1000.0, 1000.0, ALU.mult, ALU.add),
                       ins=[e8], outs=[e8])
                p.emit("dve", lambda e: e.tensor_add(cand.ap[:, :], e8.ap[:, :],
                                                     iota8_row128.ap[:, :]),
                       ins=[e8, iota8_row128], outs=[cand])
                p.emit("dve", lambda e: e.tensor_reduce(Qc.ap[:, 0:1], cand.ap[:, :],
                                                        axis=AX.X, op=ALU.min),
                       ins=[cand], outs=[Qc])
                p.emit("dve", lambda e: e.tensor_scalar(e8.ap[:, :], iota8_row128.ap[:, :],
                                                        Qc.ap[:, 0:1], None, ALU.is_equal),
                       ins=[iota8_row128, Qc, e8], outs=[e8])
                p.emit("dve", lambda e: e.scalar_tensor_tensor(
                    lg2.ap[:, :], e8.ap[:, :], -1e9, lgt.ap[:, :], ALU.mult, ALU.add),
                    ins=[e8, lgt], outs=[lg2])
                p.emit("dve", lambda e: e.reduce_max(m2c.ap[:, :], lg2.ap[:, :], axis=AX.X),
                       ins=[lg2], outs=[m2c])
                p.emit("dve", lambda e: e.tensor_scalar(e8.ap[:, :], lg2.ap[:, :],
                                                        m2c.ap[:, 0:1], None, ALU.is_equal),
                       ins=[lg2, m2c, e8], outs=[e8])
                p.emit("dve", lambda e: e.tensor_scalar(e8.ap[:, :], e8.ap[:, :],
                                                        -1000.0, 1000.0, ALU.mult, ALU.add),
                       ins=[e8], outs=[e8])
                p.emit("dve", lambda e: e.tensor_add(cand.ap[:, :], e8.ap[:, :],
                                                     iota8_row128.ap[:, :]),
                       ins=[e8, iota8_row128], outs=[cand])
                p.emit("dve", lambda e: e.tensor_reduce(Qc.ap[:, 1:2], cand.ap[:, :],
                                                        axis=AX.X, op=ALU.min),
                       ins=[cand], outs=[Qc])
                p.emit("dve", lambda e: e.tensor_scalar(nm1.ap[:, :], m1.ap[:, :],
                                                        -1.0, None, ALU.mult),
                       ins=[m1], outs=[nm1])
                p.emit("act", lambda e: e.activation(dx.ap[:, :], m2c.ap[:, :], AF.Exp,
                                                     bias=nm1.ap[:, 0:1], scale=1.0),
                       ins=[m2c, nm1], outs=[dx])
                p.emit("dve", lambda e: e.tensor_scalar(Qc.ap[:, 2:3], dx.ap[:, :],
                                                        1.0, None, ALU.add),
                       ins=[dx, Qc], outs=[Qc])
                p.emit("dve", lambda e: e.reciprocal(Qc.ap[:, 2:3], Qc.ap[:, 2:3]),
                       ins=[Qc], outs=[Qc])
                p.emit("dve", lambda e: e.tensor_mul(Qc.ap[:, 3:4], dx.ap[:, 0:1],
                                                     Qc.ap[:, 2:3]),
                       ins=[dx, Qc], outs=[Qc])
                p.emit("pe", lambda e: e.transpose(psQT.ap[:, :], Qc.ap[:, :],
                                                   ident_f.ap[:, :]),
                       ins=[Qc, ident_f], outs=[psQT])
                R4tok = p.emit("dve", lambda e, cc=cc: e.tensor_copy(
                    R4.ap[:, cc * 128:(cc + 1) * 128], psQT.ap[:, :]),
                    ins=[psQT], waits=R4war if cc == 0 else ())
        p.setw(R4, R4tok)
        ohT = [sb(f"ohT{k}", [8, T], F32, stD) for k in range(2)]
        pos0 = sb("pos0", [8, T], F32, stD)
        p8 = sb("p8", [8, T], F32, stD)
        cnt2 = sb("cnt2", [8, 2], F32, stD)
        agsb = sb("agsb", [8, 16], F32, stD)
        off2 = sb("off2", [8, 2], F32, stD)
        psE = psb("psE", [8, TD], F32, stD)
        psR1 = psb("psR1", [1, TD], F32, stD)
        trow_ = [sb(f"trow{k}", [1, T], F32, stD) for k in range(2)]
        arow_t = sb("arowt", [1, T], F32, stD)
        sxr = sb("sxr", [1, TD], F32, stD)
        m8 = sb("m8", [8, T], F32, stD)
        for k in range(2):
            war = p.begin_fill(ohT[k])
            lt = None
            for tt in range(NT):
                p.emit("pe", lambda e, k=k, tt=tt: e.matmul(
                    psE.ap[:, :], consts["hc_sel8"].ap[0:4, k * 128:k * 128 + 8],
                    R4.ap[:, tt * TD:(tt + 1) * TD], start=True, stop=True),
                    ins=[R4, consts["hc_sel8"]], outs=[psE])
                lt = p.emit("dve", lambda e, k=k, tt=tt: e.tensor_scalar(
                    ohT[k].ap[:, tt * TD:(tt + 1) * TD], psE.ap[:, :],
                    iota8_col.ap[:, 0:1], None, ALU.is_equal),
                    ins=[psE, iota8_col], waits=war if tt == 0 else ())
            p.setw(ohT[k], lt)
            p.emit("dve", lambda e, k=k: e.reduce_sum(cnt2.ap[:, k:k + 1],
                                                      ohT[k].ap[:, :], axis=AX.X),
                   ins=[ohT[k]], outs=[cnt2] if k == 1 else [])
        cdma = p.dma("sp", dram["cnt0"], cnt2, ins=[cnt2])
        cctok = p.emit("pool", lambda e: e.collective_compute(
            "AllGather", ALU.bypass, replica_groups=[list(range(8))],
            ins=[dram["cnt0"].opt()], outs=[dram["ag0"].opt()]),
            waits=[cdma], collective=True)
        agld = p.dma("sp", agsb.ap.rearrange("e (c k) -> e c k", c=8),
                     dram["ag0"].rearrange("(c e) k -> e c k", c=8),
                     waits=[cctok], key=agsb)
        p.setw(agsb, agld)
        p.emit("dve", lambda e: e.tensor_mul(agsb.ap[:, :], agsb.ap[:, :],
                                             consts["hc_prev2"].ap[:, :]),
               ins=[consts["hc_prev2"], agsb], outs=[agsb])
        ot = p.emit("dve", lambda e: e.tensor_copy(off2.ap[:, :], agsb.ap[:, 0:2]),
                    ins=[agsb], outs=[off2])
        for c in range(1, 8):
            ot = p.emit("dve", lambda e, c=c: e.tensor_add(
                off2.ap[:, :], off2.ap[:, :], agsb.ap[:, 2 * c:2 * c + 2]),
                ins=[agsb], outs=[off2], waits=[ot])
        for k in range(2):
            pdst = pos0 if k == 0 else p8
            p.emit("dve", lambda e, k=k, pdst=pdst: e.tensor_tensor_scan(
                pdst.ap[:, :], ohT[k].ap[:, :], ohT[k].ap[:, :], 0.0,
                ALU.add, ALU.bypass), ins=[ohT[k]], outs=[pdst])
            p.emit("dve", lambda e, k=k, pdst=pdst: e.tensor_scalar(
                pdst.ap[:, :], pdst.ap[:, :], off2.ap[:, k:k + 1], None, ALU.add),
                ins=[off2, pdst], outs=[pdst])
        p.emit("dve", lambda e: e.tensor_add(p8.ap[:, :], p8.ap[:, :], pos0.ap[:, :]),
               ins=[pos0, p8], outs=[p8])
        for k in range(2):
            pdst = pos0 if k == 0 else p8
            p.emit("dve", lambda e, pdst=pdst: e.tensor_scalar(
                pdst.ap[:, :], pdst.ap[:, :], 2048.0, None, ALU.is_lt),
                ins=[pdst], outs=[pdst])
            p.emit("dve", lambda e, k=k, pdst=pdst: e.tensor_mul(
                ohT[k].ap[:, :], ohT[k].ap[:, :], pdst.ap[:, :]),
                ins=[pdst], outs=[ohT[k]])
            twar = p.begin_fill(trow_[k])
            ttok = None
            for tt in range(NT):
                p.emit("pe", lambda e, k=k, tt=tt: e.matmul(
                    psR1.ap[0:1, :], iota8_col.ap[:, 0:1],
                    ohT[k].ap[:, tt * TD:(tt + 1) * TD], start=True, stop=True),
                    ins=[ohT[k], iota8_col], outs=[psR1])
                ttok = p.emit("dve", lambda e, k=k, tt=tt: e.tensor_copy(
                    trow_[k].ap[0:1, tt * TD:(tt + 1) * TD], psR1.ap[0:1, :]),
                    ins=[psR1], waits=twar if tt == 0 else ())
            p.setw(trow_[k], ttok)
        p.emit("dve", lambda e: e.tensor_max(arow_t.ap[0:1, :], trow_[0].ap[0:1, :],
                                             trow_[1].ap[0:1, :]),
               ins=[trow_[0], trow_[1]], outs=[arow_t])
        for k in range(2):
            twar = p.begin_fill(trow_[k])
            ttok = None
            for tt in range(NT):
                p.emit("pe", lambda e, k=k, tt=tt: e.matmul(
                    psR1.ap[0:1, :], ones_col.ap[0:8, 0:1],
                    ohT[k].ap[:, tt * TD:(tt + 1) * TD], start=True, stop=True),
                    ins=[ohT[k], ones_col], outs=[psR1])
                p.emit("pe", lambda e, k=k, tt=tt: e.matmul(
                    psE.ap[0:1, :], consts["hc_sel8"].ap[0:4, (2 + k) * 128:(2 + k) * 128 + 1],
                    R4.ap[:, tt * TD:(tt + 1) * TD], start=True, stop=True),
                    ins=[R4, consts["hc_sel8"]], outs=[psE])
                p.emit("dve", lambda e: e.tensor_copy(sxr.ap[0:1, :], psE.ap[0:1, :]),
                       ins=[psE], outs=[sxr])
                ttok = p.emit("dve", lambda e, k=k, tt=tt: e.tensor_mul(
                    trow_[k].ap[0:1, tt * TD:(tt + 1) * TD], psR1.ap[0:1, :],
                    sxr.ap[0:1, :]),
                    ins=[psR1, sxr], waits=twar if tt == 0 else ())
            p.setw(trow_[k], ttok)
        p.emit("dve", lambda e: e.tensor_add(brow.ap[0:1, :], trow_[0].ap[0:1, :],
                                             trow_[1].ap[0:1, :]),
               ins=[trow_[0], trow_[1]], outs=[brow])
        war = p.begin_fill(m8)
        lt = None
        for tt in range(NT):
            bcast(arow_t, tt * TD, TD, 8, psE)
            lt = p.emit("dve", lambda e, tt=tt: e.tensor_copy(
                m8.ap[:, tt * TD:(tt + 1) * TD], psE.ap[:, 0:TD]),
                ins=[psE], waits=war if tt == 0 else ())
        p.setw(m8, lt)
        war = p.begin_fill(p8)
        lt = None
        for tt in range(NT):
            bcast(brow, tt * TD, TD, 8, psE)
            lt = p.emit("dve", lambda e, tt=tt: e.tensor_copy(
                p8.ap[:, tt * TD:(tt + 1) * TD], psE.ap[:, 0:TD]),
                ins=[psE], waits=war if tt == 0 else ())
        p.setw(p8, lt)
        p.emit("dve", lambda e: e.tensor_scalar(m8.ap[:, :], m8.ap[:, :],
                                                iota8_col.ap[:, 0:1], None, ALU.is_equal),
               ins=[iota8_col, m8], outs=[m8])
        p.emit("dve", lambda e: e.tensor_mul(arow.ap[:, :], m8.ap[:, :], p8.ap[:, :]),
               ins=[m8, p8], outs=[arow])
        p.emit("dve", lambda e: e.tensor_scalar(brow.ap[0:1, :], brow.ap[0:1, :],
                                                -1.0, 1.0, ALU.mult, ALU.add),
               ins=[], outs=[brow])
        p.flush("d1")
        stD.close()

    phase_l0_router()
    if upto <= 3:
        return nc, dram

    def phase_l0_experts():
        stackH1.close()
        stE = contextlib.ExitStack()
        st0 = contextlib.ExitStack()
        h1t = [sb(f"eh1t{i}", [128, T], F32, st0) for i in range(2)]
        zf = [sb(f"ezf{i}", [128, T], F32, st0) for i in range(2)]
        wzB = sb("ewzB", [128, T], F32, st0)
        itile = [sb(f"einit{i}", [128, T], F32, st0) for i in range(2)]
        psI = psb("epsI", [128, TD], F32, st0)
        init_toks = []
        wwar = p.begin_fill(wzB)
        wl = None
        for it in range(NT):
            p.emit("pe", lambda e, it=it: e.matmul(
                psI.ap[:, :], ones_row.ap[0:1, :],
                brow.ap[0:1, it * TD:(it + 1) * TD], start=True, stop=True),
                ins=[brow, ones_row], outs=[psI])
            wl = p.emit("dve", lambda e, it=it: e.tensor_copy(
                wzB.ap[:, it * TD:(it + 1) * TD], psI.ap[:, :]),
                ins=[psI], waits=wwar if it == 0 else ())
        p.setw(wzB, wl)
        for c in range(NDC):
            b = c % 2
            sl = slice(c * 128, (c + 1) * 128)
            p.dma("sp", h1t[b], dram["h1_f32"][sl, :])
            p.emit("dve", lambda e, c=c, b=b: e.tensor_copy(
                zf[b].ap[:, :], z_bf.ap[:, c * T:(c + 1) * T]),
                ins=[z_bf], outs=[zf[b]])
            p.emit("dve", lambda e, b=b: e.tensor_mul(zf[b].ap[:, :], zf[b].ap[:, :],
                                                      wzB.ap[:, :]),
                   ins=[wzB], outs=[zf[b]])
            p.emit("dve", lambda e, b=b: e.tensor_add(itile[b].ap[:, :], h1t[b].ap[:, :],
                                                      zf[b].ap[:, :]),
                   ins=[h1t[b], zf[b]], outs=[itile[b]])
            init_toks.append(p.dma("sp", dram["acc0"][sl, :], itile[b], ins=[itile[b]]))
        p.flush("e_init")
        st0.close()
        hbuf = sb("hbuf", [128, 16 * T], BF16, stE)
        hbufc = [Buf(hbuf.ap[:, hc * T:(hc + 1) * T]) for hc in range(16)]
        wc1 = [sb(f"wc1_{i}", [128, 8 * 128], BF16, stE) for i in range(2)]
        wc2 = [sb("wc2_0", [128, 8 * 128], BF16, stE)]
        wstg = [sb("wstg0", [128, 8 * 128], F32, stE)]
        aef = sb("aef", [128, TD], F32, stE)
        alB = sb("alB", [128, T], F32, stE)
        ctile = sb("ctile0", [128, 2 * TD], F32, stE)
        psA = [psb(f"epsA{i}", [128, TD], F32, stE) for i in range(2)]
        psB2 = [psb(f"epsB{i}", [128, TD], F32, stE) for i in range(2)]
        psY = [psb(f"epsY{i}", [128, TD], F32, stE) for i in range(2)]
        psAl = psb("epsAl", [128, TD], F32, stE)
        w3t = sb("w3t", [128, 16 * D], BF16, stE)
        prev_acc = init_toks[-1]
        for ex in range(8):
            w3war = p.begin_fill(w3t)
            w3l = p.dma("pool", w3t.ap.rearrange("p (c d) -> p c d", c=16),
                        dram["L0_ffn_w3"][ex].rearrange("(c p) d -> p c d", p=128),
                        waits=w3war, key=w3t)
            p.setw(w3t, w3l)
            awar = p.begin_fill(alB)
            al = None
            for it in range(NT):
                p.emit("pe", lambda e, ex=ex, it=it: e.matmul(
                    psAl.ap[:, :], consts["hc_sel8"].ap[:, ex * 128:(ex + 1) * 128],
                    arow.ap[:, it * TD:(it + 1) * TD], start=True, stop=True),
                    ins=[arow, consts["hc_sel8"]], outs=[psAl])
                al = p.emit("dve", lambda e, it=it: e.tensor_copy(
                    alB.ap[:, it * TD:(it + 1) * TD], psAl.ap[:, :]),
                    ins=[psAl], waits=awar if it == 0 else ())
            p.setw(alB, al)
            for hc in range(16):
                hcwar = p.begin_fill(hbufc[hc])
                b = hc % 2
                for wbuf, wname in ((wc1[b], "L0_ffn_w1"), (wc2[0], "L0_ffn_w2")):
                    hw_cast_load(wbuf, dram[wname][ex][:, hc * 128:(hc + 1) * 128],
                                 8, 128, wstg[0])
                for it in range(NT):
                    pb = (hc * NT + it) % 2
                    for psx, wbuf in ((psA[pb], wc1[b]), (psB2[pb], wc2[0])):
                        war = p.begin_fill(psx)
                        ptok = None
                        for c in range(NDC):
                            ptok = p.emit("pe", lambda e, c=c, it=it, psx=psx, wbuf=wbuf:
                                          e.matmul(
                                psx.ap[:, :], wbuf.ap[:, c * 128:(c + 1) * 128],
                                z_bf.ap[:, c * T + it * TD:c * T + (it + 1) * TD],
                                start=(c == 0), stop=(c == NDC - 1)),
                                ins=[wbuf, z_bf], waits=war if c == 0 else ())
                        p.setw(psx, ptok)
                    p.emit("dve", lambda e, pb=pb: e.tensor_copy(aef.ap[:, :], psA[pb].ap[:, :]),
                           ins=[psA[pb]], outs=[aef])
                    p.emit("dve", lambda e, pb=pb: e.tensor_mul(aef.ap[:, :], psB2[pb].ap[:, :],
                                                                aef.ap[:, :]),
                           ins=[psB2[pb]], outs=[aef])
                    hl = p.emit("act", lambda e, hc=hc, it=it: e.activation(
                        hbuf.ap[:, hc * T + it * TD:hc * T + (it + 1) * TD],
                        aef.ap[:, :], AF.Gelu),
                        ins=[aef], waits=hcwar if it == 0 else ())
                p.setw(hbufc[hc], hl)
            new_prev = prev_acc
            for dc in range(NDC):
                for half in range(2):
                    ct = Buf(ctile.ap[:, 0:2 * TD]) if False else ctile
                    cwar = p.begin_fill(ctile)
                    clast = None
                    for t2 in range(2):
                        it = half * 2 + t2
                        pb = (dc * NT + it) % 2
                        war = p.begin_fill(psY[pb])
                        ptok = None
                        for hc in range(16):
                            ptok = p.emit("pe", lambda e, hc=hc, dc=dc, it=it, pb=pb: e.matmul(
                                psY[pb].ap[:, :],
                                w3t.ap[:, hc * D + dc * 128:hc * D + dc * 128 + 128],
                                hbuf.ap[:, hc * T + it * TD:hc * T + (it + 1) * TD],
                                start=(hc == 0), stop=(hc == 15)),
                                ins=[w3t, hbufc[hc]], waits=war if hc == 0 else ())
                        p.setw(psY[pb], ptok)
                        clast = p.emit("dve", lambda e, it=it, t2=t2, pb=pb: e.tensor_mul(
                            ctile.ap[:, t2 * TD:(t2 + 1) * TD], psY[pb].ap[:, :],
                            alB.ap[:, it * TD:(it + 1) * TD]),
                            ins=[psY[pb], alB], waits=cwar if t2 == 0 else ())
                    p.setw(ctile, clast)
                    tok = p.dma("pool",
                                dram["acc0"][dc * 128:(dc + 1) * 128,
                                             half * 2 * TD:(half + 1) * 2 * TD],
                                ctile, ins=[ctile],
                                waits=[prev_acc] if dc == 0 and half == 0 else (),
                                accum_op=ALU.add)
                    new_prev = tok
            prev_acc = new_prev
        p.flush("e_experts")
        stE.close()

    phase_l0_experts()
    if upto <= 4:
        return nc, dram

    def phase_combine(x_dram, s_dram, acc_dram, out_dram, phname):
        stF = contextlib.ExitStack()
        if dbg and phname == "f0":
            p.dma("pool", dram["acc0b"], acc_dram)
        at = [sb(f"fat{i}", [128, T], F32, stF) for i in range(2)]
        gt = [sb(f"fgt{i}", [128, T], F32, stF) for i in range(2)]
        st_ = [sb(f"fst{i}", [128, T], F32, stF) for i in range(2)]
        xt = [sb(f"fxt{i}", [128, T], F32, stF) for i in range(2)]
        ot = [sb(f"fot{i}", [128, T], F32, stF) for i in range(2)]
        for c in range(NDC):
            b = c % 2
            sl = slice(c * 128, (c + 1) * 128)
            p.dma("sp", at[b], acc_dram[sl, :])
            p.dma("sp", st_[b], s_dram[sl, :])
            p.dma("act", xt[b], x_dram[sl, :])
            p.emit("act", lambda e, b=b: e.activation(gt[b].ap[:, :], at[b].ap[:, :],
                                                      AF.Silu),
                   ins=[at[b]], outs=[gt[b]])
            p.emit("dve", lambda e, b=b: e.tensor_mul(gt[b].ap[:, :], gt[b].ap[:, :],
                                                      st_[b].ap[:, :]),
                   ins=[st_[b]], outs=[gt[b]])
            p.emit("dve", lambda e, b=b: e.tensor_add(ot[b].ap[:, :], gt[b].ap[:, :],
                                                      xt[b].ap[:, :]),
                   ins=[gt[b], xt[b]], outs=[ot[b]])
            p.dma("sp", out_dram[sl, :], ot[b], ins=[ot[b]])
        p.flush(phname)
        stF.close()

    phase_combine(dram["x"], dram["s0"], dram["acc0"], dram["x1"], "f0")
    if upto <= 5:
        return nc, dram

    stackL0.close()
    phase_ln(dram["x1"], dram["xn1_f32"], xn_bf, 1e-5, "ln1")
    if upto <= 6:
        return nc, dram

    def phase_l1_attn():
        stackI = contextlib.ExitStack()
        stQKV = contextlib.ExitStack()
        qT = sb("qT3", [128, NDC * T], BF16, stQKV, side="right")
        kT = sb("kT3", [128, NDC * T], BF16, stQKV, side="right")
        vT = sb("vT3", [128, NDC * T], BF16, stQKV, side="right")
        st1 = contextlib.ExitStack()
        wq = load_w_bf("wqkv1", dram["L1_attn_w_qkv"], D, 3 * D, st1)
        mmps = [psb(f"i1ps{i}", [128, TD], F32, st1) for i in range(2)]
        for fi, dst in ((0, qT), (1, kT), (2, vT)):
            war = p.begin_fill(dst)
            lastw = None
            for fc in range(NDC):
                for tt in range(NT):
                    pp = mmps[(fc * NT + tt) % 2]
                    ptok = None
                    for c in range(NDC):
                        ptok = p.emit("pe", lambda e, fi=fi, fc=fc, tt=tt, c=c, pp=pp: e.matmul(
                            pp.ap[:, :],
                            wq.ap[:, c * 3 * D + fi * D + fc * 128:
                                  c * 3 * D + fi * D + fc * 128 + 128],
                            xn_bf.ap[:, c * T + tt * TD:c * T + (tt + 1) * TD],
                            start=(c == 0), stop=(c == NDC - 1)),
                            ins=[wq, xn_bf], outs=[pp] if c == 0 else ())
                    p.setw(pp, ptok)
                    lastw = p.emit("dve", lambda e, fc=fc, tt=tt, dst=dst, pp=pp: e.tensor_copy(
                        dst.ap[:, fc * T + tt * TD:fc * T + (tt + 1) * TD], pp.ap[:, :]),
                        ins=[pp], waits=war if fc == 0 and tt == 0 else ())
            p.setw(dst, lastw)
        p.flush("i1")
        st1.close()
        st2 = contextlib.ExitStack()
        a1T = sb("a1T", [128, NDC * T], BF16, stackI)
        psS = psb("ipsS", [128, 2 * TD], F32, st2)
        psPV = psb("ipsPV", [65, TD], F32, st2)
        psBq = psb("ipsB", [128, TD], F32, st2)
        psTt = psb("ipsT", [128, 64], BF16, st2)
        psVm = psb("ipsVm", [65, 1], F32, st2)
        Pt = sb("iPt", [128, 2 * TD], BF16, st2)
        msk = sb("imsk", [128, TD], BF16, st2)
        ShI = [Buf(psS.ap[:, 0:TD]), Buf(psS.ap[:, TD:2 * TD])]
        PhI = [Buf(Pt.ap[:, 0:TD]), Buf(Pt.ap[:, TD:2 * TD])]
        v_tok = sb("iv_tok", [128, 16 * 65], BF16, st2)
        sqs_ = sb("isqs", [128, TD], F32, st2)
        s2 = sb("is2", [2, TD], F32, st2)
        m2 = sb("im2", [2, 1], F32, st2)
        qmax2 = sb("iqmax", [2, 1], F32, st2)
        kmax2 = sb("ikmax", [2, 1], F32, st2)
        negMb2 = sb("inegM", [128, 2], F32, st2)
        tau2 = sb("itau", [128, 2], F32, st2)
        thrc = sb("ithr", [128, 1], F32, st2)
        thr1 = sb("ithr1", [1, 1], F32, st2)
        nrow2 = sb("inrow", [1, 2], F32, st2)
        drow = sb("idrow", [1, TD], F32, st2)
        indr = sb("iindr", [1, TD], F32, st2)
        rbg = sb("irbg", [64, TD], F32, st2)
        ibg = sb("iibg", [64, TD], F32, st2)
        vmc = sb("ivmc", [64, 1], F32, st2)

        p.dma("sp", thr1, dram["L1_attn_threshold"])
        p.emit("pe", lambda e: e.matmul(psBq.ap[:, 0:1], ones_row.ap[0:1, :],
                                        thr1.ap[0:1, 0:1], start=True, stop=True),
               ins=[thr1, ones_row], outs=[psBq])
        p.emit("dve", lambda e: e.tensor_copy(thrc.ap[:, :], psBq.ap[:, 0:1]),
               ins=[psBq], outs=[thrc])

        def head_norms3(srcb, off, dst):
            war = p.begin_fill(dst)
            lt = None
            for tt in range(NT):
                p.emit("act", lambda e, tt=tt, srcb=srcb, off=off: e.activation(
                    sqs_.ap[:, :], srcb.ap[:, off + tt * TD:off + (tt + 1) * TD],
                    AF.Square), ins=[srcb], outs=[sqs_])
                p.emit("pe", lambda e: e.matmul(
                    psBq.ap[0:2, :], consts["hc_bo2"].ap[:, :], sqs_.ap[:, :],
                    start=True, stop=True), ins=[sqs_, consts["hc_bo2"]], outs=[psBq])
                if tt == 0:
                    lt = p.emit("dve", lambda e, dst=dst: e.reduce_max(
                        dst.ap[:, 0:1], psBq.ap[0:2, :], axis=AX.X), ins=[psBq], waits=war)
                else:
                    p.emit("dve", lambda e: e.reduce_max(
                        m2.ap[:, 0:1], psBq.ap[0:2, :], axis=AX.X), ins=[psBq], outs=[m2])
                    lt = p.emit("dve", lambda e, dst=dst: e.tensor_max(
                        dst.ap[:, 0:1], dst.ap[:, 0:1], m2.ap[:, 0:1]),
                        ins=[m2], waits=[lt])
            p.setw(dst, lt)

        for qc in range(NDC):
            head_norms3(qT, qc * T, qmax2)
            head_norms3(kT, qc * T, kmax2)
            p.emit("dve", lambda e: e.tensor_mul(qmax2.ap[:, :], qmax2.ap[:, :],
                                                 kmax2.ap[:, :]),
                   ins=[kmax2, qmax2], outs=[qmax2])
            p.emit("act", lambda e: e.activation(qmax2.ap[:, :], qmax2.ap[:, :], AF.Sqrt),
                   ins=[qmax2], outs=[qmax2])
            p.emit("dve", lambda e: e.tensor_scalar(qmax2.ap[:, :], qmax2.ap[:, :],
                                                    -0.125, None, ALU.mult),
                   ins=[qmax2], outs=[qmax2])
            p.emit("pe", lambda e: e.transpose(psBq.ap[0:1, 0:2], qmax2.ap[:, 0:1],
                                               ident_f.ap[0:2, 0:2]),
                   ins=[qmax2, ident_f], outs=[psBq])
            p.emit("dve", lambda e: e.tensor_copy(nrow2.ap[0:1, :], psBq.ap[0:1, 0:2]),
                   ins=[psBq], outs=[nrow2])
            bcast(nrow2, 0, 2, 128, psBq)
            p.emit("dve", lambda e: e.tensor_copy(negMb2.ap[:, :], psBq.ap[:, 0:2]),
                   ins=[psBq], outs=[negMb2])
            p.emit("act", lambda e: e.activation(tau2.ap[:, :], negMb2.ap[:, :], AF.Exp,
                                                 bias=thrc.ap[:, 0:1], scale=1.0),
                   ins=[negMb2, thrc], outs=[tau2])
            for h2 in range(2):
                hb = 64 * h2
                head = qc * 2 + h2
                vwar = p.begin_fill(v_tok)
                p.emit("pool", lambda e: e.memset(v_tok.ap[:, :], 1.0), waits=vwar)
                vlast = None
                for cc in range(16):
                    p.emit("pe", lambda e, cc=cc, hb=hb, qc=qc: e.transpose(
                        psTt.ap[:, :], vT.ap[hb:hb + 64, qc * T + cc * 128:qc * T + (cc + 1) * 128],
                        ident_bf.ap[hb:hb + 64, hb:hb + 64]),
                        ins=[vT, ident_bf], outs=[psTt])
                    vlast = p.emit("dve", lambda e, cc=cc: e.tensor_copy(
                        v_tok.ap[:, cc * 65:cc * 65 + 64], psTt.ap[:, :]), ins=[psTt])
                p.setw(v_tok, vlast)
                vmwar = p.begin_fill(psVm)
                vmtok = None
                for cc in range(16):
                    vmtok = p.emit("pe", lambda e, cc=cc: e.matmul(
                        psVm.ap[:, :], v_tok.ap[:, cc * 65:(cc + 1) * 65],
                        ones_col_bf.ap[:, 0:1], start=(cc == 0), stop=(cc == 15)),
                        ins=[v_tok, ones_col_bf], waits=vmwar if cc == 0 else ())
                p.setw(psVm, vmtok)
                p.emit("dve", lambda e: e.tensor_scalar(vmc.ap[:, 0:1], psVm.ap[0:64, 0:1],
                                                        1.0 / T, None, ALU.mult),
                       ins=[psVm], outs=[vmc])
                for it in range(NT):
                    pvwar = p.begin_fill(psPV)
                    pvtok = None

                    def i2_scores(jc, hb=hb, qc=qc, h2=h2, it=it):
                        jj = jc % 2
                        p.emit("pe", lambda e, jc=jc, jj=jj: e.matmul(
                            ShI[jj].ap,
                            kT.ap[hb:hb + 64, qc * T + jc * 128:qc * T + (jc + 1) * 128],
                            qT.ap[hb:hb + 64, qc * T + it * TD:qc * T + (it + 1) * TD],
                            start=True, stop=True, tile_position=(hb, 0)),
                            ins=[kT, qT], outs=[ShI[jj]])
                        p.emit("act", lambda e, jj=jj: e.activation(
                            PhI[jj].ap, ShI[jj].ap, AF.Exp,
                            bias=negMb2.ap[:, h2:h2 + 1], scale=0.125),
                            ins=[ShI[jj], negMb2], outs=[PhI[jj]])
                        p.emit("dve", lambda e, jj=jj: e.scalar_tensor_tensor(
                            PhI[jj].ap, PhI[jj].ap, tau2.ap[:, h2:h2 + 1], PhI[jj].ap,
                            ALU.is_ge, ALU.mult),
                            ins=[PhI[jj], tau2], outs=[PhI[jj]])

                    def i2_pv(jc):
                        nonlocal pvtok
                        jj = jc % 2
                        pvtok = p.emit("pe", lambda e, jc=jc, jj=jj: e.matmul(
                            psPV.ap[:, :], v_tok.ap[:, jc * 65:(jc + 1) * 65],
                            PhI[jj].ap, start=(jc == 0), stop=(jc == 15)),
                            ins=[v_tok, PhI[jj]],
                            waits=pvwar if jc == 0 else ())

                    for jc in range(16):
                        if jc >= 2:
                            i2_pv(jc - 2)
                        i2_scores(jc)
                    i2_pv(14)
                    i2_pv(15)
                    p.setw(psPV, pvtok)
                    p.emit("dve", lambda e: e.tensor_copy(drow.ap[0:1, :], psPV.ap[64:65, :]),
                           ins=[psPV], outs=[drow])
                    p.emit("dve", lambda e: e.tensor_scalar(indr.ap[0:1, :], drow.ap[0:1, :],
                                                            0.0, None, ALU.is_equal),
                           ins=[drow], outs=[indr])
                    p.emit("dve", lambda e: e.tensor_add(drow.ap[0:1, :], drow.ap[0:1, :],
                                                         indr.ap[0:1, :]),
                           ins=[indr], outs=[drow])
                    p.emit("dve", lambda e: e.reciprocal(drow.ap[0:1, :], drow.ap[0:1, :]),
                           ins=[drow], outs=[drow])
                    bcast(drow, 0, TD, 64, psBq)
                    p.emit("dve", lambda e: e.tensor_copy(rbg.ap[:, :], psBq.ap[0:64, 0:TD]),
                           ins=[psBq], outs=[rbg])
                    bcast(indr, 0, TD, 64, psBq)
                    p.emit("dve", lambda e: e.tensor_copy(ibg.ap[:, :], psBq.ap[0:64, 0:TD]),
                           ins=[psBq], outs=[ibg])
                    p.emit("dve", lambda e: e.tensor_mul(rbg.ap[:, :], psPV.ap[0:64, :],
                                                         rbg.ap[:, :]),
                           ins=[psPV], outs=[rbg])
                    p.emit("dve", lambda e: e.tensor_scalar(ibg.ap[:, :], ibg.ap[:, :],
                                                            vmc.ap[0:64, 0:1], None, ALU.mult),
                           ins=[vmc], outs=[ibg])
                    p.emit("dve", lambda e, hb=hb, qc=qc, it=it: e.tensor_add(
                        a1T.ap[hb:hb + 64, qc * T + it * TD:qc * T + (it + 1) * TD],
                        rbg.ap[:, :], ibg.ap[:, :]),
                        ins=[rbg, ibg], outs=[a1T])
        p.flush("i2")
        st2.close()
        stQKV.close()
        st3 = contextlib.ExitStack()
        wo1 = load_w_bf("wo1", dram["L1_attn_w_out"], D, D, st3)
        bocol = sb("ibo", [128, 8], F32, st3)
        wps = [psb(f"i3ps{i}", [128, TD], F32, st3) for i in range(2)]
        stile = sb("istile", [128, TD], F32, st3)
        load_col_vec(bocol, dram["L1_attn_b_out"], 8, wps[0])
        for ec in range(NDC):
            for it in range(NT):
                pp = wps[(ec * NT + it) % 2]
                ptok = None
                for cl in range(NDC):
                    ptok = p.emit("pe", lambda e, ec=ec, it=it, cl=cl, pp=pp: e.matmul(
                        pp.ap[:, :],
                        wo1.ap[:, cl * D + ec * 128:cl * D + ec * 128 + 128],
                        a1T.ap[:, cl * T + it * TD:cl * T + (it + 1) * TD],
                        start=(cl == 0), stop=(cl == NDC - 1)),
                        ins=[wo1, a1T], outs=[pp] if cl == 0 else ())
                p.setw(pp, ptok)
                p.emit("dve", lambda e, ec=ec, pp=pp: e.tensor_scalar(
                    stile.ap[:, :], pp.ap[:, :], bocol.ap[:, ec:ec + 1], None, ALU.add),
                    ins=[pp, bocol], outs=[stile])
                p.dma("sp", dram["s1"][ec * 128:(ec + 1) * 128, it * TD:(it + 1) * TD],
                      stile, ins=[stile])
        p.flush("i3")
        st3.close()
        stackI.close()

    phase_l1_attn()
    if upto <= 7:
        return nc, dram


    stackL1 = contextlib.ExitStack()
    ao_bf = sb("ao_bf", [128, NDC * T], BF16, stackL1)
    stackLG = contextlib.ExitStack()
    lgT8 = sb("lgT8", [8, T], F32, stackLG)

    def phase_ln2():
        stH = contextlib.ExitStack()
        xt = sb("hxt", [128, T], F32, stH)
        sq = sb("hsq", [128, T], F32, stH)
        stat = psb("hstat", [1, 2 * TD], F32, stH)
        psG = psb("hpsG", [8, T], F32, stH)
        mrow = sb("hm", [1, T], F32, stH)
        srow = sb("hs", [1, T], F32, stH)
        trow = sb("ht", [1, T], F32, stH)
        bps = psb("hb", [128, TD], F32, stH)
        mB = sb("hmB", [128, T], F32, stH)
        rB = sb("hrB", [128, T], F32, stH)
        tmp = sb("htmp", [128, T], F32, stH)
        nwcol = sb("hnw", [128, 8], F32, stH)
        nbcol = sb("hnb", [128, 8], F32, stH)
        wgf = sb("hwgf", [128, 64], F32, stH)
        for dst, wname in ((nwcol, "L1_ffn_attn_nw"), (nbcol, "L1_ffn_attn_nb")):
            load_col_vec(dst, dram[wname], 8, bps)
        war = p.begin_fill(wgf)
        gl = None
        for c in range(NDC):
            gl = p.dma("sp", wgf.ap[:, c * 8:(c + 1) * 8],
                       dram["L1_ffn_wg"][c * 128:(c + 1) * 128, :],
                       waits=war if c == 0 else (), key=wgf)
        p.setw(wgf, gl)
        for sweep, (dstrow, use_sq) in enumerate(((mrow, False), (srow, True))):
            for half in range(2):
                war = p.begin_fill(stat)
                tok = None
                for c in range(NDC):
                    p.dma("sp", Buf(xt.ap[:, 0:2 * TD]),
                          dram["xn1_f32"][c * 128:(c + 1) * 128,
                                          half * 2 * TD:(half + 1) * 2 * TD],
                          outs=[xt])
                    if use_sq:
                        p.emit("act", lambda e: e.activation(sq.ap[:, 0:2 * TD],
                                                             xt.ap[:, 0:2 * TD], AF.Square),
                               ins=[xt], outs=[sq])
                        rbuf = sq
                    else:
                        rbuf = xt
                    for t2 in range(2):
                        tok = p.emit("pe", lambda e, t2=t2, c=c, rb=rbuf: e.matmul(
                            stat.ap[0:1, t2 * TD:(t2 + 1) * TD], ones_col.ap[:, 0:1],
                            rb.ap[:, t2 * TD:(t2 + 1) * TD],
                            start=(c == 0), stop=(c == NDC - 1)),
                            ins=[rbuf, ones_col], waits=war if c == 0 and t2 == 0 else ())
                p.setw(stat, tok)
                p.emit("dve", lambda e, dr=dstrow, half=half: e.tensor_scalar(
                    dr.ap[0:1, half * 2 * TD:(half + 1) * 2 * TD], stat.ap[0:1, :],
                    1.0 / D, None, ALU.mult),
                    ins=[stat], outs=[dstrow] if half == 1 else [])
        p.emit("dve", lambda e: e.tensor_mul(trow.ap[0:1, :], mrow.ap[0:1, :], mrow.ap[0:1, :]),
               ins=[mrow], outs=[trow])
        p.emit("dve", lambda e: e.tensor_sub(srow.ap[0:1, :], srow.ap[0:1, :], trow.ap[0:1, :]),
               ins=[trow, srow], outs=[srow])
        p.emit("dve", lambda e: e.tensor_scalar(srow.ap[0:1, :], srow.ap[0:1, :],
                                                1e-12, None, ALU.add), ins=[srow], outs=[srow])
        p.emit("act", lambda e: e.activation(srow.ap[0:1, :], srow.ap[0:1, :], AF.Sqrt),
               ins=[srow], outs=[srow])
        p.emit("dve", lambda e: e.reciprocal(srow.ap[0:1, :], srow.ap[0:1, :]),
               ins=[srow], outs=[srow])
        for dst, row in ((mB, mrow), (rB, srow)):
            war = p.begin_fill(dst)
            lastc = None
            for tt in range(NT):
                bcast(row, tt * TD, TD, 128, bps)
                lastc = p.emit("dve", lambda e, tt=tt, dst=dst: e.tensor_copy(
                    dst.ap[:, tt * TD:(tt + 1) * TD], bps.ap[:, 0:TD]),
                    ins=[bps], waits=war if tt == 0 else ())
            p.setw(dst, lastc)
        bfwar = p.begin_fill(ao_bf)
        lwar = p.begin_fill(psG)
        lastbf = None
        ltok = None
        for c in range(NDC):
            p.dma("sp", xt, dram["xn1_f32"][c * 128:(c + 1) * 128, :])
            p.emit("dve", lambda e: e.tensor_sub(tmp.ap[:, :], xt.ap[:, :], mB.ap[:, :]),
                   ins=[xt, mB], outs=[tmp])
            p.emit("dve", lambda e: e.tensor_mul(tmp.ap[:, :], tmp.ap[:, :], rB.ap[:, :]),
                   ins=[rB], outs=[tmp])
            p.emit("dve", lambda e, c=c: e.tensor_scalar(
                tmp.ap[:, :], tmp.ap[:, :], nwcol.ap[:, c:c + 1], nbcol.ap[:, c:c + 1],
                ALU.mult, ALU.add), ins=[nwcol, nbcol], outs=[tmp])
            for tt in range(NT):
                ltok = p.emit("pe", lambda e, c=c, tt=tt: e.matmul(
                    psG.ap[:, tt * TD:(tt + 1) * TD], wgf.ap[:, c * 8:(c + 1) * 8],
                    tmp.ap[:, tt * TD:(tt + 1) * TD],
                    start=(c == 0), stop=(c == NDC - 1)),
                    ins=[tmp, wgf], waits=lwar if c == 0 and tt == 0 else ())
            lastbf = p.emit("dve", lambda e, c=c: e.tensor_copy(
                ao_bf.ap[:, c * T:(c + 1) * T], tmp.ap[:, :]),
                ins=[tmp], waits=bfwar if c == 0 else ())
        p.setw(psG, ltok)
        p.setw(ao_bf, lastbf)
        p.emit("dve", lambda e: e.tensor_copy(lgT8.ap[:, :], psG.ap[:, :]),
               ins=[psG], outs=[lgT8])
        p.flush("h_ln2")
        stH.close()

    phase_ln2()
    if upto <= 8:
        return nc, dram

    def phase_l1_gate():
        stJ = contextlib.ExitStack()
        R2 = sb("R2", [2, T], F32, stJ)
        psTk = psb("jpsTk", [128, 8], F32, stJ)
        psQT = psb("jpsQT", [2, 128], F32, stJ)
        psE = psb("jpsE", [8, TD], F32, stJ)
        psR1 = psb("jpsR1", [1, TD], F32, stJ)
        psC = psb("jpsC", [1, TD], F32, stJ)
        lgsb = sb("jlg", [8, TD], F32, stJ)
        lgt = sb("jlgt", [128, 8], F32, stJ)
        m1 = sb("jm1", [128, 1], F32, stJ)
        nm1 = sb("jnm1", [128, 1], F32, stJ)
        e8 = sb("je8", [128, 8], F32, stJ)
        cand = sb("jcand", [128, 8], F32, stJ)
        esum = sb("jesum", [128, 1], F32, stJ)
        Qc = sb("jQc", [128, 2], F32, stJ)
        ohT = sb("johT", [8, T], F32, stJ)
        pos = sb("jpos", [8, T], F32, stJ)
        cnt1s = sb("jcnt", [8, 1], F32, stJ)
        agsb = sb("jag", [8, 8], F32, stJ)
        off1 = sb("joff", [8, 1], F32, stJ)
        trow = sb("jtrow", [1, T], F32, stJ)
        grow = sb("jgrow", [1, T], F32, stJ)
        sxr = sb("jsxr", [1, TD], F32, stJ)
        w8 = sb("jw8", [8, T], F32, stJ)
        c0r = sb("jc0", [1, T], F32, stJ)
        c1r = sb("jc1", [1, T], F32, stJ)
        mxr = sb("jmx", [1, T], F32, stJ)
        R2war = p.begin_fill(R2)
        R2tok = None
        for tt in range(NT):
            p.emit("dve", lambda e, tt=tt: e.tensor_copy(
                lgsb.ap[:, :], lgT8.ap[:, tt * TD:(tt + 1) * TD]),
                ins=[lgT8], outs=[lgsb])
            for s in range(4):
                cc = tt * 4 + s
                p.emit("pe", lambda e, s=s: e.transpose(
                    psTk.ap[:, :], lgsb.ap[:, s * 128:(s + 1) * 128],
                    ident_f.ap[0:8, 0:8]), ins=[lgsb, ident_f], outs=[psTk])
                p.emit("dve", lambda e: e.tensor_copy(lgt.ap[:, :], psTk.ap[:, :]),
                       ins=[psTk], outs=[lgt])
                p.emit("dve", lambda e: e.reduce_max(m1.ap[:, :], lgt.ap[:, :], axis=AX.X),
                       ins=[lgt], outs=[m1])
                p.emit("dve", lambda e: e.tensor_scalar(e8.ap[:, :], lgt.ap[:, :],
                                                        m1.ap[:, 0:1], None, ALU.is_equal),
                       ins=[lgt, m1], outs=[e8])
                p.emit("dve", lambda e: e.tensor_scalar(e8.ap[:, :], e8.ap[:, :],
                                                        -1000.0, 1000.0, ALU.mult, ALU.add),
                       ins=[e8], outs=[e8])
                p.emit("dve", lambda e: e.tensor_add(cand.ap[:, :], e8.ap[:, :],
                                                     iota8_row128.ap[:, :]),
                       ins=[e8, iota8_row128], outs=[cand])
                p.emit("dve", lambda e: e.tensor_reduce(Qc.ap[:, 0:1], cand.ap[:, :],
                                                        axis=AX.X, op=ALU.min),
                       ins=[cand], outs=[Qc])
                p.emit("dve", lambda e: e.tensor_scalar(nm1.ap[:, :], m1.ap[:, :],
                                                        -1.0, None, ALU.mult),
                       ins=[m1], outs=[nm1])
                p.emit("act", lambda e: e.activation(e8.ap[:, :], lgt.ap[:, :], AF.Exp,
                                                     bias=nm1.ap[:, 0:1], scale=1.0,
                                                     accum_out=esum.ap[:, 0:1]),
                       ins=[lgt, nm1], outs=[e8, esum])
                p.emit("dve", lambda e: e.reciprocal(Qc.ap[:, 1:2], esum.ap[:, 0:1]),
                       ins=[esum], outs=[Qc])
                p.emit("pe", lambda e: e.transpose(psQT.ap[:, :], Qc.ap[:, :],
                                                   ident_f.ap[:, :]),
                       ins=[Qc, ident_f], outs=[psQT])
                R2tok = p.emit("dve", lambda e, cc=cc: e.tensor_copy(
                    R2.ap[:, cc * 128:(cc + 1) * 128], psQT.ap[:, :]),
                    ins=[psQT], waits=R2war if cc == 0 else ())
        p.setw(R2, R2tok)
        war = p.begin_fill(ohT)
        lt = None
        for tt in range(NT):
            p.emit("pe", lambda e, tt=tt: e.matmul(
                psE.ap[:, :], consts["hc_sel8"].ap[0:2, 0:8],
                R2.ap[:, tt * TD:(tt + 1) * TD], start=True, stop=True),
                ins=[R2, consts["hc_sel8"]], outs=[psE])
            lt = p.emit("dve", lambda e, tt=tt: e.tensor_scalar(
                ohT.ap[:, tt * TD:(tt + 1) * TD], psE.ap[:, :],
                iota8_col.ap[:, 0:1], None, ALU.is_equal),
                ins=[psE, iota8_col], waits=war if tt == 0 else ())
        p.setw(ohT, lt)
        p.emit("dve", lambda e: e.reduce_sum(cnt1s.ap[:, 0:1], ohT.ap[:, :], axis=AX.X),
               ins=[ohT], outs=[cnt1s])
        cdma = p.dma("sp", dram["cnt1"], cnt1s, ins=[cnt1s])
        cctok = p.emit("pool", lambda e: e.collective_compute(
            "AllGather", ALU.bypass, replica_groups=[list(range(8))],
            ins=[dram["cnt1"].opt()], outs=[dram["ag1"].opt()]),
            waits=[cdma], collective=True)
        agld = p.dma("sp", agsb.ap.rearrange("e (c k) -> e c k", c=8),
                     dram["ag1"].rearrange("(c e) k -> e c k", c=8),
                     waits=[cctok], key=agsb, allow_slow_non_contiguous=True)
        p.setw(agsb, agld)
        p.emit("dve", lambda e: e.tensor_mul(agsb.ap[:, :], agsb.ap[:, :],
                                             consts["hc_prev1"].ap[:, :]),
               ins=[consts["hc_prev1"], agsb], outs=[agsb])
        ot = p.emit("dve", lambda e: e.tensor_copy(off1.ap[:, :], agsb.ap[:, 0:1]),
                    ins=[agsb], outs=[off1])
        for c in range(1, 8):
            ot = p.emit("dve", lambda e, c=c: e.tensor_add(
                off1.ap[:, :], off1.ap[:, :], agsb.ap[:, c:c + 1]),
                ins=[agsb], outs=[off1], waits=[ot])
        p.emit("dve", lambda e: e.tensor_tensor_scan(
            pos.ap[:, :], ohT.ap[:, :], ohT.ap[:, :], 0.0, ALU.add, ALU.bypass),
            ins=[ohT], outs=[pos])
        p.emit("dve", lambda e: e.tensor_scalar(
            pos.ap[:, :], pos.ap[:, :], off1.ap[:, 0:1], None, ALU.add),
            ins=[off1, pos], outs=[pos])
        p.emit("dve", lambda e: e.tensor_sub(pos.ap[:, :], pos.ap[:, :], ohT.ap[:, :]),
               ins=[ohT, pos], outs=[pos])
        p.emit("dve", lambda e: e.tensor_scalar(pos.ap[:, :], pos.ap[:, :],
                                                2048.0, None, ALU.is_lt),
               ins=[pos], outs=[pos])
        p.emit("dve", lambda e: e.tensor_mul(ohT.ap[:, :], ohT.ap[:, :], pos.ap[:, :]),
               ins=[pos], outs=[ohT])
        rcb = load_w_bf("rcb", dram["L1_ffn_res_coef"], D, 2, stJ)
        for cx, dstr in ((0, c0r), (1, c1r)):
            war = p.begin_fill(dstr)
            dl = None
            for tt in range(NT):
                ctok = None
                cwar = p.begin_fill(psC)
                for c in range(NDC):
                    ctok = p.emit("pe", lambda e, cx=cx, c=c, tt=tt: e.matmul(
                        psC.ap[0:1, :], rcb.ap[:, c * 2 + cx:c * 2 + cx + 1],
                        ao_bf.ap[:, c * T + tt * TD:c * T + (tt + 1) * TD],
                        start=(c == 0), stop=(c == NDC - 1)),
                        ins=[rcb, ao_bf], waits=cwar if c == 0 else ())
                p.setw(psC, ctok)
                dl = p.emit("dve", lambda e, tt=tt, dstr=dstr: e.tensor_copy(
                    dstr.ap[0:1, tt * TD:(tt + 1) * TD], psC.ap[0:1, :]),
                    ins=[psC], waits=war if tt == 0 else ())
            p.setw(dstr, dl)
        p.emit("dve", lambda e: e.tensor_max(mxr.ap[0:1, :], c0r.ap[0:1, :], c1r.ap[0:1, :]),
               ins=[c0r, c1r], outs=[mxr])
        for dstr in (c0r, c1r):
            p.emit("dve", lambda e, dstr=dstr: e.tensor_sub(
                dstr.ap[0:1, :], dstr.ap[0:1, :], mxr.ap[0:1, :]),
                ins=[mxr], outs=[dstr])
            p.emit("act", lambda e, dstr=dstr: e.activation(
                dstr.ap[0:1, :], dstr.ap[0:1, :], AF.Exp), ins=[], outs=[dstr])
        p.emit("dve", lambda e: e.tensor_add(mxr.ap[0:1, :], c0r.ap[0:1, :], c1r.ap[0:1, :]),
               ins=[c0r, c1r], outs=[mxr])
        p.emit("dve", lambda e: e.reciprocal(mxr.ap[0:1, :], mxr.ap[0:1, :]),
               ins=[], outs=[mxr])
        p.emit("dve", lambda e: e.tensor_mul(c0r.ap[0:1, :], c0r.ap[0:1, :], mxr.ap[0:1, :]),
               ins=[mxr], outs=[c0r])
        p.emit("dve", lambda e: e.tensor_mul(c1r.ap[0:1, :], c1r.ap[0:1, :], mxr.ap[0:1, :]),
               ins=[mxr], outs=[c1r])
        p.emit("dve", lambda e: e.tensor_copy(brow.ap[0:1, :], c0r.ap[0:1, :]),
               ins=[c0r], outs=[brow])
        twar = p.begin_fill(trow)
        ttok = None
        for tt in range(NT):
            p.emit("pe", lambda e, tt=tt: e.matmul(
                psR1.ap[0:1, :], ones_col.ap[0:8, 0:1],
                ohT.ap[:, tt * TD:(tt + 1) * TD], start=True, stop=True),
                ins=[ohT, ones_col], outs=[psR1])
            p.emit("pe", lambda e, tt=tt: e.matmul(
                psE.ap[0:1, :], consts["hc_sel8"].ap[0:2, 128:129],
                R2.ap[:, tt * TD:(tt + 1) * TD], start=True, stop=True),
                ins=[R2, consts["hc_sel8"]], outs=[psE])
            p.emit("dve", lambda e: e.tensor_copy(sxr.ap[0:1, :], psE.ap[0:1, :]),
                   ins=[psE], outs=[sxr])
            ttok = p.emit("dve", lambda e, tt=tt: e.tensor_mul(
                trow.ap[0:1, tt * TD:(tt + 1) * TD], psR1.ap[0:1, :], sxr.ap[0:1, :]),
                ins=[psR1, sxr], waits=twar if tt == 0 else ())
        p.setw(trow, ttok)
        p.emit("dve", lambda e: e.tensor_mul(grow.ap[0:1, :], trow.ap[0:1, :],
                                             c1r.ap[0:1, :]),
               ins=[trow, c1r], outs=[grow])
        war = p.begin_fill(w8)
        lt = None
        for tt in range(NT):
            bcast(grow, tt * TD, TD, 8, psE)
            lt = p.emit("dve", lambda e, tt=tt: e.tensor_copy(
                w8.ap[:, tt * TD:(tt + 1) * TD], psE.ap[:, 0:TD]),
                ins=[psE], waits=war if tt == 0 else ())
        p.setw(w8, lt)
        p.emit("dve", lambda e: e.tensor_mul(arow.ap[:, :], ohT.ap[:, :], w8.ap[:, :]),
               ins=[ohT, w8], outs=[arow])
        p.flush("j")
        stJ.close()

    phase_l1_gate()
    stackLG.close()
    if upto <= 9:
        return nc, dram

    def phase_l1_resmlp():
        stK = contextlib.ExitStack()
        rhbuf = sb("rhbuf", [128, 32 * TD], BF16, stK)
        wric = [sb(f"wric{i}", [128, 8 * 128], BF16, stK) for i in range(2)]
        wroc = [sb(f"wroc{i}", [128, 32 * 128], BF16, stK) for i in range(2)]
        kstg = [sb(f"kstg{i}", [128, 32 * 128], F32, stK) for i in range(2)]
        kstg2 = [sb(f"kstg2_{i}", [128, 8 * 128], F32, stK) for i in range(2)]
        ribc = sb("ribc", [128, 32], F32, stK)
        robc = sb("robc", [128, 8], F32, stK)
        cfB = sb("cfB", [128, TD], F32, stK)
        xnt = sb("kxnt", [128, TD], F32, stK)
        otile = [sb(f"kot{i}", [128, TD], F32, stK) for i in range(2)]
        psH = [psb(f"kpsH{i}", [128, TD], F32, stK) for i in range(2)]
        psO = [psb(f"kpsO{i}", [128, TD], F32, stK) for i in range(2)]
        psB = psb("kpsB", [128, TD], F32, stK)
        load_col_vec(ribc, dram["L1_ffn_res_inter_b"][0, :], 32, psB)
        load_col_vec(robc, dram["L1_ffn_res_output_b"], 8, psB)
        for it in range(NT):
            p.emit("pe", lambda e, it=it: e.matmul(
                psB.ap[:, :], ones_row.ap[0:1, :],
                brow.ap[0:1, it * TD:(it + 1) * TD], start=True, stop=True),
                ins=[brow, ones_row], outs=[psB])
            p.emit("dve", lambda e: e.tensor_copy(cfB.ap[:, :], psB.ap[:, :]),
                   ins=[psB], outs=[cfB])
            rhwar = p.begin_fill(rhbuf)
            rhl = None
            for ic in range(32):
                b = ic % 2
                hw_cast_load(wric[b],
                             dram["L1_ffn_res_inter_w"][:, ic * 128:(ic + 1) * 128],
                             8, 128, kstg2[b])
                htok = None
                hwar = p.begin_fill(psH[b])
                for c in range(NDC):
                    htok = p.emit("pe", lambda e, c=c, it=it, b=b: e.matmul(
                        psH[b].ap[:, :], wric[b].ap[:, c * 128:(c + 1) * 128],
                        ao_bf.ap[:, c * T + it * TD:c * T + (it + 1) * TD],
                        start=(c == 0), stop=(c == NDC - 1)),
                        ins=[wric[b], ao_bf], waits=hwar if c == 0 else ())
                p.setw(psH[b], htok)
                rhl = p.emit("act", lambda e, ic=ic, b=b: e.activation(
                    rhbuf.ap[:, ic * TD:(ic + 1) * TD], psH[b].ap[:, :],
                    AF.Gelu_apprx_tanh, bias=ribc.ap[:, ic:ic + 1], scale=1.0),
                    ins=[psH[b], ribc], waits=rhwar if ic == 0 else ())
            p.setw(rhbuf, rhl)
            for dc in range(NDC):
                b = dc % 2
                hw_cast_load(wroc[b],
                             dram["L1_ffn_res_output_w"][:, dc * 128:(dc + 1) * 128],
                             32, 128, kstg[b])
                otok = None
                owar = p.begin_fill(psO[b])
                for ic in range(32):
                    otok = p.emit("pe", lambda e, ic=ic, b=b: e.matmul(
                        psO[b].ap[:, :], wroc[b].ap[:, ic * 128:(ic + 1) * 128],
                        rhbuf.ap[:, ic * TD:(ic + 1) * TD],
                        start=(ic == 0), stop=(ic == 31)),
                        ins=[wroc[b], rhbuf], waits=owar if ic == 0 else ())
                p.setw(psO[b], otok)
                p.dma("sp", xnt, dram["xn1_f32"][dc * 128:(dc + 1) * 128,
                                                 it * TD:(it + 1) * TD])
                ot_ = otile[b]
                p.emit("dve", lambda e, dc=dc, b=b, ot_=ot_: e.tensor_scalar(
                    ot_.ap[:, :], psO[b].ap[:, :], robc.ap[:, dc:dc + 1], None, ALU.add),
                    ins=[psO[b], robc], outs=[ot_])
                p.emit("dve", lambda e, ot_=ot_: e.tensor_mul(ot_.ap[:, :], ot_.ap[:, :],
                                                              cfB.ap[:, :]),
                       ins=[cfB], outs=[ot_])
                p.emit("dve", lambda e, ot_=ot_: e.tensor_add(ot_.ap[:, :], ot_.ap[:, :],
                                                              xnt.ap[:, :]),
                       ins=[xnt], outs=[ot_])
                p.dma("sp", dram["acc1"][dc * 128:(dc + 1) * 128, it * TD:(it + 1) * TD],
                      ot_, ins=[ot_])
        p.flush("k")
        stK.close()

    phase_l1_resmlp()
    if upto <= 10:
        return nc, dram

    def phase_l1_experts():
        stL = contextlib.ExitStack()
        hbuf = sb("lhbuf", [128, 16 * T], BF16, stL)
        hbufc = [Buf(hbuf.ap[:, hc * T:(hc + 1) * T]) for hc in range(16)]
        woh = sb("lwoh", [128, 16 * D], BF16, stL)
        wic = [sb(f"lwic{i}", [128, 8 * 128], BF16, stL) for i in range(2)]
        lstg = [sb("lstg0", [128, 8 * 128], F32, stL)]
        ibc = sb("libc", [128, 16], F32, stL)
        obc = sb("lobc", [128, 8], F32, stL)
        alB = sb("lalB", [128, T], F32, stL)
        ctile = [sb(f"lct{i}", [128, TD], F32, stL) for i in range(2)]
        psA = [psb(f"lpsA{i}", [128, TD], F32, stL) for i in range(2)]
        psY = [psb(f"lpsY{i}", [128, TD], F32, stL) for i in range(2)]
        psAl = psb("lpsAl", [128, TD], F32, stL)
        prev_acc = None
        for ex in range(8):
            load_col_vec(obc, dram["L1_ffn_output_b"][ex], 8, psAl)
            awar = p.begin_fill(alB)
            al = None
            for it in range(NT):
                p.emit("pe", lambda e, ex=ex, it=it: e.matmul(
                    psAl.ap[:, :], consts["hc_sel8"].ap[:, ex * 128:(ex + 1) * 128],
                    arow.ap[:, it * TD:(it + 1) * TD], start=True, stop=True),
                    ins=[arow, consts["hc_sel8"]], outs=[psAl])
                al = p.emit("dve", lambda e, it=it: e.tensor_copy(
                    alB.ap[:, it * TD:(it + 1) * TD], psAl.ap[:, :]),
                    ins=[psAl], waits=awar if it == 0 else ())
            p.setw(alB, al)
            for hh in range(2):
                load_col_vec(ibc, dram["L1_ffn_inter_b"][ex, hh * 2048:(hh + 1) * 2048],
                             16, psAl)
                war = p.begin_fill(woh)
                wl = p.dma("pool", woh.ap.rearrange("p (c d) -> p c d", c=16),
                           dram["L1_ffn_output_w"][ex][hh * 2048:(hh + 1) * 2048, :]
                           .rearrange("(c p) d -> p c d", p=128),
                           waits=war, key=woh)
                p.setw(woh, wl)
                for ic in range(16):
                    hcwar = p.begin_fill(hbufc[ic])
                    b = ic % 2
                    hw_cast_load(wic[b],
                                 dram["L1_ffn_inter_w"][ex][:, hh * 2048 + ic * 128:
                                                            hh * 2048 + (ic + 1) * 128],
                                 8, 128, lstg[0])
                    for it in range(NT):
                        pb = (ic * NT + it) % 2
                        pwar = p.begin_fill(psA[pb])
                        ptok = None
                        for c in range(NDC):
                            ptok = p.emit("pe", lambda e, c=c, it=it, b=b, pb=pb: e.matmul(
                                psA[pb].ap[:, :], wic[b].ap[:, c * 128:(c + 1) * 128],
                                ao_bf.ap[:, c * T + it * TD:c * T + (it + 1) * TD],
                                start=(c == 0), stop=(c == NDC - 1)),
                                ins=[wic[b], ao_bf], waits=pwar if c == 0 else ())
                        p.setw(psA[pb], ptok)
                        hl = p.emit("act", lambda e, ic=ic, it=it, pb=pb: e.activation(
                            hbuf.ap[:, ic * T + it * TD:ic * T + (it + 1) * TD],
                            psA[pb].ap[:, :], AF.Gelu_apprx_tanh,
                            bias=ibc.ap[:, ic:ic + 1], scale=1.0),
                            ins=[psA[pb], ibc],
                            waits=hcwar if it == 0 else ())
                    p.setw(hbufc[ic], hl)
                for dc in range(NDC):
                    for it in range(NT):
                        pb = (dc * NT + it) % 2
                        ct = ctile[pb]
                        ywar = p.begin_fill(psY[pb])
                        ytok = None
                        for ic in range(16):
                            ytok = p.emit("pe", lambda e, ic=ic, dc=dc, it=it, pb=pb: e.matmul(
                                psY[pb].ap[:, :],
                                woh.ap[:, ic * D + dc * 128:ic * D + dc * 128 + 128],
                                hbuf.ap[:, ic * T + it * TD:ic * T + (it + 1) * TD],
                                start=(ic == 0), stop=(ic == 15)),
                                ins=[woh, hbufc[ic]], waits=ywar if ic == 0 else ())
                        p.setw(psY[pb], ytok)
                        cwar = p.begin_fill(ct)
                        if hh == 0:
                            p.emit("dve", lambda e, dc=dc, pb=pb, ct=ct: e.tensor_scalar(
                                ct.ap[:, :], psY[pb].ap[:, :], obc.ap[:, dc:dc + 1],
                                None, ALU.add), ins=[psY[pb], obc], waits=cwar)
                        else:
                            p.emit("dve", lambda e, pb=pb, ct=ct: e.tensor_copy(
                                ct.ap[:, :], psY[pb].ap[:, :]),
                                ins=[psY[pb]], waits=cwar)
                        clast = p.emit("dve", lambda e, it=it, ct=ct: e.tensor_mul(
                            ct.ap[:, :], ct.ap[:, :],
                            alB.ap[:, it * TD:(it + 1) * TD]),
                            ins=[alB], outs=[])
                        p.setw(ct, clast)
                        tok = p.dma("pool", dram["acc1"][dc * 128:(dc + 1) * 128,
                                                         it * TD:(it + 1) * TD],
                                    ct, ins=[ct],
                                    waits=[prev_acc] if prev_acc is not None and
                                          dc == 0 and it == 0 else (),
                                    accum_op=ALU.add)
                        prev_acc = tok
        p.flush("l")
        stL.close()

    phase_l1_experts()
    if upto <= 11:
        return nc, dram

    phase_combine(dram["x1"], dram["s1"], dram["acc1"], dram["out"], "m")
    return nc, dram




def _in_maps(inputs):
    maps = []
    x = np.asarray(inputs["L0_attn_x"], dtype=np.float32)
    base = {}
    for name, shape in WEIGHT_SPECS:
        base[name] = np.ascontiguousarray(
            np.asarray(inputs[name], dtype=np.float32).reshape(shape))
    for c in range(NCORES):
        m = dict(base)
        m["x"] = np.ascontiguousarray(x[c].T)
        m.update(host_consts(c))
        maps.append(m)
    return maps


def kernel(**inputs):
    nc, _ = build_program()
    res = run_bass_kernel_spmd(nc, _in_maps(inputs), core_ids=list(range(NCORES)))
    out = np.stack([np.ascontiguousarray(np.asarray(r["out"]).T) for r in res.results])
    return out.astype(np.float32)
